# version 05
# speedup vs baseline: 1.2126x; 1.0388x over previous
import math
import numpy as np
import ml_dtypes
import concourse.bass as bass
import concourse.mybir as mybir
from concourse.bass_utils import run_bass_kernel_spmd
from contextlib import ExitStack

F32 = mybir.dt.float32
BF16 = mybir.dt.bfloat16
I32 = mybir.dt.int32
AX = mybir.AxisListType.X
ALU = mybir.AluOpType
AF = mybir.ActivationFunctionType

D = 1024
NCORES = 8
TILE = 128
OWN = 2048
NT = OWN // TILE
SEQ = 4096


ENGINES = ("sync", "scalar", "vector", "tensor", "gpsimd")


class Plan:
    def __init__(self, nc):
        self.nc = nc
        self.steps = {e: [] for e in ENGINES}
        self.count = {}
        for e in ENGINES:
            self.count["s_" + e] = 0
            self.count["d_" + e] = 0

    def do(self, eng, fn, after=(), gate=True):
        own = "s_" + eng
        waits = []
        for ev in after:
            if ev is None:
                continue
            (k, v) = ev
            if k == own and not gate:
                continue
            waits.append((k, v))
        self.steps[eng].append((waits, fn, own, 1))
        self.count[own] += 1
        return (own, self.count[own])

    def dma(self, eng, out, in_, after=(), **kw):
        own = "d_" + eng
        waits = [(own, self.count[own])] if self.count[own] else []
        for ev in after:
            if ev is None:
                continue
            (k, v) = ev
            if k == "s_" + eng:
                continue
            waits.append((k, v))
        self.steps[eng].append(
            (waits, lambda E: E.dma_start(out=out, in_=in_, **kw), own, 16))
        self.count[own] += 16
        return (own, self.count[own])

    def wait(self, eng, events):
        waits = [ev for ev in events if ev is not None]
        self.steps[eng].append((waits, None, None, 0))

    def finalize(self, ctx):
        nc = self.nc
        sems = {}
        for k in self.count:
            sems[k] = ctx.enter_context(nc.semaphore(k))
        flush = {e: nc.alloc_sbuf_tensor(f"flush_{e}", [1, 8], F32)
                 for e in ("scalar", "vector", "gpsimd")}
        block = ctx.enter_context(nc.Block())

        needed = {k: set() for k in self.count}
        for eng in ENGINES:
            wm = {k: 0 for k in self.count}
            for waits, fn, semk, inc in self.steps[eng]:
                for (k2, v) in waits:
                    if v > wm[k2]:
                        needed[k2].add(v)
                        wm[k2] = v

        def make_section(eng):
            own = "s_" + eng
            pts = sorted(needed[own])

            def body(E):
                wm = {k: 0 for k in self.count}
                cum = 0
                last_pub = 0
                pi = 0
                n = len(self.steps[eng])
                for idx, (waits, fn, semk, inc) in enumerate(self.steps[eng]):
                    for (k2, v) in waits:
                        if v > wm[k2]:
                            E.wait_ge(sems[k2], v)
                            wm[k2] = v
                    if fn is None:
                        continue
                    ins = fn(E)
                    if not inc:
                        continue
                    if semk != own:
                        ins.then_inc(sems[semk], inc)
                        continue
                    cum += inc
                    while pi < len(pts) and pts[pi] <= cum:
                        pi += 1
                    if (pi > 0 and pts[pi - 1] == cum):
                        if eng == "scalar":
                            E.activation(flush[eng].ap()[:, 0:1], flush[eng].ap()[:, 4:5],
                                         AF.Identity).then_inc(sems[semk], cum - last_pub)
                        elif eng in ("vector", "gpsimd"):
                            E.memset(flush[eng].ap()[:, 0:4], 0.0).then_inc(sems[semk], cum - last_pub)
                        else:
                            ins.then_inc(sems[semk], cum - last_pub)
                        last_pub = cum
            return body

        for eng in ENGINES:
            if self.steps[eng]:
                getattr(block, eng)(make_section(eng))



def to_fm(a):
    return np.ascontiguousarray(np.swapaxes(a, -1, -2))


def vec_cols(v):
    return np.ascontiguousarray(np.asarray(v, dtype=np.float32).reshape(-1, 128).T)


def alloc_identity(nc, plan, dt, name="ident"):
    ident = nc.alloc_sbuf_tensor(name, [128, 128], dt)
    ev0 = plan.do("gpsimd", lambda E: E.memset(ident.ap(), 0.0))
    ev = plan.do("gpsimd", lambda E: E.affine_select(
        out=ident.ap(), in_=ident.ap(), compare_op=ALU.not_equal, fill=1.0,
        base=0, pattern=[[-1, 128]], channel_multiplier=1), after=[ev0])
    return ident, ev



def const_col(nc, plan, name, value, dt=F32):
    col = nc.alloc_sbuf_tensor(name, [128, 1], dt)
    ev = plan.do("vector", lambda E: E.memset(col.ap(), value))
    return col, ev


def emit_ln_fm(nc, plan, name, T, xf32, xbf, out, ones_col, ones_row, ev_in,
               G=512, eps=1e-5, out_bf=None):
    sb = nc.alloc_sbuf_tensor
    ps = nc.alloc_psum_tensor
    eps_col, ev_eps = const_col(nc, plan, f"{name}_eps", eps)
    sq = sb(f"{name}_sq", [128, 8 * G], BF16)
    rows = sb(f"{name}_rows", [1, 4 * G], F32)
    rstd_r = sb(f"{name}_rstd", [1, G], F32)
    nmr_r = sb(f"{name}_nmr", [1, G], F32)
    rb_sb = sb(f"{name}_rb", [128, G], F32)
    nb_sb = sb(f"{name}_nb", [128, G], F32)
    t1 = sb(f"{name}_t1", [128, 8 * G], F32)
    ps_s1 = ps(f"{name}_ps1", [1, G], F32)
    ps_s2 = ps(f"{name}_ps2", [1, G], F32)
    ps_b = ps(f"{name}_psb", [128, 2 * G], F32)
    mean = rows.ap()[:, 0:G]
    msq = rows.ap()[:, G:2 * G]
    var = rows.ap()[:, 2 * G:3 * G]
    tmp = rows.ap()[:, 3 * G:4 * G]
    all_done = []
    prev_done = list(ev_in)
    for g in range(T // G):
        c0 = g * G
        ev_s = []
        for c in range(8):
            xs = xbf.ap()[:, c * T + c0: c * T + c0 + G]
            e1 = plan.do("tensor", lambda E, xs=xs, c=c: E.matmul(
                ps_s1.ap(), ones_col.ap(), xs, start=(c == 0), stop=(c == 7)),
                after=prev_done)
            eq = plan.do("scalar", lambda E, xs=xs, c=c: E.activation(
                sq.ap()[:, bass.ts(c, G)], xs, AF.Square), after=prev_done)
            e2 = plan.do("tensor", lambda E, c=c: E.matmul(
                ps_s2.ap()[:, 0:G], ones_col.ap(), sq.ap()[:, bass.ts(c, G)],
                start=(c == 0), stop=(c == 7)), after=[eq])
            if c == 7:
                ev_s = [e1, e2]
        e = plan.do("vector", lambda E: E.tensor_scalar_mul(mean, ps_s1.ap()[:, 0:G], 1.0 / D), after=ev_s)
        e = plan.do("vector", lambda E: E.tensor_scalar_mul(msq, ps_s2.ap()[:, 0:G], 1.0 / D), after=[e])
        e = plan.do("vector", lambda E: E.tensor_mul(tmp, mean, mean), after=[e], gate=True)
        e = plan.do("vector", lambda E: E.tensor_sub(var, msq, tmp), after=[e], gate=True)
        e = plan.do("scalar", lambda E: E.activation(tmp, var, AF.Sqrt, bias=eps_col.ap()[0:1, 0:1]), after=[e, ev_eps])
        e = plan.do("vector", lambda E: E.reciprocal(rstd_r.ap(), tmp), after=[e], gate=True)
        erow = plan.do("vector", lambda E: E.tensor_mul(nmr_r.ap(), mean, rstd_r.ap()), after=[e], gate=True)
        eb1 = plan.do("tensor", lambda E: E.matmul(
            ps_b.ap()[:, 0:G], ones_row.ap(), rstd_r.ap(), start=True, stop=True), after=[erow])
        eb2 = plan.do("tensor", lambda E: E.matmul(
            ps_b.ap()[:, G:2 * G], ones_row.ap(), nmr_r.ap(), start=True, stop=True), after=[erow])
        ev_rb = plan.do("scalar", lambda E: E.activation(rb_sb.ap(), ps_b.ap()[:, 0:G], AF.Identity), after=[eb1])
        ev_nb = plan.do("scalar", lambda E: E.activation(nb_sb.ap(), ps_b.ap()[:, G:2 * G], AF.Identity), after=[eb2])
        group_done = []
        for c in range(8):
            xs = xf32.ap()[:, c * T + c0: c * T + c0 + G]
            os = out.ap()[:, c * T + c0: c * T + c0 + G]
            ts = t1.ap()[:, bass.ts(c, G)]
            e1 = plan.do("gpsimd", lambda E, xs=xs, ts=ts: E.tensor_mul(ts, xs, rb_sb.ap()),
                         after=[ev_rb] + list(ev_in))
            e2 = plan.do("vector", lambda E, os=os, ts=ts: E.tensor_sub(os, ts, nb_sb.ap()),
                         after=[e1, ev_nb])
            if out_bf is not None:
                obs = out_bf.ap()[:, c * T + c0: c * T + c0 + G]
                e2 = plan.do("scalar", lambda E, os=os, obs=obs: E.activation(obs, os, AF.Identity),
                             after=[e2])
            group_done.append(e2)
        prev_done = group_done
        all_done.extend(group_done)
    return all_done


def emit_ln_tm(nc, plan, name, ntiles, T, xf32, out_f32, identf, eps_col,
               ev_eps, ev_id, ev_in, out_bf=None, scr=None):
    sb = nc.alloc_sbuf_tensor
    ps = nc.alloc_psum_tensor
    if scr is not None and "xtm" in scr:
        xtm, sqtm, xh, colz, ps_t = scr["xtm"], scr["sqtm"], scr["xh"], scr["colz"], scr["ps_t"]
    else:
        xtm = sb(f"{name}_xtm", [128, 1024], F32)
        sqtm = sb(f"{name}_sqtm", [128, 1024], F32)
        xh = sb(f"{name}_xh", [128, 1024], F32)
        colz = sb(f"{name}_cols", [128, 8], F32)
        ps_t = ps(f"{name}_pst", [128, 128], F32)
        if scr is not None:
            scr.update(xtm=xtm, sqtm=sqtm, xh=xh, colz=colz, ps_t=ps_t)
    s1c = colz.ap()[:, 0:1]
    s2c = colz.ap()[:, 1:2]
    mc_ = colz.ap()[:, 2:3]
    vc_ = colz.ap()[:, 3:4]
    t0c = colz.ap()[:, 4:5]
    rsc = colz.ap()[:, 5:6]
    done = []
    prev = list(ev_in)
    for t in range(ntiles):
        ev_tp = []
        for c in range(8):
            et = plan.do("tensor", lambda E, c=c, t=t: E.transpose(
                ps_t.ap(), xf32.ap()[:, c * T + t * 128: c * T + t * 128 + 128],
                identf.ap()), after=[ev_id] + prev)
            ee = plan.do("scalar", lambda E, c=c: E.activation(
                xtm.ap()[:, bass.ts(c, 128)], ps_t.ap(), AF.Identity), after=[et])
            plan.do("tensor", lambda E: E.nop(), after=[ee])
            ev_tp.append(ee)
        e1 = plan.do("vector", lambda E: E.reduce_sum(s1c, xtm.ap(), axis=AX), after=ev_tp)
        eq = plan.do("scalar", lambda E: E.activation(sqtm.ap(), xtm.ap(), AF.Square), after=ev_tp)
        e2 = plan.do("vector", lambda E: E.reduce_sum(s2c, sqtm.ap(), axis=AX), after=[eq, e1])
        e3 = plan.do("vector", lambda E: E.tensor_scalar_mul(mc_, s1c, 1.0 / D), after=[e2])
        e4 = plan.do("vector", lambda E: E.tensor_scalar(vc_, mc_, mc_, None, ALU.mult), after=[e3])
        e5 = plan.do("vector", lambda E: E.scalar_tensor_tensor(
            t0c, s2c, 1.0 / D, vc_, ALU.mult, ALU.subtract), after=[e4])
        e6 = plan.do("scalar", lambda E: E.activation(t0c, t0c, AF.Ln, bias=eps_col.ap()[:, 0:1]), after=[e5, ev_eps])
        e7 = plan.do("scalar", lambda E: E.activation(rsc, t0c, AF.Exp, scale=-0.5), after=[e6])
        e8 = plan.do("vector", lambda E: E.tensor_scalar(
            xh.ap(), xtm.ap(), mc_, rsc, ALU.subtract, ALU.mult), after=[e7])
        tile_done = []
        for c in range(8):
            et = plan.do("tensor", lambda E, c=c: E.transpose(
                ps_t.ap(), xh.ap()[:, bass.ts(c, 128)], identf.ap()), after=[e8, ev_id])
            ee = plan.do("scalar", lambda E, c=c, t=t: E.activation(
                out_f32.ap()[:, c * T + t * 128: c * T + t * 128 + 128],
                ps_t.ap(), AF.Identity), after=[et])
            if out_bf is not None:
                ee = plan.do("scalar", lambda E, c=c, t=t, ee=ee: E.activation(
                    out_bf.ap()[:, c * T + t * 128: c * T + t * 128 + 128],
                    out_f32.ap()[:, c * T + t * 128: c * T + t * 128 + 128],
                    AF.Identity), after=[ee])
            plan.do("tensor", lambda E: E.nop(), after=[ee])
            tile_done.append(ee)
        done.extend(tile_done)
        prev = tile_done
    return done



def build_ln0():
    nc = bass.Bass("TRN2", target_bir_lowering=False)
    xin = nc.dram_tensor("xin", [D, OWN], F32, kind="ExternalInput").ap()
    xn_out = nc.dram_tensor("xn", [D, OWN], F32, kind="ExternalOutput").ap()
    with ExitStack() as ctx:
        plan = Plan(nc)
        sb = nc.alloc_sbuf_tensor
        xf32 = sb("xf32", [128, 8 * OWN], F32)
        xo = sb("xo", [128, 8 * OWN], F32)
        identf, ev_id = alloc_identity(nc, plan, F32, name="identf")
        eps_col, ev_eps = const_col(nc, plan, "ln0_eps", 1e-5)
        x_t = xin.rearrange("(c k) t -> c k t", k=128)
        ev_ld = []
        for c in range(8):
            ev_ld.append(plan.dma("sync", xf32.ap()[:, bass.ts(c, OWN)], x_t[c]))
        evs = emit_ln_tm(nc, plan, "ln0", NT, OWN, xf32, xo, identf, eps_col,
                         ev_eps, ev_id, ev_ld)
        st_evs = []
        xo_t = xn_out.rearrange("(c k) t -> c k t", k=128)
        for c in range(8):
            st_evs.append(plan.dma("sync", xo_t[c], xo.ap()[:, bass.ts(c, OWN)],
                                   after=evs))
        plan.wait("sync", st_evs)
        plan.finalize(ctx)
    return nc



HALO_I = 5
TS_I = OWN + HALO_I * TILE
NTS_I = TS_I // TILE


def build_inner_attn():
    nc = bass.Bass("TRN2", target_bir_lowering=False)
    xnh = nc.dram_tensor("xnh", [D, TS_I], F32, kind="ExternalInput").ap()
    xres = nc.dram_tensor("xres", [D, OWN], F32, kind="ExternalInput").ap()
    wqkv = nc.dram_tensor("wqkv", [D, 1536], F32, kind="ExternalInput").ap()
    wo = nc.dram_tensor("wo", [512, D], F32, kind="ExternalInput").ap()
    tabs_in = {}
    for t in ("tqc", "tqs", "tkc", "tks"):
        tabs_in[t] = nc.dram_tensor(t, [TS_I, 64], F32, kind="ExternalInput").ap()
    qsb_in = nc.dram_tensor("qsb", [128, 512], F32, kind="ExternalInput").ap()
    ksb_in = nc.dram_tensor("ksb", [128, 512], F32, kind="ExternalInput").ap()
    maskT_in = nc.dram_tensor("maskT", [NT, 5, 128, 128], BF16, kind="ExternalInput").ap()
    xmid_out = nc.dram_tensor("xmid", [D, OWN], F32, kind="ExternalOutput").ap()

    with ExitStack() as ctx:
        plan = Plan(nc)
        sb = nc.alloc_sbuf_tensor
        ps = nc.alloc_psum_tensor

        xnbf = sb("xnbf", [128, 8 * TS_I], BF16)
        wqkv_bf = sb("wqkv_bf", [128, 8 * 1536], BF16)
        wo_bf = sb("wo_bf", [128, 4 * 1024], BF16)
        tb_s = sb("tb_s", [128, 2 * 256], F32)
        qsb = sb("sb_qsb", [128, 512], F32)
        ksb = sb("sb_ksb", [128, 512], F32)
        kfm = sb("kfm", [128, 4 * TS_I], BF16)
        qfm = sb("qfm", [128, 4 * OWN], BF16)
        vbuf = sb("vbuf", [128, NTS_I * 8 * 65], BF16)
        ofm = sb("ofm", [128, 4 * OWN], BF16)
        sq_s = sb("sq_s", [128, 512], F32)
        ss_s = sb("ss_s", [128, 32], F32)
        k1_s = sb("k1_s", [128, 512], F32)
        k2_s = sb("k2_s", [128, 512], F32)
        ra_s = sb("ra_s", [128, 512], F32)
        rb_s = sb("rb_s", [128, 512], F32)
        ktm_s = sb("ktm_s", [128, 512], BF16)
        pm_s2 = [sb("pm_s0", [128, 640], BF16), sb("pm_s1", [128, 640], BF16)]
        pmm_s2 = [sb("pmm_s0", [128, 640], BF16), sb("pmm_s1", [128, 640], BF16)]
        r_s2 = [sb("r_s0", [128, 1], F32), sb("r_s1", [128, 1], F32)]
        on_s2 = [sb("on_s0", [128, 64], BF16), sb("on_s1", [128, 64], BF16)]
        mk_s = sb("mk_s", [128, 2 * 5 * 128], BF16)
        xr_s = sb("xr_s", [128, 8 * 512], F32)
        xm_s = sb("xm_s", [128, 8 * 512], F32)
        bias_col = sb("bias_col", [128, 1], F32)
        ones_row = sb("ones_row", [1, 128], F32)

        ps_q = ps("ps_q", [128, 512], F32)
        ps_k = ps("ps_k", [128, 512], F32)
        ps_v = ps("ps_v", [128, 512], F32)
        ps_st2 = [ps("ps_st0", [128, 640], F32), ps("ps_st1", [128, 640], F32)]
        ps_tpd = ps("ps_tpd", [128, 256], BF16)

        identb, ev_id = alloc_identity(nc, plan, BF16)
        ev_vones = plan.do("gpsimd", lambda E: E.memset(vbuf.ap(), 1.0))
        ev_or = plan.do("vector", lambda E: E.memset(ones_row.ap(), 1.0))

        ev_w = []
        for c in range(8):
            ev_w.append(plan.dma("gpsimd", wqkv_bf.ap()[:, bass.ts(c, 1536)],
                                 wqkv.rearrange("(c k) n -> c k n", k=128)[c]))
        for c in range(4):
            ev_w.append(plan.dma("gpsimd", wo_bf.ap()[:, bass.ts(c, 1024)],
                                 wo.rearrange("(c k) n -> c k n", k=128)[c]))
        ev_xn = []
        for c in range(8):
            ev_xn.append(plan.dma("gpsimd", xnbf.ap()[:, bass.ts(c, TS_I)],
                                  xnh.rearrange("(c k) t -> c k t", k=128)[c]))
        ev_sc = [plan.dma("sync", qsb.ap(), qsb_in),
                 plan.dma("sync", ksb.ap(), ksb_in)]

        e = plan.do("vector", lambda E: E.tensor_mul(sq_s.ap(), qsb.ap(), ksb.ap()), after=ev_sc)
        e = plan.do("vector", lambda E: E.reduce_max(ss_s.ap()[:, 30:31], sq_s.ap(), axis=AX,
                                                     apply_absolute_value=True), after=[e], gate=True)
        e = plan.do("vector", lambda E: E.tensor_scalar(ss_s.ap()[:, 31:32], ss_s.ap()[:, 30:31],
                                                        -16.0, 60.0, ALU.mult, ALU.add), after=[e], gate=True)
        ev_bias = plan.do("vector", lambda E: E.tensor_scalar_min(bias_col.ap(), ss_s.ap()[:, 31:32], 0.0),
                          after=[e], gate=True)

        NH = 8
        kfm_done = {}
        vb_done = {}
        qfm_done = {}
        prev = ev_w + ev_xn + ev_sc + [ev_id, ev_vones, ev_bias]

        def qk_path(s, which, ps_in, ev_mm, prev, ev_tb):
            slot = (s % 2) * 256
            tco = slot + (0 if which == "q" else 128)
            tso = slot + (64 if which == "q" else 192)
            scb = qsb if which == "q" else ksb
            tgt = qfm if which == "q" else kfm
            tcol = (s - HALO_I) * TILE if which == "q" else s * TILE
            ssv = ss_s.ap()[:, 0:8]
            srv = ss_s.ap()[:, 8:16]
            riv = ss_s.ap()[:, 16:24]
            p3 = lambda a: a.rearrange("p (h d) -> p h d", h=NH)
            b3 = lambda a: a.rearrange("p (u d) -> p u d", u=1).broadcast_to([128, NH, 64])
            e1 = plan.do("scalar", lambda E: E.activation(sq_s.ap(), ps_in.ap(), AF.Square),
                         after=[ev_mm] + prev)
            e2 = plan.do("vector", lambda E: E.reduce_sum(
                ss_s.ap()[:, 0:8].rearrange("p (h u) -> p h u", u=1), p3(sq_s.ap()), axis=AX), after=[e1])
            e3 = plan.do("scalar", lambda E: E.activation(srv, ssv, AF.Sqrt), after=[e2], gate=True)
            e4 = plan.do("vector", lambda E: E.tensor_scalar_max(ss_s.ap()[:, 24:32], srv, 1e-12), after=[e3])
            e5 = plan.do("vector", lambda E: E.reciprocal(riv, ss_s.ap()[:, 24:32]), after=[e4], gate=True)
            e6 = plan.do("vector", lambda E: E.tensor_mul(
                p3(k1_s.ap()), p3(ps_in.ap()),
                riv.rearrange("p (h u) -> p h u", u=1).broadcast_to([128, NH, 64])), after=[e5], gate=True)
            e7 = plan.do("gpsimd", lambda E: E.tensor_mul(k2_s.ap(), k1_s.ap(), scb.ap()), after=[e6])
            tc_b = tb_s.ap()[:, tco:tco + 64].rearrange("p (u d) -> p u d", u=1).broadcast_to([128, NH, 64])
            ts_lo = tb_s.ap()[:, tso:tso + 32].rearrange("p (u d) -> p u d", u=1).broadcast_to([128, NH, 32])
            ts_hi = tb_s.ap()[:, tso + 32:tso + 64].rearrange("p (u d) -> p u d", u=1).broadcast_to([128, NH, 32])
            e8 = plan.do("vector", lambda E: E.tensor_mul(p3(ra_s.ap()), p3(k2_s.ap()), tc_b), after=[e7] + ev_tb)
            k2lo = p3(k2_s.ap())[:, :, 0:32]
            k2hi = p3(k2_s.ap())[:, :, 32:64]
            e9 = plan.do("gpsimd", lambda E: E.tensor_mul(p3(rb_s.ap())[:, :, 0:32], k2hi, ts_lo), after=[e7])
            e10 = plan.do("gpsimd", lambda E: E.tensor_mul(p3(rb_s.ap())[:, :, 32:64], k2lo, ts_hi), after=[e9])
            e11 = plan.do("vector", lambda E: E.tensor_add(ktm_s.ap(), ra_s.ap(), rb_s.ap()), after=[e8, e10])
            evs = []
            for cc in range(4):
                et = plan.do("tensor", lambda E, cc=cc: E.transpose(
                    ps_tpd.ap()[:, 0:128], ktm_s.ap()[:, bass.ts(cc, 128)], identb.ap()), after=[e11, ev_id])
                ee = plan.do("scalar", lambda E, cc=cc: E.activation(
                    tgt.ap()[:, cc * (TS_I if which == "k" else OWN) + tcol: cc * (TS_I if which == "k" else OWN) + tcol + 128],
                    ps_tpd.ap()[:, 0:128], AF.Identity), after=[et])
                evs.append(ee)
                plan.do("tensor", lambda E: E.nop(), after=[ee])
            return evs, e11

        tab_src = {t: tabs_in[t].rearrange("(s k) d -> s k d", k=128)
                   for t in ("tqc", "tqs", "tkc", "tks")}
        prev_tb = {}
        for s in range(1, NTS_I):
            slot = (s % 2) * 256
            ev_tb = []
            tdep = prev_tb.get(s - 2, [])
            for ti, t in enumerate(("tqc", "tqs", "tkc", "tks")):
                ev_tb.append(plan.dma("scalar", tb_s.ap()[:, slot + ti * 64: slot + (ti + 1) * 64],
                                      tab_src[t][s], after=tdep))
            ev_mm_q = ev_mm_k = ev_mm_v = None
            for kc in range(8):
                off = kc * TS_I + s * TILE - (1 if kc >= 4 else 0)
                lhs = xnbf.ap()[:, off: off + 128]
                ev_mm_q = plan.do("tensor", lambda E, lhs=lhs, kc=kc: E.matmul(
                    ps_q.ap(), lhs, wqkv_bf.ap()[:, kc * 1536 + 0: kc * 1536 + 512],
                    start=(kc == 0), stop=(kc == 7)), after=prev)
                ev_mm_k = plan.do("tensor", lambda E, lhs=lhs, kc=kc: E.matmul(
                    ps_k.ap(), lhs, wqkv_bf.ap()[:, kc * 1536 + 512: kc * 1536 + 1024],
                    start=(kc == 0), stop=(kc == 7)), after=prev)
                ev_mm_v = plan.do("tensor", lambda E, lhs=lhs, kc=kc: E.matmul(
                    ps_v.ap(), lhs, wqkv_bf.ap()[:, kc * 1536 + 1024: kc * 1536 + 1536],
                    start=(kc == 0), stop=(kc == 7)), after=prev)
            ev_v = plan.do("scalar", lambda E, s=s: E.activation(
                vbuf.ap()[:, s * 8 * 65: (s + 1) * 8 * 65].rearrange("p (h d) -> p h d", d=65)[:, :, 0:64],
                ps_v.ap().rearrange("p (h d) -> p h d", d=64), AF.Identity),
                after=[ev_mm_v, ev_vones])
            vb_done[s] = ev_v
            evk, lastk = qk_path(s, "k", ps_k, ev_mm_k, prev, ev_tb)
            kfm_done[s] = evk
            nxt = list(evk) + [ev_v]
            if s >= HALO_I:
                evq, lastq = qk_path(s, "q", ps_q, ev_mm_q, [lastk] + evk, ev_tb)
                qfm_done[s - HALO_I] = evq
                nxt += evq
            prev_tb[s] = [nxt[-1]]
            prev = nxt

        p1_done = prev
        attn_done = []
        last_use = {0: {"st": None, "o": None, "tp": None, "pm": None},
                    1: {"st": None, "o": None, "tp": None, "pm": None}}
        prev_masks = {0: p1_done, 1: p1_done}
        mask_ld = {}
        hi = 0
        for j in range(NT):
            mslot = j % 2
            mk_ap = mk_s.ap()[:, mslot * 640:(mslot + 1) * 640]
            ev_mk = None
            for k in range(5):
                ev_mk = plan.dma("sync", mk_ap[:, bass.ts(k, 128)], maskT_in[j][k],
                                 after=prev_masks[mslot] + p1_done)
            masked_chunks = list(range(5)) if j < 4 else [0, 4]
            for h in range(NH):
                pp = hi % 2
                ps_st = ps_st2[pp]
                pm_s, pmm_s, r_s, on_s = pm_s2[pp], pmm_s2[pp], r_s2[pp], on_s2[pp]
                ps_o_ap = (ps_v if pp == 0 else ps_q).ap()[:, 0:65]
                tp_ap = ps_tpd.ap()[0:64, pp * 128:pp * 128 + 128]
                lu = last_use[pp]
                cc = h // 2
                rows0 = (h % 2) * 64
                ev_st = None
                for k in range(5):
                    lhs = kfm.ap()[rows0:rows0 + 64, cc * TS_I + (j + 1 + k) * 128: cc * TS_I + (j + 1 + k) * 128 + 128]
                    rhs = qfm.ap()[rows0:rows0 + 64, cc * OWN + j * 128: cc * OWN + j * 128 + 128]
                    ev_st = plan.do("tensor", lambda E, lhs=lhs, rhs=rhs, k=k, ps_st=ps_st: E.matmul(
                        ps_st.ap()[:, bass.ts(k, 128)], lhs, rhs, start=True, stop=True),
                        after=[lu["st"]] + p1_done)
                ev_exp = plan.do("scalar", lambda E, pm_s=pm_s, ps_st=ps_st: E.activation(
                    pm_s.ap(), ps_st.ap(), AF.Exp, bias=bias_col.ap()[:, 0:1], scale=8.0),
                    after=[ev_st, ev_bias, lu["pm"]])
                ev_msk = []
                for k in masked_chunks:
                    ev_msk.append(plan.do("gpsimd", lambda E, k=k, mk_ap=mk_ap, pm_s=pm_s, pmm_s=pmm_s: E.tensor_mul(
                        pmm_s.ap()[:, bass.ts(k, 128)], pm_s.ap()[:, bass.ts(k, 128)],
                        mk_ap[:, bass.ts(k, 128)]), after=[ev_exp, ev_mk]))
                ev_pv = lu["o"]
                for k in range(5):
                    src = pmm_s if k in masked_chunks else pm_s
                    rhs = vbuf.ap()[:, ((j + 1 + k) * 8 + h) * 65: ((j + 1 + k) * 8 + h) * 65 + 65]
                    ev_pv = plan.do("tensor", lambda E, rhs=rhs, k=k, src=src, ps_o_ap=ps_o_ap: E.matmul(
                        ps_o_ap, src.ap()[:, bass.ts(k, 128)], rhs,
                        start=(k == 0), stop=(k == 4)),
                        after=[ev_exp, ev_pv] + ev_msk + [vb_done[j + 1 + k]])
                ev_r = plan.do("vector", lambda E, r_s=r_s, ps_o_ap=ps_o_ap: E.reciprocal(
                    r_s.ap(), ps_o_ap[:, 64:65]), after=[ev_pv])
                ev_on = plan.do("vector", lambda E, on_s=on_s, r_s=r_s, ps_o_ap=ps_o_ap: E.tensor_scalar_mul(
                    on_s.ap(), ps_o_ap[:, 0:64], r_s.ap()), after=[ev_r], gate=True)
                ev_tp = plan.do("tensor", lambda E, on_s=on_s, tp_ap=tp_ap: E.transpose(
                    tp_ap, on_s.ap(), identb.ap()), after=[ev_on, ev_id, lu["tp"]])
                ocol = cc * OWN + j * 128
                ev_o = plan.do("scalar", lambda E, ocol=ocol, rows0=rows0, tp_ap=tp_ap: E.activation(
                    ofm.ap()[rows0:rows0 + 64, ocol:ocol + 128], tp_ap,
                    AF.Identity), after=[ev_tp])
                attn_done.append(ev_o)
                lu["st"] = ev_exp
                lu["pm"] = ev_pv
                lu["o"] = ev_on
                lu["tp"] = ev_o
                hi += 1
            prev_masks[mslot] = attn_done[-8:]
        st_evs = []
        xr_done = {}
        prev_mc = attn_done[-8:]
        for g in range(4):
            xr_ap = xr_s.ap()[:, 0:8 * 512]
            dep = [] if g < 1 else [xr_done[g - 1]]
            ev_xr = []
            for c in range(8):
                ev_xr.append(plan.dma("sync", xr_ap[:, bass.ts(c, 512)],
                                      xres.rearrange("(c k) t -> c k t", k=128)[c][:, g * 512:(g + 1) * 512],
                                      after=dep))
            xm_done = []
            for mc in range(8):
                ev_mm = None
                for kc in range(4):
                    lhs = wo_bf.ap()[:, kc * 1024 + mc * 128: kc * 1024 + mc * 128 + 128]
                    rhs = ofm.ap()[:, kc * OWN + g * 512: kc * OWN + (g) * 512 + 512]
                    ev_mm = plan.do("tensor", lambda E, lhs=lhs, rhs=rhs, kc=kc: E.matmul(
                        ps_q.ap(), lhs, rhs, start=(kc == 0), stop=(kc == 3)),
                        after=attn_done + prev_mc)
                e = plan.do("vector", lambda E, mc=mc, xr_ap=xr_ap: E.scalar_tensor_tensor(
                    xm_s.ap()[:, bass.ts(mc, 512)], ps_q.ap(), 0.0,
                    xr_ap[:, bass.ts(mc, 512)], ALU.add, ALU.add),
                    after=[ev_mm] + ev_xr)
                xm_done.append(e)
                prev_mc = [e]
            ev_st_g = []
            for c in range(8):
                ev_st_g.append(plan.dma("sync", xmid_out.rearrange("(c k) t -> c k t", k=128)[c][:, g * 512:(g + 1) * 512],
                                        xm_s.ap()[:, bass.ts(c, 512)], after=xm_done))
            xr_done[g] = ev_st_g[-1]
            prev_mc = prev_mc + ev_st_g
            st_evs += ev_st_g
        plan.wait("sync", st_evs)
        plan.finalize(ctx)
    return nc



def rotary_tables_i(t0):
    pos = np.arange(t0 - HALO_I * TILE, t0 + OWN, dtype=np.float64)
    pos = np.clip(pos, 0, None)
    half = np.arange(0, 64, 2, dtype=np.float64)
    inv_freq = 1.0 / (10000.0 ** (half / 64.0))
    fr32 = pos[:, None] * inv_freq[None, :]
    base = (half + 0.4 * 64.0) / (1.4 * 64.0)
    power = (pos - SEQ // 2) / float(512 // 2)
    sc32 = base[None, :] ** power[:, None]
    cos = np.cos(fr32)
    sin = np.sin(fr32)
    cq32 = cos * sc32
    ck32 = cos / sc32
    sq32 = sin * sc32
    sk32 = sin / sc32
    def cat(a):
        return np.concatenate([a, a], axis=1).astype(np.float32)
    tqc = cat(cq32)
    tkc = cat(ck32)
    tqs = np.concatenate([-sq32, sq32], axis=1).astype(np.float32)
    tks = np.concatenate([-sk32, sk32], axis=1).astype(np.float32)
    return {"tqc": tqc, "tqs": tqs, "tkc": tkc, "tks": tks}


def masks_inner(t0):
    out = np.zeros((NT, 5, 128, 128), np.float32)
    r = np.arange(128)
    for j in range(NT):
        qt = t0 // 128 + j
        qtok = qt * 128 + r[None, :]
        for k in range(5):
            ktok = (qt - 4 + k) * 128 + r[:, None]
            valid = (ktok <= qtok) & (qtok - ktok <= 512) & (ktok >= 0)
            out[j, k] = valid
    return out.astype(ml_dtypes.bfloat16)


def inner_attn_inputs(xn_fm_b, x_res_fm_b, t0, wqkv_l, qs_l, ks_l, wo_l, tabs, maskT):
    lo = t0 - HALO_I * TILE
    if lo < 0:
        pad = np.zeros((D, -lo), np.float32)
        xnh = np.concatenate([pad, xn_fm_b[:, 0:t0 + OWN]], axis=1)
    else:
        xnh = xn_fm_b[:, lo:t0 + OWN]
    return {
        "xnh": np.ascontiguousarray(xnh),
        "xres": np.ascontiguousarray(x_res_fm_b[:, t0:t0 + OWN]),
        "wqkv": np.ascontiguousarray(wqkv_l),
        "wo": np.ascontiguousarray(wo_l),
        "qsb": np.ascontiguousarray(np.broadcast_to(np.tile(qs_l, 8)[None, :], (128, 512)).astype(np.float32)),
        "ksb": np.ascontiguousarray(np.broadcast_to(np.tile(ks_l, 8)[None, :], (128, 512)).astype(np.float32)),
        "maskT": maskT,
        **{k: np.ascontiguousarray(v) for k, v in tabs.items()},
    }



HID = 2730
HIDP = 2752
NBLK = 22


def _blk(j):
    off = 256 * j if j < 21 else 5376
    wa = 128 if j < 21 else 64
    return off, wa


def build_inner_ffn(ng_run=8, nblk_run=NBLK, do_w2=True, do_ln=7):
    nc = bass.Bass("TRN2", target_bir_lowering=False)
    xg_in = nc.dram_tensor("xg", [D, OWN + 1], F32, kind="ExternalInput").ap()
    w1_in = nc.dram_tensor("w1re", [D, 2 * HIDP], F32, kind="ExternalInput").ap()
    b1_in = nc.dram_tensor("b1re", [128, 2 * NBLK], F32, kind="ExternalInput").ap()
    w2_in = nc.dram_tensor("w2", [HIDP, D], F32, kind="ExternalInput").ap()
    b2_in = nc.dram_tensor("b2c", [128, 8], F32, kind="ExternalInput").ap()
    lng_in = nc.dram_tensor("lng", [128, 1024], F32, kind="ExternalInput").ap()
    lnb_in = nc.dram_tensor("lnb", [128, 1024], F32, kind="ExternalInput").ap()
    xo_out = nc.dram_tensor("xi1", [D, OWN], F32, kind="ExternalOutput").ap()

    G = 256
    NG = OWN // G
    TS = OWN + 1
    with ExitStack() as ctx:
        plan = Plan(nc)
        sb = nc.alloc_sbuf_tensor
        ps = nc.alloc_psum_tensor
        w1_bf = sb("w1_bf", [128, 8 * 2 * HIDP], BF16)
        w2_bf = sb("w2_bf", [128, NBLK * 1024], BF16)
        b1c = sb("b1c", [128, 2 * NBLK], F32)
        b2c = sb("b2c_s", [128, 8], F32)
        xg32 = sb("xg32", [128, 8 * (G + 1)], F32)
        xgbf = sb("xgbf", [128, 8 * G], BF16)
        h2n = sb("h2n", [128, 8 * G], BF16)
        h2tm = sb("h2tm", [128, 2 * 1024], BF16)
        sqtm = sb("sqtm", [128, 1024], F32)
        xhtm = sb("xhtm", [128, 2 * 1024], F32)
        hntm = sb("hntm", [128, 2 * 1024], BF16)
        cols = sb("cols_f", [128, 16], F32)
        lngb = sb("lngb", [128, 1024], F32)
        lnbb = sb("lnbb", [128, 1024], F32)
        gg = sb("gg_f", [128, G], F32)
        hg = sb("hg_f", [128, NBLK * G], BF16)
        xo_sb = sb("xo_sb", [128, 8 * G], F32)
        eps_col, ev_eps = const_col(nc, plan, "ffn_eps", 1e-5)
        identb, ev_id = alloc_identity(nc, plan, BF16, name="identb_f")

        ps_tp = ps("f_pstp", [128, 128], BF16)
        ps_a2 = [ps("f_psa0", [128, 512], F32), ps("f_psa1", [128, 512], F32)]
        ps_g2 = [ps("f_psg0", [128, 512], F32), ps("f_psg1", [128, 512], F32)]
        ps_acc2 = [ps("f_acc0", [128, 512], F32), ps("f_acc1", [128, 512], F32)]
        gg2 = [sb("gg_f0", [128, 256], F32), sb("gg_f1", [128, 256], F32)]

        ev_w = []
        for c in range(8):
            ev_w.append(plan.dma("gpsimd", w1_bf.ap()[:, bass.ts(c, 2 * HIDP)],
                                 w1_in.rearrange("(c k) n -> c k n", k=128)[c]))
        for j in range(NBLK):
            _, wa = _blk(j)
            ev_w.append(plan.dma("gpsimd", w2_bf.ap()[0:wa, bass.ts(j, 1024)],
                                 w2_in[j * 128: j * 128 + wa, :]))
        ev_w.append(plan.dma("sync", b1c.ap(), b1_in))
        ev_w.append(plan.dma("sync", b2c.ap(), b2_in))
        ev_w.append(plan.dma("sync", lngb.ap(), lng_in))
        ev_w.append(plan.dma("sync", lnbb.ap(), lnb_in))
        ev_w += [ev_eps, ev_id]

        xg_t = xg_in.rearrange("(c k) t -> c k t", k=128)
        st_evs = []
        prev = list(ev_w)
        for g in range(min(NG, ng_run)):
            ev_ld = []
            for c in range(8):
                e = plan.dma("sync", xg32.ap()[:, c * (G + 1): (c + 1) * (G + 1)],
                             xg_t[c][:, g * G: g * G + G + 1], after=prev)
                off = c * (G + 1) + (1 if c < 4 else 0)
                e = plan.do("scalar", lambda E, c=c, off=off: E.activation(
                    xgbf.ap()[:, bass.ts(c, G)],
                    xg32.ap()[:, off: off + G], AF.Identity), after=[e])
                ev_ld.append(e)

            def h2bf(c):
                return xgbf.ap()[:, bass.ts(c, G)]

            def h232(c):
                off = c * (G + 1) + (1 if c < 4 else 0)
                return xg32.ap()[:, off: off + G]

            def own32(c):
                return xg32.ap()[:, c * (G + 1) + 1: c * (G + 1) + 1 + G]

            ev_h2n = []
            prev_sub = list(ev_ld)
            for sub in range(2):
                ev_tp = []
                for c in range(8):
                    et = plan.do("tensor", lambda E, c=c, sub=sub: E.transpose(
                        ps_tp.ap(), h2bf(c)[:, sub * 128:(sub + 1) * 128], identb.ap()),
                        after=[ev_id] + prev_sub)
                    ee = plan.do("scalar", lambda E, c=c, sub=sub: E.activation(
                        h2tm.ap()[:, sub * 1024 + c * 128: sub * 1024 + (c + 1) * 128],
                        ps_tp.ap(), AF.Identity), after=[et])
                    eg = plan.do("tensor", lambda E: E.nop(), after=[ee])
                    ev_tp.append(ee)
                tm = h2tm.ap()[:, sub * 1024:(sub + 1) * 1024]
                s1c = cols.ap()[:, 0:1]
                s2c = cols.ap()[:, 1:2]
                mc_ = cols.ap()[:, 2:3]
                vc_ = cols.ap()[:, 3:4]
                t0c = cols.ap()[:, 4:5]
                rsc = cols.ap()[:, 5:6]
                e1 = plan.do("vector", lambda E, tm=tm: E.reduce_sum(s1c, tm, axis=AX), after=ev_tp)
                eq = plan.do("scalar", lambda E, tm=tm: E.activation(sqtm.ap(), tm, AF.Square), after=ev_tp)
                e2 = plan.do("vector", lambda E: E.reduce_sum(s2c, sqtm.ap(), axis=AX), after=[eq, e1])
                e3 = plan.do("vector", lambda E: E.tensor_scalar_mul(mc_, s1c, 1.0 / D), after=[e2])
                e4 = plan.do("vector", lambda E: E.tensor_scalar(vc_, mc_, mc_, None, ALU.mult), after=[e3])
                e5 = plan.do("vector", lambda E: E.scalar_tensor_tensor(
                    t0c, s2c, 1.0 / D, vc_, ALU.mult, ALU.subtract), after=[e4])
                e6 = plan.do("scalar", lambda E: E.activation(t0c, t0c, AF.Ln, bias=eps_col.ap()[:, 0:1]), after=[e5, ev_eps])
                e7 = plan.do("scalar", lambda E: E.activation(rsc, t0c, AF.Exp, scale=-0.5), after=[e6])
                xh = xhtm.ap()[:, sub * 1024:(sub + 1) * 1024]
                hn = hntm.ap()[:, sub * 1024:(sub + 1) * 1024]
                e8 = plan.do("vector", lambda E, tm=tm, xh=xh: E.tensor_scalar(
                    xh, tm, mc_, rsc, ALU.subtract, ALU.mult), after=[e7])
                e9 = plan.do("gpsimd", lambda E, xh=xh: E.tensor_mul(sqtm.ap(), xh, lngb.ap()), after=[e8])
                e9 = plan.do("vector", lambda E, hn=hn: E.tensor_add(hn, sqtm.ap(), lnbb.ap()), after=[e9])
                for c in range(8):
                    et = plan.do("tensor", lambda E, c=c, sub=sub, hn=hn: E.transpose(
                        ps_tp.ap(), hn[:, c * 128:(c + 1) * 128], identb.ap()),
                        after=[e9, ev_id])
                    ee = plan.do("scalar", lambda E, c=c, sub=sub: E.activation(
                        h2n.ap()[:, c * G + sub * 128: c * G + sub * 128 + 128],
                        ps_tp.ap(), AF.Identity), after=[et])
                    eg = plan.do("tensor", lambda E: E.nop(), after=[ee])
                    ev_h2n.append(ee)
                prev_sub = ev_h2n[-8:]

            ev_hg_all = []
            lu_a = {0: None, 1: None}
            for j in range(min(NBLK, nblk_run)):
                pp = j % 2
                ps_a = ps_a2[pp]
                ps_g = ps_g2[pp]
                gg = gg2[pp]
                off, wa = _blk(j)
                dep = [lu_a[pp]] if lu_a[pp] is not None else list(ev_h2n)
                ea = eg = None
                for kc in range(8):
                    la = w1_bf.ap()[:, kc * 2 * HIDP + off: kc * 2 * HIDP + off + wa]
                    lg = w1_bf.ap()[:, kc * 2 * HIDP + off + wa: kc * 2 * HIDP + off + 2 * wa]
                    ea = plan.do("tensor", lambda E, la=la, kc=kc, wa=wa, ps_a=ps_a: E.matmul(
                        ps_a.ap()[0:wa, 0:G], la, h2n.ap()[:, bass.ts(kc, G)],
                        start=(kc == 0), stop=(kc == 7)), after=dep + ev_h2n)
                    eg = plan.do("tensor", lambda E, lg=lg, kc=kc, wa=wa, ps_g=ps_g: E.matmul(
                        ps_g.ap()[0:wa, 0:G], lg, h2n.ap()[:, bass.ts(kc, G)],
                        start=(kc == 0), stop=(kc == 7)), after=dep + ev_h2n)
                e_gg = plan.do("scalar", lambda E, j=j, wa=wa, ps_g=ps_g, gg=gg: E.activation(
                    gg.ap()[0:wa, :], ps_g.ap()[0:wa, 0:G], AF.Gelu,
                    bias=b1c.ap()[0:wa, 2 * j + 1: 2 * j + 2]), after=[eg])
                e_hg = plan.do("vector", lambda E, j=j, wa=wa, ps_a=ps_a, gg=gg: E.scalar_tensor_tensor(
                    hg.ap()[0:wa, bass.ts(j, G)], ps_a.ap()[0:wa, 0:G], b1c.ap()[0:wa, 2 * j: 2 * j + 1],
                    gg.ap()[0:wa, :], ALU.add, ALU.mult), after=[ea, e_gg])
                lu_a[pp] = e_hg
                ev_hg_all.append(e_hg)
            xm_done = []
            lu_c = {0: None, 1: None}
            for mc in range(8):
                if not do_w2:
                    e = plan.do("vector", lambda E, mc=mc: E.tensor_copy(
                        xo_sb.ap()[:, bass.ts(mc, G)], own32(mc)), after=ev_hg_all + ev_h2n)
                    xm_done.append(e)
                    continue
                pp = mc % 2
                ps_acc = ps_acc2[pp]
                dep2 = [lu_c[pp]] if lu_c[pp] is not None else []
                ev_mm2 = None
                for j in range(min(NBLK, nblk_run)):
                    off, wa = _blk(j)
                    lw = w2_bf.ap()[0:wa, j * 1024 + mc * 128: j * 1024 + mc * 128 + 128]
                    ev_mm2 = plan.do("tensor", lambda E, lw=lw, j=j, wa=wa, ps_acc=ps_acc: E.matmul(
                        ps_acc.ap()[:, 0:G], lw, hg.ap()[0:wa, bass.ts(j, G)],
                        start=(j == 0), stop=(j == min(NBLK, nblk_run) - 1)),
                        after=ev_hg_all + dep2)
                e = plan.do("vector", lambda E, mc=mc, ps_acc=ps_acc: E.scalar_tensor_tensor(
                    xo_sb.ap()[:, bass.ts(mc, G)], ps_acc.ap()[:, 0:G], b2c.ap()[:, mc:mc + 1],
                    own32(mc), ALU.add, ALU.add), after=[ev_mm2] + ev_h2n)
                xm_done.append(e)
                lu_c[pp] = e
            ev_st = []
            for c in range(8):
                ev_st.append(plan.dma("sync", xo_out.rearrange("(c k) t -> c k t", k=128)[c][:, g * G:(g + 1) * G],
                                      xo_sb.ap()[:, bass.ts(c, G)], after=xm_done))

            st_evs += ev_st
            prev = xm_done + ev_st
        plan.wait("sync", st_evs)
        plan.finalize(ctx)
    return nc


def inner_ffn_inputs(xm_fm_b, t0, w1_l, b1_l, w2_l, b2_l, lng_l, lnb_l):
    if t0 == 0:
        first = np.zeros((D, 1), np.float32)
    else:
        first = xm_fm_b[:, t0 - 1:t0]
    xg = np.concatenate([first, xm_fm_b[:, t0:t0 + OWN]], axis=1)
    w1p = np.zeros((D, 2 * HIDP), np.float32)
    b1cols = np.zeros((128, 2 * NBLK), np.float32)
    for j in range(NBLK):
        off, wa = _blk(j)
        wreal = 128 if j < 21 else 42
        a0 = 128 * j
        ca = np.arange(a0, a0 + wreal)
        cg = HID + ca
        w1p[:, off:off + wreal] = w1_l[:, ca]
        w1p[:, off + wa:off + wa + wreal] = w1_l[:, cg]
        b1cols[0:wreal, 2 * j] = b1_l[ca]
        b1cols[0:wreal, 2 * j + 1] = b1_l[cg]
    w1re = np.ascontiguousarray(w1p)
    w2p = np.zeros((HIDP, D), np.float32)
    w2p[0:HID] = w2_l
    return {
        "xg": np.ascontiguousarray(xg),
        "w1re": w1re,
        "b1re": b1cols,
        "w2": w2p,
        "b2c": vec_cols(b2_l),
        "lng": np.ascontiguousarray(np.broadcast_to(np.asarray(lng_l, np.float32)[None, :], (128, 1024))),
        "lnb": np.ascontiguousarray(np.broadcast_to(np.asarray(lnb_l, np.float32)[None, :], (128, 1024))),
    }



NTW = 19
TSW = NTW * TILE


def build_attn0():
    nc = bass.Bass("TRN2", target_bir_lowering=False)
    xnw_in = nc.dram_tensor("xnw", [D, TSW], F32, kind="ExternalInput").ap()
    wq_in = nc.dram_tensor("wq", [D, D], F32, kind="ExternalInput").ap()
    wk_in = nc.dram_tensor("wk", [D, 256], F32, kind="ExternalInput").ap()
    wv_in = nc.dram_tensor("wv", [D, 256], F32, kind="ExternalInput").ap()
    wo_in = nc.dram_tensor("wo0", [D, D], F32, kind="ExternalInput").ap()
    maskT_in = nc.dram_tensor("maskT0", [NT, 3, 128, 128], BF16, kind="ExternalInput").ap()
    maskS_in = nc.dram_tensor("maskS0", [NT, 4, 128], BF16, kind="ExternalInput").ap()
    a0_out = nc.dram_tensor("a0", [D, OWN], F32, kind="ExternalOutput").ap()

    NH = 16
    with ExitStack() as ctx:
        plan = Plan(nc)
        sb = nc.alloc_sbuf_tensor
        ps = nc.alloc_psum_tensor
        xnw_bf = sb("xnw_bf", [128, 8 * TSW], BF16)
        wq_bf = sb("wq_bf", [128, 8 * 1024], BF16)
        wk_bf = sb("wk_bf", [128, 8 * 256], BF16)
        wv_bf = sb("wv_bf", [128, 8 * 256], BF16)
        wo_bf = sb("wo_bf", [128, 8 * 1024], BF16)
        kfm = sb("a_kfm", [128, 2 * TSW], BF16)
        vbuf = sb("a_vbuf", [128, NTW * 4 * 65], BF16)
        qfm = sb("a_qfm", [128, 8 * OWN], BF16)
        ofm = sb("a_ofm", [128, 8 * OWN], BF16)
        pm_s = sb("a_pm", [128, 512], BF16)
        pmm_s = sb("a_pmm", [128, 512], BF16)
        mk_s = sb("a_mk", [128, 3 * 128], BF16)
        mks_s = sb("a_mks", [4, 16 * 128], BF16)
        r_s = sb("a_r", [128, 1], F32)
        on_s = sb("a_on", [128, 64], BF16)
        xr_s = sb("a_xr", [128, 8 * 512], F32)
        bias20, ev_b20 = const_col(nc, plan, "a_bias20", -20.0)
        identb, ev_idb = alloc_identity(nc, plan, BF16, name="a_identb")

        ps_qkv = ps("a_psqkv", [128, 512], F32)
        ps_st = ps("a_psst", [128, 512], F32)
        ps_tp = ps("a_pstp", [128, 128], BF16)
        ps_o = ps("a_pso", [128, 128], F32)
        ps_x = ps("a_psx", [128, 512], F32)

        ev_vones = plan.do("gpsimd", lambda E: E.memset(vbuf.ap(), 1.0))

        ev_w = []
        for c in range(8):
            ev_w.append(plan.dma("gpsimd", wq_bf.ap()[:, bass.ts(c, 1024)],
                                 wq_in.rearrange("(c k) n -> c k n", k=128)[c]))
            ev_w.append(plan.dma("gpsimd", wk_bf.ap()[:, bass.ts(c, 256)],
                                 wk_in.rearrange("(c k) n -> c k n", k=128)[c]))
            ev_w.append(plan.dma("gpsimd", wv_bf.ap()[:, bass.ts(c, 256)],
                                 wv_in.rearrange("(c k) n -> c k n", k=128)[c]))
            ev_w.append(plan.dma("gpsimd", wo_bf.ap()[:, bass.ts(c, 1024)],
                                 wo_in.rearrange("(c k) n -> c k n", k=128)[c]))
            ev_w.append(plan.dma("gpsimd", xnw_bf.ap()[:, bass.ts(c, TSW)],
                                 xnw_in.rearrange("(c k) t -> c k t", k=128)[c]))
        for j in range(NT):
            ev_w.append(plan.dma("scalar", mks_s.ap()[:, bass.ts(j, 128)], maskS_in[j]))
        prev = ev_w + [ev_vones, ev_b20, ev_idb]

        vb_done = {}
        ngrp = [(0, 512), (512, 512), (1024, 512), (1536, 512), (2048, 384)]
        ev_kv = []
        for mc in range(2):
            for (n0, nn) in ngrp:
                e = None
                for kc in range(8):
                    lhs = wk_bf.ap()[:, kc * 256 + mc * 128: kc * 256 + mc * 128 + 128]
                    e = plan.do("tensor", lambda E, lhs=lhs, kc=kc, n0=n0, nn=nn: E.matmul(
                        ps_qkv.ap()[:, 0:nn], lhs, xnw_bf.ap().rearrange(
                            "p (c t) -> p c t", t=TSW)[:, :, n0:n0 + nn].rearrange("p c t -> p (c t)") if False else xnw_bf.ap()[:, kc * TSW + n0: kc * TSW + n0 + nn],
                        start=(kc == 0), stop=(kc == 7)), after=prev)
                ee = plan.do("scalar", lambda E, mc=mc, n0=n0, nn=nn: E.activation(
                    kfm.ap()[:, mc * TSW + n0: mc * TSW + n0 + nn], ps_qkv.ap()[:, 0:nn],
                    AF.Identity), after=[e])
                ev_kv.append(ee)
                prev = [ee]
        for t in range(NTW):
            e = None
            for kc in range(8):
                lhs = xnw_bf.ap()[:, kc * TSW + t * 128: kc * TSW + t * 128 + 128]
                e = plan.do("tensor", lambda E, lhs=lhs, kc=kc: E.matmul(
                    ps_qkv.ap()[:, 0:256], lhs, wv_bf.ap()[:, bass.ts(kc, 256)],
                    start=(kc == 0), stop=(kc == 7)), after=prev)
            ee = plan.do("scalar", lambda E, t=t: E.activation(
                vbuf.ap()[:, t * 260: (t + 1) * 260].rearrange("p (h d) -> p h d", d=65)[:, :, 0:64],
                ps_qkv.ap()[:, 0:256].rearrange("p (h d) -> p h d", d=64), AF.Identity),
                after=[e, ev_vones])
            vb_done[t] = ee
            prev = [ee]
        ev_q = []
        for mc in range(8):
            for g in range(4):
                e = None
                for kc in range(8):
                    lhs = wq_bf.ap()[:, kc * 1024 + mc * 128: kc * 1024 + mc * 128 + 128]
                    rhs = xnw_bf.ap()[:, kc * TSW + 3 * 128 + g * 512: kc * TSW + 3 * 128 + g * 512 + 512]
                    e = plan.do("tensor", lambda E, lhs=lhs, rhs=rhs, kc=kc: E.matmul(
                        ps_qkv.ap(), lhs, rhs, start=(kc == 0), stop=(kc == 7)), after=prev)
                ee = plan.do("scalar", lambda E, mc=mc, g=g: E.activation(
                    qfm.ap()[:, mc * OWN + g * 512: mc * OWN + (g + 1) * 512], ps_qkv.ap(),
                    AF.Identity), after=[e])
                ev_q.append(ee)
                prev = [ee]

        attn_done = []
        ev_prev_head = list(prev) + ev_kv
        mask_prev = []
        for j in range(NT):
            ev_mk = None
            for k in range(3):
                ev_mk = plan.dma("sync", mk_s.ap()[:, bass.ts(k, 128)], maskT_in[j][k],
                                 after=mask_prev)
            masked_chunks = [0, 1, 2] if j < 4 else [0, 2]
            for h in range(NH):
                g = h // 4
                cc2 = g // 2
                rows0 = (g % 2) * 64
                qcc = (h % 4) + 4 * (h // 8)
                qrows0 = ((h // 4) % 2) * 64
                occ = h // 2
                orows0 = (h % 2) * 64
                ev_st = None
                for k in range(3):
                    lhs = kfm.ap()[rows0:rows0 + 64, cc2 * TSW + (j + 1 + k) * 128: cc2 * TSW + (j + 1 + k) * 128 + 128]
                    rhs = qfm.ap()[qrows0:qrows0 + 64, qcc * OWN + j * 128: qcc * OWN + j * 128 + 128]
                    ev_st = plan.do("tensor", lambda E, lhs=lhs, rhs=rhs, k=k: E.matmul(
                        ps_st.ap()[:, bass.ts(k, 128)], lhs, rhs, start=True, stop=True),
                        after=ev_prev_head)
                lhs_sink = kfm.ap()[rows0:rows0 + 64, cc2 * TSW + 0: cc2 * TSW + 4]
                rhs_q = qfm.ap()[qrows0:qrows0 + 64, qcc * OWN + j * 128: qcc * OWN + j * 128 + 128]
                ev_sk = plan.do("tensor", lambda E, lhs_sink=lhs_sink, rhs_q=rhs_q: E.matmul(
                    ps_st.ap()[0:4, 384:512], lhs_sink, rhs_q, start=True, stop=True),
                    after=ev_prev_head)
                ev_exp = plan.do("scalar", lambda E: E.activation(
                    pm_s.ap()[:, 0:384], ps_st.ap()[:, 0:384], AF.Exp,
                    bias=bias20.ap()[:, 0:1], scale=0.125), after=[ev_st, ev_b20])
                ev_exp2 = plan.do("scalar", lambda E: E.activation(
                    pm_s.ap()[0:4, 384:512], ps_st.ap()[0:4, 384:512], AF.Exp,
                    bias=bias20.ap()[0:4, 0:1], scale=0.125), after=[ev_sk, ev_exp])
                ev_msk = []
                for k in masked_chunks:
                    ev_msk.append(plan.do("gpsimd", lambda E, k=k: E.tensor_mul(
                        pmm_s.ap()[:, bass.ts(k, 128)], pm_s.ap()[:, bass.ts(k, 128)],
                        mk_s.ap()[:, bass.ts(k, 128)]), after=[ev_exp, ev_mk]))
                ev_msk.append(plan.do("gpsimd", lambda E, j=j: E.tensor_mul(
                    pmm_s.ap()[0:4, 384:512], pm_s.ap()[0:4, 384:512],
                    mks_s.ap()[:, bass.ts(j, 128)]), after=[ev_exp2]))
                ev_pv = None
                for k in range(3):
                    src = pmm_s if k in masked_chunks else pm_s
                    rhs = vbuf.ap()[:, ((j + 1 + k) * 4 + g) * 65: ((j + 1 + k) * 4 + g) * 65 + 65]
                    ev_pv = plan.do("tensor", lambda E, rhs=rhs, k=k, src=src: E.matmul(
                        ps_o.ap()[:, 0:65], src.ap()[:, bass.ts(k, 128)], rhs,
                        start=(k == 0), stop=False),
                        after=[ev_exp] + ev_msk + [vb_done[j + 1 + k]])
                rhs_vs = vbuf.ap()[0:4, (0 * 4 + g) * 65: (0 * 4 + g) * 65 + 65]
                ev_pv = plan.do("tensor", lambda E, rhs_vs=rhs_vs: E.matmul(
                    ps_o.ap()[:, 0:65], pmm_s.ap()[0:4, 384:512], rhs_vs,
                    start=False, stop=True), after=[ev_pv] + ev_msk + [vb_done[0]])
                ev_r = plan.do("vector", lambda E: E.reciprocal(r_s.ap(), ps_o.ap()[:, 64:65]),
                               after=[ev_pv])
                ev_on = plan.do("vector", lambda E: E.tensor_scalar_mul(
                    on_s.ap(), ps_o.ap()[:, 0:64], r_s.ap()), after=[ev_r], gate=True)
                ev_tp = plan.do("tensor", lambda E: E.transpose(
                    ps_tp.ap()[0:64, 0:128], on_s.ap(), identb.ap()), after=[ev_on, ev_idb])
                ocol = occ * OWN + j * 128
                ev_o = plan.do("scalar", lambda E, ocol=ocol, orows0=orows0: E.activation(
                    ofm.ap()[orows0:orows0 + 64, ocol:ocol + 128], ps_tp.ap()[0:64, :],
                    AF.Identity), after=[ev_tp])
                attn_done.append(ev_o)
                ev_prev_head = [ev_exp2, ev_on, ev_o]
            mask_prev = attn_done[-1:]

        st_evs = []
        prev_g = attn_done[-16:]
        a0_t = a0_out.rearrange("(c k) t -> c k t", k=128)
        for g in range(4):
            ev_prev_mc = list(prev_g)
            g_done = []
            for mc in range(8):
                e = None
                for kc in range(8):
                    lhs = wo_bf.ap()[:, kc * 1024 + mc * 128: kc * 1024 + mc * 128 + 128]
                    rhs = ofm.ap()[:, kc * OWN + g * 512: kc * OWN + g * 512 + 512]
                    e = plan.do("tensor", lambda E, lhs=lhs, rhs=rhs, kc=kc: E.matmul(
                        ps_x.ap(), lhs, rhs, start=(kc == 0), stop=(kc == 7)),
                        after=attn_done + ev_prev_mc)
                e = plan.do("scalar", lambda E, mc=mc: E.activation(
                    xr_s.ap()[:, bass.ts(mc, 512)], ps_x.ap(), AF.Identity), after=[e])
                ev_prev_mc = [e]
                g_done.append(e)
            for c in range(8):
                ev = plan.dma("sync", a0_t[c][:, g * 512:(g + 1) * 512],
                              xr_s.ap()[:, bass.ts(c, 512)], after=g_done)
                st_evs.append(ev)
            prev_g = g_done + st_evs[-8:]
        plan.wait("sync", st_evs)
        plan.finalize(ctx)
    return nc



def build_cmb():
    nc = bass.Bass("TRN2", target_bir_lowering=False)
    x_in = nc.dram_tensor("cx", [D, OWN], F32, kind="ExternalInput").ap()
    a_in = nc.dram_tensor("ca", [D, OWN], F32, kind="ExternalInput").ap()
    f_in = nc.dram_tensor("cf", [D, OWN], F32, kind="ExternalInput").ap()
    x1_out = nc.dram_tensor("x1", [D, OWN], F32, kind="ExternalOutput").ap()
    xn1_out = nc.dram_tensor("xn1", [D, OWN], F32, kind="ExternalOutput").ap()
    G = 512
    with ExitStack() as ctx:
        plan = Plan(nc)
        sb = nc.alloc_sbuf_tensor
        xg = sb("c_x", [128, 8 * G], F32)
        ag = sb("c_a", [128, 8 * G], F32)
        fg = sb("c_f", [128, 8 * G], F32)
        tg = sb("c_t", [128, 8 * G], F32)
        x1g = sb("c_x1", [128, 8 * G], F32)
        xn1g = sb("c_xn1", [128, 8 * G], F32)
        identf, ev_idf = alloc_identity(nc, plan, F32, name="c_identf")
        eps_col, ev_eps = const_col(nc, plan, "c_eps", 1e-5)
        scr = {}
        x_t = x_in.rearrange("(c k) t -> c k t", k=128)
        a_t = a_in.rearrange("(c k) t -> c k t", k=128)
        f_t = f_in.rearrange("(c k) t -> c k t", k=128)
        x1_t = x1_out.rearrange("(c k) t -> c k t", k=128)
        xn1_t = xn1_out.rearrange("(c k) t -> c k t", k=128)
        st_evs = []
        prev = [ev_idf, ev_eps]
        for g in range(OWN // G):
            ev_ld = []
            for c in range(8):
                ev_ld.append(plan.dma("sync", xg.ap()[:, bass.ts(c, G)],
                                      x_t[c][:, g * G:(g + 1) * G], after=prev))
                ev_ld.append(plan.dma("scalar", ag.ap()[:, bass.ts(c, G)],
                                      a_t[c][:, g * G:(g + 1) * G], after=prev))
                ev_ld.append(plan.dma("gpsimd", fg.ap()[:, bass.ts(c, G)],
                                      f_t[c][:, g * G:(g + 1) * G], after=prev))
            add_done = []
            for c in range(8):
                e = plan.do("vector", lambda E, c=c: E.tensor_add(
                    tg.ap()[:, bass.ts(c, G)], ag.ap()[:, bass.ts(c, G)],
                    fg.ap()[:, bass.ts(c, G)]), after=ev_ld)
                e = plan.do("gpsimd", lambda E, c=c: E.tensor_add(
                    x1g.ap()[:, bass.ts(c, G)], tg.ap()[:, bass.ts(c, G)],
                    xg.ap()[:, bass.ts(c, G)]), after=[e] + ev_ld)
                add_done.append(e)
            ev_st1 = []
            for c in range(8):
                ev_st1.append(plan.dma("sync", x1_t[c][:, g * G:(g + 1) * G],
                                       x1g.ap()[:, bass.ts(c, G)], after=add_done))
            ln_done = emit_ln_tm(nc, plan, f"cln{g}", 4, G, x1g, xn1g, identf,
                                 eps_col, ev_eps, ev_idf, add_done, scr=scr)
            ev_st2 = []
            for c in range(8):
                ev_st2.append(plan.dma("sync", xn1_t[c][:, g * G:(g + 1) * G],
                                       xn1g.ap()[:, bass.ts(c, G)], after=ln_done))
            st_evs += ev_st1 + ev_st2
            prev = ln_done + ev_st1 + ev_st2
        plan.wait("sync", st_evs)
        plan.finalize(ctx)
    return nc



def masks_attn0(t0):
    mw = np.zeros((NT, 3, 128, 128), np.float32)
    msk = np.zeros((NT, 4, 128), np.float32)
    r = np.arange(128)
    for j in range(NT):
        qt = t0 // 128 + j
        qtok = qt * 128 + r[None, :]
        for k in range(3):
            ktok = (qt - 2 + k) * 128 + r[:, None]
            valid = (ktok <= qtok) & (qtok - ktok <= 255) & (ktok >= 4)
            mw[j, k] = valid
        ksink = np.arange(4)[:, None]
        msk[j] = (ksink <= qtok)
    return mw.astype(ml_dtypes.bfloat16), msk.astype(ml_dtypes.bfloat16)


def attn0_inputs(xn_fm_b, t0, wq, wk, wv, wo):
    lo = t0 - 2 * TILE
    sink = xn_fm_b[:, 0:128]
    if lo < 0:
        pad = np.zeros((D, -lo), np.float32)
        win = np.concatenate([pad, xn_fm_b[:, 0:t0 + OWN]], axis=1)
    else:
        win = xn_fm_b[:, lo:t0 + OWN]
    xnw = np.concatenate([sink, win], axis=1)
    mw, msk = masks_attn0(t0)
    perm = np.zeros(1024, np.int64)
    for ck in range(8):
        hd0 = ck if ck < 4 else ck + 4
        hd1 = ck + 4 if ck < 4 else ck + 8
        perm[ck * 128: ck * 128 + 64] = np.arange(hd0 * 64, hd0 * 64 + 64)
        perm[ck * 128 + 64: ck * 128 + 128] = np.arange(hd1 * 64, hd1 * 64 + 64)
    wq = np.asarray(wq, np.float32)[:, perm]
    return {
        "xnw": np.ascontiguousarray(xnw),
        "wq": np.ascontiguousarray(wq), "wk": np.ascontiguousarray(wk),
        "wv": np.ascontiguousarray(wv), "wo0": np.ascontiguousarray(wo),
        "maskT0": mw, "maskS0": msk,
    }



NTW1 = 21
TSW1 = NTW1 * TILE


def build_l1attn():
    nc = bass.Bass("TRN2", target_bir_lowering=False)
    xnw_in = nc.dram_tensor("xnw1", [D, TSW1], F32, kind="ExternalInput").ap()
    wq_in = nc.dram_tensor("wq1", [D, D], F32, kind="ExternalInput").ap()
    wk_in = nc.dram_tensor("wk1", [D, D], F32, kind="ExternalInput").ap()
    wv_in = nc.dram_tensor("wv1", [D, D], F32, kind="ExternalInput").ap()
    wo_in = nc.dram_tensor("wo1", [D, D], F32, kind="ExternalInput").ap()
    maskT_in = nc.dram_tensor("maskT1", [NT, 5, 128, 128], BF16, kind="ExternalInput").ap()
    a1_out = nc.dram_tensor("a1", [D, OWN], F32, kind="ExternalOutput").ap()

    with ExitStack() as ctx:
        plan = Plan(nc)
        sb = nc.alloc_sbuf_tensor
        ps = nc.alloc_psum_tensor
        xnw_bf = sb("l_xnw", [128, 8 * TSW1], BF16)
        wbuf = sb("l_wbuf", [128, 8 * 1024], BF16)
        wo_bf = sb("l_wo", [128, 8 * 1024], BF16)
        kfm = sb("l_kfm", [128, 4 * TSW1], BF16)
        vbuf = sb("l_vbuf", [128, NTW1 * 8 * 65], BF16)
        qfm = sb("l_qfm", [128, 4 * OWN], BF16)
        ofm = sb("l_ofm", [128, 8 * OWN], BF16)
        pm_s2 = [sb("l_pm_0", [128, 640], BF16), sb("l_pm_1", [128, 640], BF16)]
        pmm_s2 = [sb("l_pmm_0", [128, 640], BF16), sb("l_pmm_1", [128, 640], BF16)]
        pm2_s2 = [sb("l_pm2_0", [16, 128], BF16), sb("l_pm2_1", [16, 128], BF16)]
        mk_s = sb("l_mk", [128, 2 * 5 * 128], BF16)
        r_s2 = [sb("l_r0", [128, 1], F32), sb("l_r1", [128, 1], F32)]
        on_s2 = [sb("l_on0", [128, 64], BF16), sb("l_on1", [128, 64], BF16)]
        ast_s = sb("l_ast", [128, 8 * 256], F32)
        bias20, ev_b20 = const_col(nc, plan, "l_bias20", -20.0)
        identb, ev_idb = alloc_identity(nc, plan, BF16, name="l_identb")

        ps_qkv = ps("l_psqkv", [128, 512], F32)
        ps_st2 = [ps("l_psst0", [128, 640], F32), ps("l_psst1", [128, 640], F32)]
        ps_sk = ps("l_pssk", [128, 256], F32)
        ps_tpd = ps("l_pstpd", [128, 256], BF16)
        ps_o = ps("l_pso", [128, 128], F32)

        ev_vones = plan.do("gpsimd", lambda E: E.memset(vbuf.ap(), 1.0))
        ev_xn = []
        for c in range(8):
            ev_xn.append(plan.dma("gpsimd", xnw_bf.ap()[:, bass.ts(c, TSW1)],
                                  xnw_in.rearrange("(c k) t -> c k t", k=128)[c]))
        ev_wo = []
        for c in range(8):
            ev_wo.append(plan.dma("gpsimd", wo_bf.ap()[:, bass.ts(c, 1024)],
                                  wo_in.rearrange("(c k) n -> c k n", k=128)[c]))
        prev = ev_xn + ev_wo + [ev_vones, ev_b20, ev_idb]

        attn_done_all = []
        ngrp = [(0, 512), (512, 512), (1024, 512), (1536, 512), (2048, 512), (2560, 128)]
        for hp in range(2):
            hoff = hp * 512
            ev_w = []
            for c in range(8):
                ev_w.append(plan.dma("gpsimd", wbuf.ap()[:, bass.ts(c, 512)],
                                     wk_in.rearrange("(c k) n -> c k n", k=128)[c][:, hoff:hoff + 512],
                                     after=prev))
            ev_kv = []
            for mc in range(4):
                for (n0, nn) in ngrp:
                    e = None
                    for kc in range(8):
                        lhs = wbuf.ap()[:, kc * 512 + mc * 128: kc * 512 + mc * 128 + 128]
                        e = plan.do("tensor", lambda E, lhs=lhs, kc=kc, n0=n0, nn=nn: E.matmul(
                            ps_qkv.ap()[:, 0:nn], lhs, xnw_bf.ap()[:, kc * TSW1 + n0: kc * TSW1 + n0 + nn],
                            start=(kc == 0), stop=(kc == 7)), after=ev_w + prev)
                    ee = plan.do("scalar", lambda E, mc=mc, n0=n0, nn=nn: E.activation(
                        kfm.ap()[:, mc * TSW1 + n0: mc * TSW1 + n0 + nn], ps_qkv.ap()[:, 0:nn],
                        AF.Identity), after=[e])
                    ev_kv.append(ee)
                    prev = [ee]
            ev_w2 = []
            for c in range(8):
                ev_w2.append(plan.dma("gpsimd", wbuf.ap()[:, bass.ts(c, 512)],
                                      wv_in.rearrange("(c k) n -> c k n", k=128)[c][:, hoff:hoff + 512],
                                      after=prev + ev_kv))
            vb_done = {}
            for t in range(NTW1):
                e = None
                for kc in range(8):
                    lhs = xnw_bf.ap()[:, kc * TSW1 + t * 128: kc * TSW1 + t * 128 + 128]
                    e = plan.do("tensor", lambda E, lhs=lhs, kc=kc: E.matmul(
                        ps_qkv.ap(), lhs, wbuf.ap()[:, bass.ts(kc, 512)],
                        start=(kc == 0), stop=(kc == 7)), after=ev_w2 + prev)
                ee = plan.do("scalar", lambda E, t=t: E.activation(
                    vbuf.ap()[:, t * 520: (t + 1) * 520].rearrange("p (h d) -> p h d", d=65)[:, :, 0:64],
                    ps_qkv.ap().rearrange("p (h d) -> p h d", d=64), AF.Identity),
                    after=[e, ev_vones])
                vb_done[t] = ee
                prev = [ee]
            ev_w3 = []
            for c in range(8):
                ev_w3.append(plan.dma("gpsimd", wbuf.ap()[:, bass.ts(c, 512)],
                                      wq_in.rearrange("(c k) n -> c k n", k=128)[c][:, hoff:hoff + 512],
                                      after=prev))
            for mc in range(4):
                for g in range(4):
                    e = None
                    for kc in range(8):
                        lhs = wbuf.ap()[:, kc * 512 + mc * 128: kc * 512 + mc * 128 + 128]
                        rhs = xnw_bf.ap()[:, kc * TSW1 + 5 * 128 + g * 512: kc * TSW1 + 5 * 128 + g * 512 + 512]
                        e = plan.do("tensor", lambda E, lhs=lhs, rhs=rhs, kc=kc: E.matmul(
                            ps_qkv.ap(), lhs, rhs, start=(kc == 0), stop=(kc == 7)),
                            after=ev_w3 + prev)
                    ee = plan.do("scalar", lambda E, mc=mc, g=g: E.activation(
                        qfm.ap()[:, mc * OWN + g * 512: mc * OWN + (g + 1) * 512], ps_qkv.ap(),
                        AF.Identity), after=[e])
                    prev = [ee]
            attn_done = []
            last_use = {0: {"st": None, "o": None, "tp": None, "pm": None, "sk": None},
                        1: {"st": None, "o": None, "tp": None, "pm": None, "sk": None}}
            p2_prev = list(prev)
            prev_masks = {0: p2_prev, 1: p2_prev}
            lu_o_any = [None]
            hi = 0
            for j in range(NT):
                mslot = j % 2
                mk_ap = mk_s.ap()[:, mslot * 640:(mslot + 1) * 640]
                ev_mk = None
                for k in range(5):
                    ev_mk = plan.dma("sync", mk_ap[:, bass.ts(k, 128)], maskT_in[j][k],
                                     after=prev_masks[mslot] + p2_prev)
                masked_chunks = [0, 1, 2, 3, 4] if j < 5 else [0, 4]
                for hl in range(8):
                    pp = hi % 2
                    ps_st = ps_st2[pp]
                    pm_s, pmm_s, pm2_s = pm_s2[pp], pmm_s2[pp], pm2_s2[pp]
                    r_s, on_s = r_s2[pp], on_s2[pp]
                    sk_ap = ps_sk.ap()[0:16, pp * 128:pp * 128 + 128]
                    tp_ap = ps_tpd.ap()[0:64, pp * 128:pp * 128 + 128]
                    lu = last_use[pp]
                    cc = hl // 2
                    rows0 = (hl % 2) * 64
                    ev_st = None
                    for k in range(5):
                        lhs = kfm.ap()[rows0:rows0 + 64, cc * TSW1 + (j + 1 + k) * 128: cc * TSW1 + (j + 1 + k) * 128 + 128]
                        rhs = qfm.ap()[rows0:rows0 + 64, cc * OWN + j * 128: cc * OWN + j * 128 + 128]
                        ev_st = plan.do("tensor", lambda E, lhs=lhs, rhs=rhs, k=k, ps_st=ps_st: E.matmul(
                            ps_st.ap()[:, bass.ts(k, 128)], lhs, rhs, start=True, stop=True),
                            after=[lu["st"]] + p2_prev)
                    lhs_sink = kfm.ap()[rows0:rows0 + 64, cc * TSW1 + 0: cc * TSW1 + 16]
                    rhs_q = qfm.ap()[rows0:rows0 + 64, cc * OWN + j * 128: cc * OWN + j * 128 + 128]
                    ev_sk = plan.do("tensor", lambda E, lhs_sink=lhs_sink, rhs_q=rhs_q, sk_ap=sk_ap: E.matmul(
                        sk_ap, lhs_sink, rhs_q, start=True, stop=True),
                        after=[lu["sk"], ev_st])
                    ev_exp = plan.do("scalar", lambda E, pm_s=pm_s, ps_st=ps_st: E.activation(
                        pm_s.ap(), ps_st.ap(), AF.Exp, bias=bias20.ap()[:, 0:1], scale=0.125),
                        after=[ev_st, ev_b20, lu["pm"]])
                    ev_exp2 = plan.do("scalar", lambda E, pm2_s=pm2_s, sk_ap=sk_ap: E.activation(
                        pm2_s.ap(), sk_ap, AF.Exp,
                        bias=bias20.ap()[0:16, 0:1], scale=0.125), after=[ev_sk, ev_exp])
                    ev_msk = []
                    for k in masked_chunks:
                        ev_msk.append(plan.do("gpsimd", lambda E, k=k, mk_ap=mk_ap, pm_s=pm_s, pmm_s=pmm_s: E.tensor_mul(
                            pmm_s.ap()[:, bass.ts(k, 128)], pm_s.ap()[:, bass.ts(k, 128)],
                            mk_ap[:, bass.ts(k, 128)]), after=[ev_exp, ev_mk]))
                    ev_pv = lu_o_any[0]
                    for k in range(5):
                        src = pmm_s if k in masked_chunks else pm_s
                        rhs = vbuf.ap()[:, ((j + 1 + k) * 8 + hl) * 65: ((j + 1 + k) * 8 + hl) * 65 + 65]
                        ev_pv = plan.do("tensor", lambda E, rhs=rhs, k=k, src=src: E.matmul(
                            ps_o.ap()[:, 0:65], src.ap()[:, bass.ts(k, 128)], rhs,
                            start=(k == 0), stop=False),
                            after=[ev_exp, ev_pv] + ev_msk + [vb_done[j + 1 + k]])
                    rhs_vs = vbuf.ap()[0:16, (0 * 8 + hl) * 65: (0 * 8 + hl) * 65 + 65]
                    ev_pv = plan.do("tensor", lambda E, rhs_vs=rhs_vs, pm2_s=pm2_s: E.matmul(
                        ps_o.ap()[:, 0:65], pm2_s.ap(), rhs_vs,
                        start=False, stop=True), after=[ev_pv, ev_exp2, vb_done[0]])
                    ev_r = plan.do("vector", lambda E, r_s=r_s: E.reciprocal(r_s.ap(), ps_o.ap()[:, 64:65]),
                                   after=[ev_pv])
                    ev_on = plan.do("vector", lambda E, on_s=on_s, r_s=r_s: E.tensor_scalar_mul(
                        on_s.ap(), ps_o.ap()[:, 0:64], r_s.ap()), after=[ev_r], gate=True)
                    ev_tp = plan.do("tensor", lambda E, on_s=on_s, tp_ap=tp_ap: E.transpose(
                        tp_ap, on_s.ap(), identb.ap()), after=[ev_on, ev_idb, lu["tp"]])
                    h = hp * 8 + hl
                    ocol = (h // 2) * OWN + j * 128
                    orows0 = (h % 2) * 64
                    ev_o = plan.do("scalar", lambda E, ocol=ocol, orows0=orows0, tp_ap=tp_ap: E.activation(
                        ofm.ap()[orows0:orows0 + 64, ocol:ocol + 128], tp_ap,
                        AF.Identity), after=[ev_tp])
                    attn_done.append(ev_o)
                    lu["st"] = ev_exp
                    lu["sk"] = ev_exp2
                    lu["pm"] = ev_pv
                    lu["tp"] = ev_o
                    lu_o_any[0] = ev_on
                    hi += 1
                prev_masks[mslot] = attn_done[-8:]
            attn_done_all += attn_done
            prev = attn_done[-8:]

        st_evs = []
        prev_g = attn_done_all[-16:]
        a1_t = a1_out.rearrange("(c k) t -> c k t", k=128)
        for g in range(8):
            ev_prev_mc = list(prev_g)
            g_done = []
            for mc in range(8):
                e = None
                for kc in range(8):
                    lhs = wo_bf.ap()[:, kc * 1024 + mc * 128: kc * 1024 + mc * 128 + 128]
                    rhs = ofm.ap()[:, kc * OWN + g * 256: kc * OWN + g * 256 + 256]
                    e = plan.do("tensor", lambda E, lhs=lhs, rhs=rhs, kc=kc: E.matmul(
                        ps_qkv.ap()[:, 0:256], lhs, rhs, start=(kc == 0), stop=(kc == 7)),
                        after=attn_done_all + ev_prev_mc)
                e = plan.do("scalar", lambda E, mc=mc: E.activation(
                    ast_s.ap()[:, bass.ts(mc, 256)], ps_qkv.ap()[:, 0:256], AF.Identity), after=[e])
                ev_prev_mc = [e]
                g_done.append(e)
            for c in range(8):
                st_evs.append(plan.dma("sync", a1_t[c][:, g * 256:(g + 1) * 256],
                                       ast_s.ap()[:, bass.ts(c, 256)], after=g_done))
            prev_g = g_done + st_evs[-8:]
        plan.wait("sync", st_evs)
        plan.finalize(ctx)
    return nc


def masks_l1(t0):
    mw = np.zeros((NT, 5, 128, 128), np.float32)
    r = np.arange(128)
    for j in range(NT):
        qt = t0 // 128 + j
        qtok = qt * 128 + r[None, :]
        for k in range(5):
            ktok = (qt - 4 + k) * 128 + r[:, None]
            valid = (ktok <= qtok) & (qtok - ktok <= 512) & (ktok >= 16)
            mw[j, k] = valid
    return mw.astype(ml_dtypes.bfloat16)


def l1attn_inputs(xn1_fm_b, t0, w_qkv, w_out):
    lo = t0 - 4 * TILE
    sink = xn1_fm_b[:, 0:128]
    if lo < 0:
        pad = np.zeros((D, -lo), np.float32)
        win = np.concatenate([pad, xn1_fm_b[:, 0:t0 + OWN]], axis=1)
    else:
        win = xn1_fm_b[:, lo:t0 + OWN]
    xnw = np.concatenate([sink, win], axis=1)
    w_qkv = np.asarray(w_qkv, np.float32)
    return {
        "xnw1": np.ascontiguousarray(xnw),
        "wq1": np.ascontiguousarray(w_qkv[0:1024].T),
        "wk1": np.ascontiguousarray(w_qkv[1024:2048].T),
        "wv1": np.ascontiguousarray(w_qkv[2048:3072].T),
        "wo1": np.ascontiguousarray(np.asarray(w_out, np.float32).T),
        "maskT1": masks_l1(t0),
    }



def build_wquant():
    nc = bass.Bass("TRN2", target_bir_lowering=False)
    wg_in = nc.dram_tensor("wgt", [D, 4096], F32, kind="ExternalInput").ap()
    wu_in = nc.dram_tensor("wut", [D, 4096], F32, kind="ExternalInput").ap()
    wd_in = nc.dram_tensor("wdt", [4096, D], F32, kind="ExternalInput").ap()
    wgq_out = nc.dram_tensor("wgq", [D, 4096], BF16, kind="ExternalOutput").ap()
    wuq_out = nc.dram_tensor("wuq", [D, 4096], BF16, kind="ExternalOutput").ap()
    wdq_out = nc.dram_tensor("wdq", [4096, D], BF16, kind="ExternalOutput").ap()
    minv_out = nc.dram_tensor("minv", [128, 3], F32, kind="ExternalOutput").ap()

    with ExitStack() as ctx:
        plan = Plan(nc)
        sb = nc.alloc_sbuf_tensor
        ps = nc.alloc_psum_tensor
        W = 4096
        wbuf = sb("wq_wbuf", [128, 2 * W], F32)
        tbuf = sb("wq_tbuf", [128, W], F32)
        t2buf = sb("wq_t2buf", [128, W], F32)
        ibuf = sb("wq_ibuf", [128, W], I32)
        qbuf = sb("wq_qbuf", [128, 2 * W], BF16)
        colz = sb("wq_cols", [128, 16], F32)
        hib = sb("wq_hib", [128, 2], BF16)
        ones128 = sb("wq_ones", [128, 128], BF16)
        ps_tot = ps("wq_pstot", [128, 512], F32)
        ev_on = plan.do("vector", lambda E: E.memset(ones128.ap(), 1.0))
        minv_cols = sb("wq_minv", [128, 4], F32)

        specs = [(wg_in, wgq_out, 8, 4096, 1024 * 4096, 0),
                 (wu_in, wuq_out, 8, 4096, 1024 * 4096, 1),
                 (wd_in, wdq_out, 32, 1024, 4096 * 1024, 2)]
        prev = [ev_on]
        st_evs = []
        for (win, wout, nchunks, width, nelem, widx) in specs:
            w_t = win.rearrange("(c k) n -> c k n", k=128)
            wo_t = wout.rearrange("(c k) n -> c k n", k=128)
            acc = [colz.ap()[:, 0:1], colz.ap()[:, 1:2]]
            red = colz.ap()[:, 2:3]
            h32 = colz.ap()[:, 3:4]
            lo32 = colz.ap()[:, 4:5]
            tot = colz.ap()[:, 5:6]
            minv = minv_cols.ap()[:, widx:widx + 1]
            sw = colz.ap()[:, 6:7]
            e_acc = plan.do("vector", lambda E: E.memset(acc[0], 0.0), after=prev)
            prevdma = []
            par = 0
            ev_ld = {}
            for c in range(nchunks):
                sl = wbuf.ap()[:, (c % 2) * W: (c % 2) * W + width]
                ev = plan.dma("sync", sl, w_t[c], after=prevdma + prev)
                ev_ld[c] = ev
                prevdma = [ev]
                e_r = plan.do("vector", lambda E, sl=sl: E.reduce_sum(
                    red, sl, axis=AX, apply_absolute_value=True), after=[ev], gate=True)
                e_acc = plan.do("vector", lambda E, par=par: E.tensor_add(
                    acc[1 - par], acc[par], red), after=[e_r, e_acc], gate=True)
                par = 1 - par
            e = plan.do("vector", lambda E, par=par: E.tensor_copy(hib.ap()[:, 0:1], acc[par]),
                        after=[e_acc], gate=True)
            e = plan.do("vector", lambda E: E.tensor_copy(h32, hib.ap()[:, 0:1]), after=[e], gate=True)
            e = plan.do("vector", lambda E, par=par: E.tensor_sub(lo32, acc[par], h32), after=[e], gate=True)
            e = plan.do("vector", lambda E: E.tensor_copy(hib.ap()[:, 1:2], lo32), after=[e], gate=True)
            em1 = plan.do("tensor", lambda E: E.matmul(
                ps_tot.ap()[:, 0:1], ones128.ap(), hib.ap()[:, 0:1], start=True, stop=False),
                after=[e, ev_on])
            em2 = plan.do("tensor", lambda E: E.matmul(
                ps_tot.ap()[:, 0:1], ones128.ap(), hib.ap()[:, 1:2], start=False, stop=True),
                after=[e, em1])
            e = plan.do("vector", lambda E, nelem=nelem: E.tensor_scalar_mul(
                tot, ps_tot.ap()[:, 0:1], 1.0 / nelem), after=[em2])
            e = plan.do("vector", lambda E, minv=minv: E.tensor_scalar_max(minv, tot, 1e-5),
                        after=[e], gate=True)
            e_sw = plan.do("vector", lambda E, minv=minv: E.reciprocal(sw, minv), after=[e], gate=True)
            prevq = [e_sw]
            prev_st = []
            for c in range(nchunks):
                sl = wbuf.ap()[:, (c % 2) * W: (c % 2) * W + width]
                ev = plan.dma("sync", sl, w_t[c], after=[ev_ld[c]] + prev_st + prevq)
                e1 = plan.do("scalar", lambda E, sl=sl, width=width: E.activation(
                    tbuf.ap()[:, 0:width], sl, AF.Identity, scale=sw), after=[ev, e_sw])
                e2 = plan.do("gpsimd", lambda E, width=width: E.tensor_scalar(
                    t2buf.ap()[:, 0:width],
                    tbuf.ap()[:, 0:width], 1.49, -1.49, ALU.min, ALU.max), after=[e1])
                e3 = plan.do("vector", lambda E, width=width: E.tensor_copy(
                    ibuf.ap()[:, 0:width], t2buf.ap()[:, 0:width]), after=[e2], gate=True)
                qsl = qbuf.ap()[:, (c % 2) * W: (c % 2) * W + width]
                e4 = plan.do("vector", lambda E, qsl=qsl, width=width: E.tensor_copy(
                    qsl, ibuf.ap()[:, 0:width]), after=[e3], gate=True)
                ev_st = plan.dma("scalar", wo_t[c], qsl, after=[e4])
                st_evs.append(ev_st)
                prev_st = [ev_st]
                prevq = [e4]
            prev = prevq + prev_st
        for widx in range(3):
            st_evs.append(plan.dma("sync", minv_out[:, widx:widx + 1],
                                   minv_cols.ap()[:, widx:widx + 1],
                                   allow_slow_non_contiguous=True, after=prev))
        plan.wait("sync", st_evs)
        plan.finalize(ctx)
    return nc



def build_l1f2():
    nc = bass.Bass("TRN2", target_bir_lowering=False)
    xn_in = nc.dram_tensor("xn1t", [OWN, D], F32, kind="ExternalInput").ap()
    wg_in = nc.dram_tensor("wgq2", [D, 4096], BF16, kind="ExternalInput").ap()
    wu_in = nc.dram_tensor("wuq2", [D, 4096], BF16, kind="ExternalInput").ap()
    mg_in = nc.dram_tensor("minvg", [128, 1], F32, kind="ExternalInput").ap()
    mu_in = nc.dram_tensor("minvu", [128, 1], F32, kind="ExternalInput").ap()
    xq2_out = nc.dram_tensor("xq2", [OWN, 4096], BF16, kind="ExternalOutput").ap()
    c2_out = nc.dram_tensor("c2", [128, NT], F32, kind="ExternalOutput").ap()

    with ExitStack() as ctx:
        plan = Plan(nc)
        sb = nc.alloc_sbuf_tensor
        ps = nc.alloc_psum_tensor
        xt_s = sb("f2_xt", [128, 1024], F32)
        ti_s = sb("f2_ti", [128, 1024], F32)
        ii_s = sb("f2_ii", [128, 1024], I32)
        ti2_s = sb("f2_ti2", [128, 1024], F32)
        ii2_s = sb("f2_ii2", [128, 1024], I32)
        xqtm_s = sb("f2_xqtm", [128, 1024], BF16)
        xqfm = sb("f2_xqfm", [128, 8 * OWN], BF16)
        wg_s = sb("f2_wg", [128, 2 * 8 * 512], BF16)
        wu_s = sb("f2_wu", [128, 2 * 8 * 512], BF16)
        prod = sb("f2_prod", [128, 4 * 4096], F32)
        gu_s2 = [sb("f2_gu0", [128, 512], F32), sb("f2_gu1", [128, 512], F32)]
        up_s2 = [sb("f2_up0", [128, 512], F32), sb("f2_up1", [128, 512], F32)]
        cols = sb("f2_cols", [128, 16], F32)
        cg_s = sb("f2_cg", [128, NT], F32)
        cu_s = sb("f2_cu", [128, NT], F32)
        c2_s = sb("f2_c2", [128, NT], F32)
        mg_s = sb("f2_mg", [128, 1], F32)
        mu_s = sb("f2_mu", [128, 1], F32)
        xq2_s = sb("f2_xq2", [128, 4096], BF16)
        identb, ev_id = alloc_identity(nc, plan, BF16, name="f2_identb")
        ps_tp = ps("f2_pstp", [128, 128], BF16)
        ps_g2 = [ps("f2_psg0", [128, 512], F32), ps("f2_psg1", [128, 512], F32)]
        ps_u2 = [ps("f2_psu0", [128, 512], F32), ps("f2_psu1", [128, 512], F32)]

        ev_m = [plan.dma("sync", mg_s.ap(), mg_in), plan.dma("sync", mu_s.ap(), mu_in)]
        xn_t = xn_in.rearrange("(t k) d -> t k d", k=128)
        xq2_t = xq2_out.rearrange("(t k) d -> t k d", k=128)

        qdone = []
        tq = {}

        def emit_quant_tile(t, prev):
            ev = plan.dma("scalar", xt_s.ap(), xn_t[t], after=prev)
            msc = cols.ap()[:, 0:1]
            rsc = cols.ap()[:, 1:2]
            ssc = cols.ap()[:, 2:3]
            e1 = plan.do("vector", lambda E: E.reduce_max(
                msc, xt_s.ap(), axis=AX, apply_absolute_value=True), after=[ev], gate=True)
            e2 = plan.do("vector", lambda E: E.tensor_scalar_max(rsc, msc, 1e-5), after=[e1], gate=True)
            e3 = plan.do("vector", lambda E: E.reciprocal(ssc, rsc), after=[e2], gate=True)
            e4 = plan.do("vector", lambda E, t=t: E.tensor_scalar(
                cg_s.ap()[:, t:t + 1], rsc, mg_s.ap()[:, 0:1], 1.0 / 127.0, ALU.mult, ALU.mult),
                after=[e3] + ev_m, gate=True)
            e5 = plan.do("vector", lambda E, t=t: E.tensor_scalar(
                cu_s.ap()[:, t:t + 1], rsc, mu_s.ap()[:, 0:1], 1.0 / 127.0, ALU.mult, ALU.mult),
                after=[e4] + ev_m, gate=True)
            e3b = plan.do("vector", lambda E: E.tensor_scalar_mul(
                cols.ap()[:, 3:4], ssc, 127.0), after=[e3], gate=True)
            e6 = plan.do("scalar", lambda E: E.activation(
                ti_s.ap(), xt_s.ap(), AF.Identity, scale=cols.ap()[:, 3:4]), after=[e3b])
            e7 = plan.do("vector", lambda E: E.tensor_copy(ii_s.ap(), ti_s.ap()),
                         after=[e6], gate=True)
            e8 = plan.do("vector", lambda E: E.tensor_copy(xqtm_s.ap(), ii_s.ap()),
                         after=[e7], gate=True)
            tdone = []
            for c in range(8):
                et = plan.do("tensor", lambda E, c=c: E.transpose(
                    ps_tp.ap(), xqtm_s.ap()[:, bass.ts(c, 128)], identb.ap()), after=[e8, ev_id])
                ee = plan.do("scalar", lambda E, c=c, t=t: E.activation(
                    xqfm.ap()[:, c * OWN + t * 128: c * OWN + t * 128 + 128], ps_tp.ap(),
                    AF.Identity), after=[et])
                plan.do("tensor", lambda E: E.nop(), after=[ee])
                tdone.append(ee)
            qdone.extend(tdone)
            tq[t] = tdone
            return tdone

        prev = [ev_id] + ev_m
        for t in range(4):
            prev = emit_quant_tile(t, prev)
        st_evs = []
        c2done = []
        mm_last = {0: [], 1: []}
        lu_f = {0: [], 1: []}
        prod_free = {tl: [] for tl in range(4)}
        for tb in range(4):
            for nck in range(8):
                slot = (nck % 2)
                ev_w = []
                for kc in range(8):
                    ev_w.append(plan.dma("gpsimd", wg_s.ap()[:, slot * 8 * 512 + kc * 512: slot * 8 * 512 + (kc + 1) * 512],
                                         wg_in.rearrange("(c k) n -> c k n", k=128)[kc][:, nck * 512:(nck + 1) * 512],
                                         after=mm_last[slot]))
                    ev_w.append(plan.dma("sync", wu_s.ap()[:, slot * 8 * 512 + kc * 512: slot * 8 * 512 + (kc + 1) * 512],
                                         wu_in.rearrange("(c k) n -> c k n", k=128)[kc][:, nck * 512:(nck + 1) * 512],
                                         after=mm_last[slot]))
                for tl in range(4):
                    pp = tl % 2
                    ps_g, ps_u = ps_g2[pp], ps_u2[pp]
                    gu_s, up_s = gu_s2[pp], up_s2[pp]
                    t = tb * 4 + tl
                    eg = eu = None
                    for kc in range(8):
                        lhs = xqfm.ap()[:, kc * OWN + t * 128: kc * OWN + t * 128 + 128]
                        eg = plan.do("tensor", lambda E, lhs=lhs, kc=kc, slot=slot, ps_g=ps_g: E.matmul(
                            ps_g.ap(), lhs, wg_s.ap()[:, slot * 8 * 512 + kc * 512: slot * 8 * 512 + (kc + 1) * 512],
                            start=(kc == 0), stop=(kc == 7)), after=ev_w + tq[t] + lu_f[pp])
                        eu = plan.do("tensor", lambda E, lhs=lhs, kc=kc, slot=slot, ps_u=ps_u: E.matmul(
                            ps_u.ap(), lhs, wu_s.ap()[:, slot * 8 * 512 + kc * 512: slot * 8 * 512 + (kc + 1) * 512],
                            start=(kc == 0), stop=(kc == 7)), after=ev_w + tq[t] + lu_f[pp])
                    es = plan.do("scalar", lambda E, t=t, ps_g=ps_g, gu_s=gu_s: E.activation(
                        gu_s.ap(), ps_g.ap(), AF.Silu, scale=cg_s.ap()[:, t:t + 1]), after=[eg])
                    eup = plan.do("scalar", lambda E, t=t, ps_u=ps_u, up_s=up_s, gu_s=gu_s: E.activation(
                        up_s.ap(), ps_u.ap(), AF.Identity, scale=cu_s.ap()[:, t:t + 1]), after=[eu, es])
                    epr = plan.do("vector", lambda E, tl=tl, nck=nck, gu_s=gu_s, up_s=up_s: E.tensor_mul(
                        prod.ap()[:, tl * 4096 + nck * 512: tl * 4096 + (nck + 1) * 512],
                        gu_s.ap(), up_s.ap()), after=[es, eup] + prod_free[tl])
                    lu_f[pp] = [epr]
                mm_last[slot] = [eu]
            if tb < 3:
                for t in range(4 * (tb + 1), 4 * (tb + 2)):
                    prev = emit_quant_tile(t, prev)
            for tl in range(4):
                t = tb * 4 + tl
                pr = prod.ap()[:, tl * 4096: (tl + 1) * 4096]
                m2 = cols.ap()[:, 4:5]
                r2 = cols.ap()[:, 5:6]
                s2 = cols.ap()[:, 6:7]
                e1 = plan.do("vector", lambda E, pr=pr: E.reduce_max(
                    m2, pr, axis=AX, apply_absolute_value=True),
                    after=lu_f[0] + lu_f[1] + [c2done[-1] if c2done else None], gate=True)
                e2 = plan.do("vector", lambda E: E.tensor_scalar_max(r2, m2, 1e-5), after=[e1], gate=True)
                e3 = plan.do("vector", lambda E: E.reciprocal(s2, r2), after=[e2], gate=True)
                e4 = plan.do("vector", lambda E, t=t: E.tensor_scalar(
                    c2_s.ap()[:, t:t + 1], r2, 1.0 / 127.0, None, ALU.mult), after=[e3], gate=True)
                e4b = plan.do("vector", lambda E: E.tensor_scalar_mul(
                    cols.ap()[:, 7:8], s2, 127.0), after=[e4], gate=True)
                subdone = []
                ea = None
                for sc in range(4):
                    prs = pr[:, sc * 1024:(sc + 1) * 1024]
                    ea = plan.do("scalar", lambda E, prs=prs: E.activation(
                        ti2_s.ap(), prs, AF.Identity, scale=cols.ap()[:, 7:8]), after=[e4b] + subdone)
                    eb = plan.do("vector", lambda E: E.tensor_copy(ii2_s.ap(), ti2_s.ap()),
                                 after=[ea], gate=True)
                    ec = plan.do("vector", lambda E, sc=sc: E.tensor_copy(
                        xq2_s.ap()[:, sc * 1024:(sc + 1) * 1024], ii2_s.ap()), after=[eb], gate=True)
                    subdone.append(ec)
                prod_free[tl] = [ea]
                ev_st = plan.dma("scalar", xq2_t[t], xq2_s.ap(), after=subdone)
                st_evs.append(ev_st)
                c2done.append(ev_st)
        st_evs.append(plan.dma("sync", c2_out, c2_s.ap(), after=c2done + qdone))
        plan.wait("sync", st_evs)
        plan.finalize(ctx)
    return nc



def build_l1f3():
    nc = bass.Bass("TRN2", target_bir_lowering=False)
    xq_in = nc.dram_tensor("xq2f", [4096, OWN], BF16, kind="ExternalInput").ap()
    wd_in = nc.dram_tensor("wdq3", [4096, D], BF16, kind="ExternalInput").ap()
    c2_in = nc.dram_tensor("c2b", [128, NT], F32, kind="ExternalInput").ap()
    md_in = nc.dram_tensor("minvd", [128, 1], F32, kind="ExternalInput").ap()
    x1_in = nc.dram_tensor("x1t", [OWN, D], F32, kind="ExternalInput").ap()
    a1_in = nc.dram_tensor("a1t", [OWN, D], F32, kind="ExternalInput").ap()
    x2_out = nc.dram_tensor("x2", [OWN, D], F32, kind="ExternalOutput").ap()

    with ExitStack() as ctx:
        plan = Plan(nc)
        sb = nc.alloc_sbuf_tensor
        ps = nc.alloc_psum_tensor
        wd_s = sb("f3_wd", [128, 32 * 1024], BF16)
        xq_s = sb("f3_xq", [128, 2 * 32 * 128], BF16)
        c2_s = sb("f3_c2", [128, NT], F32)
        cd_s = sb("f3_cd", [128, NT], F32)
        md_s = sb("f3_md", [128, 1], F32)
        x1_s = sb("f3_x1", [128, 2 * 1024], F32)
        a1_s = sb("f3_a1", [128, 2 * 1024], F32)
        f1_s = sb("f3_f1", [128, 1024], F32)
        t1_s = sb("f3_t1", [128, 1024], F32)
        x2_s = sb("f3_x2", [128, 2 * 1024], F32)
        ps_y = [ps("f3_psy0", [128, 512], F32), ps("f3_psy1", [128, 512], F32)]

        ev_w = []
        for c in range(32):
            ev_w.append(plan.dma("gpsimd", wd_s.ap()[:, bass.ts(c, 1024)],
                                 wd_in.rearrange("(c k) n -> c k n", k=128)[c]))
        ev_m = [plan.dma("sync", c2_s.ap(), c2_in), plan.dma("sync", md_s.ap(), md_in)]
        e_cd = plan.do("vector", lambda E: E.tensor_scalar_mul(
            cd_s.ap(), c2_s.ap(), md_s.ap()[:, 0:1]), after=ev_m)
        xq_t = xq_in.rearrange("(c k) t -> c k t", k=128)
        x1_t = x1_in.rearrange("(t k) d -> t k d", k=128)
        a1_t = a1_in.rearrange("(t k) d -> t k d", k=128)
        x2_t = x2_out.rearrange("(t k) d -> t k d", k=128)
        st_evs = []
        mm_slot = {0: [], 1: []}
        cmb_slot = {0: [], 1: []}
        st_slot = {0: None, 1: None}
        dq_prev = {0: None, 1: None}
        e4_prev = None
        for t in range(NT):
            slot = (t % 2)
            ev_x = []
            for c in range(32):
                q = "scalar" if c % 2 == 0 else "sync"
                ev_x.append(plan.dma(q, xq_s.ap()[:, slot * 4096 + c * 128: slot * 4096 + (c + 1) * 128],
                                     xq_t[c][:, t * 128:(t + 1) * 128], after=mm_slot[slot]))
            ev_l1 = plan.dma("sync", x1_s.ap()[:, slot * 1024: (slot + 1) * 1024], x1_t[t],
                             after=cmb_slot[slot])
            ev_l2 = plan.dma("gpsimd", a1_s.ap()[:, slot * 1024: (slot + 1) * 1024], a1_t[t],
                             after=cmb_slot[slot])
            ev_mm = []
            for nb in range(2):
                e = None
                for kc in range(32):
                    lhs = xq_s.ap()[:, slot * 4096 + kc * 128: slot * 4096 + (kc + 1) * 128]
                    e = plan.do("tensor", lambda E, lhs=lhs, kc=kc, nb=nb: E.matmul(
                        ps_y[nb].ap(), lhs, wd_s.ap()[:, kc * 1024 + nb * 512: kc * 1024 + (nb + 1) * 512],
                        start=(kc == 0), stop=(kc == 31)),
                        after=ev_x + ev_w + [e_cd, dq_prev[nb]])
                ev_mm.append(e)
            e1 = plan.do("vector", lambda E, t=t: E.tensor_scalar_mul(
                f1_s.ap()[:, 0:512], ps_y[0].ap(), cd_s.ap()[:, t:t + 1]), after=[ev_mm[0], e_cd])
            e2 = plan.do("vector", lambda E, t=t: E.tensor_scalar_mul(
                f1_s.ap()[:, 512:1024], ps_y[1].ap(), cd_s.ap()[:, t:t + 1]), after=[ev_mm[1], e1])
            e3 = plan.do("gpsimd", lambda E, slot=slot: E.tensor_add(
                t1_s.ap(), x1_s.ap()[:, slot * 1024:(slot + 1) * 1024],
                a1_s.ap()[:, slot * 1024:(slot + 1) * 1024]), after=[ev_l1, ev_l2, e4_prev])
            e4 = plan.do("vector", lambda E, slot=slot: E.tensor_add(
                x2_s.ap()[:, slot * 1024:(slot + 1) * 1024], t1_s.ap(), f1_s.ap()),
                after=[e2, e3, st_slot[slot]], gate=True)
            ev_st = plan.dma("sync", x2_t[t], x2_s.ap()[:, slot * 1024:(slot + 1) * 1024],
                             after=[e4])
            st_evs.append(ev_st)
            mm_slot[slot] = ev_mm
            cmb_slot[slot] = [e3]
            st_slot[slot] = ev_st
            dq_prev = {0: e1, 1: e2}
            e4_prev = e4
        plan.wait("sync", st_evs)
        plan.finalize(ctx)
    return nc



_nc_cache = {}


def _get_nc(name, builder):
    if name not in _nc_cache:
        _nc_cache[name] = builder()
    return _nc_cache[name]


def _assemble_fm(per_core):
    full = np.zeros((4, D, SEQ), np.float32)
    for c in range(NCORES):
        b, h = c // 2, c % 2
        full[b, :, h * OWN:(h + 1) * OWN] = np.asarray(per_core[c])
    return full


def kernel(L0_attn_X, L0_attn_Wq, L0_attn_Wk, L0_attn_Wv, L0_attn_Wo,
           L0_ffn_wqkv, L0_ffn_q_scale, L0_ffn_k_scale, L0_ffn_wo,
           L0_ffn_ln_g, L0_ffn_ln_b, L0_ffn_w1, L0_ffn_b1, L0_ffn_w2, L0_ffn_b2,
           L1_attn_W_qkv, L1_attn_W_out, L1_ffn_w_gate, L1_ffn_w_up, L1_ffn_w_down):
    cores = list(range(NCORES))
    x = np.ascontiguousarray(np.asarray(L0_attn_X, np.float32))
    xfm = to_fm(x)

    def run(name, builder, in_maps):
        nc = _get_nc(name, builder)
        return run_bass_kernel_spmd(nc, in_maps, core_ids=cores)

    def slices_of(full_fm):
        return [np.ascontiguousarray(full_fm[c // 2][:, (c % 2) * OWN:(c % 2) * OWN + OWN])
                for c in cores]

    res = run("ln0", build_ln0, [{"xin": s} for s in slices_of(xfm)])
    xn_fm = _assemble_fm([r["xn"] for r in res.results])

    tabs = {}
    masksI = {}
    for c in cores:
        t0 = (c % 2) * OWN
        if t0 not in tabs:
            tabs[t0] = rotary_tables_i(t0)
            masksI[t0] = masks_inner(t0)
    cur = xn_fm
    for layer in range(2):
        in_maps = []
        for c in cores:
            b, t0 = c // 2, (c % 2) * OWN
            in_maps.append(inner_attn_inputs(
                cur[b], cur[b], t0,
                np.asarray(L0_ffn_wqkv[layer], np.float32),
                np.asarray(L0_ffn_q_scale[layer], np.float32),
                np.asarray(L0_ffn_k_scale[layer], np.float32),
                np.asarray(L0_ffn_wo[layer], np.float32),
                tabs[t0], masksI[t0]))
        res = run("ia", build_inner_attn, in_maps)
        xmid_fm = _assemble_fm([r["xmid"] for r in res.results])
        in_maps = []
        for c in cores:
            b, t0 = c // 2, (c % 2) * OWN
            in_maps.append(inner_ffn_inputs(
                xmid_fm[b], t0,
                np.asarray(L0_ffn_w1[layer], np.float32),
                np.asarray(L0_ffn_b1[layer], np.float32),
                np.asarray(L0_ffn_w2[layer], np.float32),
                np.asarray(L0_ffn_b2[layer], np.float32),
                np.asarray(L0_ffn_ln_g[layer], np.float32),
                np.asarray(L0_ffn_ln_b[layer], np.float32)))
        res = run("if", build_inner_ffn, in_maps)
        cur = _assemble_fm([r["xi1"] for r in res.results])
    f0_fm = cur

    in_maps = []
    for c in cores:
        b, t0 = c // 2, (c % 2) * OWN
        in_maps.append(attn0_inputs(xn_fm[b], t0,
                                    np.asarray(L0_attn_Wq, np.float32),
                                    np.asarray(L0_attn_Wk, np.float32),
                                    np.asarray(L0_attn_Wv, np.float32),
                                    np.asarray(L0_attn_Wo, np.float32)))
    res = run("a0", build_attn0, in_maps)
    a0_fm = _assemble_fm([r["a0"] for r in res.results])

    in_maps = []
    for c in cores:
        b, t0 = c // 2, (c % 2) * OWN
        sl = slice(t0, t0 + OWN)
        in_maps.append({"cx": np.ascontiguousarray(xfm[b][:, sl]),
                        "ca": np.ascontiguousarray(a0_fm[b][:, sl]),
                        "cf": np.ascontiguousarray(f0_fm[b][:, sl])})
    res = run("cmb", build_cmb, in_maps)
    x1_fm = _assemble_fm([r["x1"] for r in res.results])
    xn1_fm = _assemble_fm([r["xn1"] for r in res.results])

    in_maps = []
    for c in cores:
        b, t0 = c // 2, (c % 2) * OWN
        in_maps.append(l1attn_inputs(xn1_fm[b], t0,
                                     np.asarray(L1_attn_W_qkv, np.float32),
                                     np.asarray(L1_attn_W_out, np.float32)))
    res = run("l1a", build_l1attn, in_maps)
    a1_fm = _assemble_fm([r["a1"] for r in res.results])

    im1 = {"wgt": np.ascontiguousarray(np.asarray(L1_ffn_w_gate, np.float32).T),
           "wut": np.ascontiguousarray(np.asarray(L1_ffn_w_up, np.float32).T),
           "wdt": np.ascontiguousarray(np.asarray(L1_ffn_w_down, np.float32).T)}
    res = run("wq", build_wquant, [im1] * NCORES)
    wgq = np.asarray(res.results[0]["wgq"])
    wuq = np.asarray(res.results[0]["wuq"])
    wdq = np.asarray(res.results[0]["wdq"])
    minv = np.asarray(res.results[0]["minv"])

    in_maps = []
    for c in cores:
        b, t0 = c // 2, (c % 2) * OWN
        xt = np.ascontiguousarray(np.swapaxes(xn1_fm[b][:, t0:t0 + OWN], 0, 1))
        in_maps.append({"xn1t": xt, "wgq2": wgq, "wuq2": wuq,
                        "minvg": np.ascontiguousarray(minv[:, 0:1]),
                        "minvu": np.ascontiguousarray(minv[:, 1:2])})
    res2 = run("l1f2", build_l1f2, in_maps)

    in_maps = []
    for c in cores:
        b, t0 = c // 2, (c % 2) * OWN
        xq2 = np.asarray(res2.results[c]["xq2"])
        c2 = np.asarray(res2.results[c]["c2"])
        in_maps.append({"xq2f": np.ascontiguousarray(xq2.T), "wdq3": wdq,
                        "c2b": c2, "minvd": np.ascontiguousarray(minv[:, 2:3]),
                        "x1t": np.ascontiguousarray(np.swapaxes(x1_fm[b][:, t0:t0 + OWN], 0, 1)),
                        "a1t": np.ascontiguousarray(np.swapaxes(a1_fm[b][:, t0:t0 + OWN], 0, 1))})
    res3 = run("l1f3", build_l1f3, in_maps)
    out = np.zeros((4, SEQ, D), np.float32)
    for c in cores:
        b, t0 = c // 2, (c % 2) * OWN
        out[b, t0:t0 + OWN] = np.asarray(res3.results[c]["x2"])
    return out


# version 06 (final)
# speedup vs baseline: 1.2239x; 1.0094x over previous
import math
import numpy as np
import ml_dtypes
import concourse.bass as bass
import concourse.mybir as mybir
from concourse.bass_utils import run_bass_kernel_spmd
from contextlib import ExitStack

F32 = mybir.dt.float32
BF16 = mybir.dt.bfloat16
I32 = mybir.dt.int32
AX = mybir.AxisListType.X
ALU = mybir.AluOpType
AF = mybir.ActivationFunctionType

D = 1024
NCORES = 8
TILE = 128
OWN = 2048
NT = OWN // TILE
SEQ = 4096


ENGINES = ("sync", "scalar", "vector", "tensor", "gpsimd")


class Plan:
    def __init__(self, nc):
        self.nc = nc
        self.steps = {e: [] for e in ENGINES}
        self.count = {}
        for e in ENGINES:
            self.count["s_" + e] = 0
            self.count["d_" + e] = 0

    def do(self, eng, fn, after=(), gate=True):
        own = "s_" + eng
        waits = []
        for ev in after:
            if ev is None:
                continue
            (k, v) = ev
            if k == own and not gate:
                continue
            waits.append((k, v))
        self.steps[eng].append((waits, fn, own, 1))
        self.count[own] += 1
        return (own, self.count[own])

    def dma(self, eng, out, in_, after=(), **kw):
        own = "d_" + eng
        waits = [(own, self.count[own])] if self.count[own] else []
        for ev in after:
            if ev is None:
                continue
            (k, v) = ev
            if k == "s_" + eng:
                continue
            waits.append((k, v))
        self.steps[eng].append(
            (waits, lambda E: E.dma_start(out=out, in_=in_, **kw), own, 16))
        self.count[own] += 16
        return (own, self.count[own])

    def wait(self, eng, events):
        waits = [ev for ev in events if ev is not None]
        self.steps[eng].append((waits, None, None, 0))

    def finalize(self, ctx):
        nc = self.nc
        sems = {}
        for k in self.count:
            sems[k] = ctx.enter_context(nc.semaphore(k))
        flush = {e: nc.alloc_sbuf_tensor(f"flush_{e}", [1, 8], F32)
                 for e in ("scalar", "vector", "gpsimd")}
        block = ctx.enter_context(nc.Block())

        needed = {k: set() for k in self.count}
        for eng in ENGINES:
            wm = {k: 0 for k in self.count}
            for waits, fn, semk, inc in self.steps[eng]:
                for (k2, v) in waits:
                    if v > wm[k2]:
                        needed[k2].add(v)
                        wm[k2] = v

        def make_section(eng):
            own = "s_" + eng
            pts = sorted(needed[own])

            def body(E):
                wm = {k: 0 for k in self.count}
                cum = 0
                last_pub = 0
                pi = 0
                n = len(self.steps[eng])
                for idx, (waits, fn, semk, inc) in enumerate(self.steps[eng]):
                    for (k2, v) in waits:
                        if v > wm[k2]:
                            E.wait_ge(sems[k2], v)
                            wm[k2] = v
                    if fn is None:
                        continue
                    ins = fn(E)
                    if not inc:
                        continue
                    if semk != own:
                        ins.then_inc(sems[semk], inc)
                        continue
                    cum += inc
                    while pi < len(pts) and pts[pi] <= cum:
                        pi += 1
                    if (pi > 0 and pts[pi - 1] == cum):
                        if eng == "scalar":
                            E.activation(flush[eng].ap()[:, 0:1], flush[eng].ap()[:, 4:5],
                                         AF.Identity).then_inc(sems[semk], cum - last_pub)
                        elif eng in ("vector", "gpsimd"):
                            E.memset(flush[eng].ap()[:, 0:4], 0.0).then_inc(sems[semk], cum - last_pub)
                        else:
                            ins.then_inc(sems[semk], cum - last_pub)
                        last_pub = cum
            return body

        for eng in ENGINES:
            if self.steps[eng]:
                getattr(block, eng)(make_section(eng))



def to_fm(a):
    return np.ascontiguousarray(np.swapaxes(a, -1, -2))


def vec_cols(v):
    return np.ascontiguousarray(np.asarray(v, dtype=np.float32).reshape(-1, 128).T)


def alloc_identity(nc, plan, dt, name="ident"):
    ident = nc.alloc_sbuf_tensor(name, [128, 128], dt)
    ev0 = plan.do("gpsimd", lambda E: E.memset(ident.ap(), 0.0))
    ev = plan.do("gpsimd", lambda E: E.affine_select(
        out=ident.ap(), in_=ident.ap(), compare_op=ALU.not_equal, fill=1.0,
        base=0, pattern=[[-1, 128]], channel_multiplier=1), after=[ev0])
    return ident, ev



def const_col(nc, plan, name, value, dt=F32):
    col = nc.alloc_sbuf_tensor(name, [128, 1], dt)
    ev = plan.do("vector", lambda E: E.memset(col.ap(), value))
    return col, ev


def emit_ln_fm(nc, plan, name, T, xf32, xbf, out, ones_col, ones_row, ev_in,
               G=512, eps=1e-5, out_bf=None):
    sb = nc.alloc_sbuf_tensor
    ps = nc.alloc_psum_tensor
    eps_col, ev_eps = const_col(nc, plan, f"{name}_eps", eps)
    sq = sb(f"{name}_sq", [128, 8 * G], BF16)
    rows = sb(f"{name}_rows", [1, 4 * G], F32)
    rstd_r = sb(f"{name}_rstd", [1, G], F32)
    nmr_r = sb(f"{name}_nmr", [1, G], F32)
    rb_sb = sb(f"{name}_rb", [128, G], F32)
    nb_sb = sb(f"{name}_nb", [128, G], F32)
    t1 = sb(f"{name}_t1", [128, 8 * G], F32)
    ps_s1 = ps(f"{name}_ps1", [1, G], F32)
    ps_s2 = ps(f"{name}_ps2", [1, G], F32)
    ps_b = ps(f"{name}_psb", [128, 2 * G], F32)
    mean = rows.ap()[:, 0:G]
    msq = rows.ap()[:, G:2 * G]
    var = rows.ap()[:, 2 * G:3 * G]
    tmp = rows.ap()[:, 3 * G:4 * G]
    all_done = []
    prev_done = list(ev_in)
    for g in range(T // G):
        c0 = g * G
        ev_s = []
        for c in range(8):
            xs = xbf.ap()[:, c * T + c0: c * T + c0 + G]
            e1 = plan.do("tensor", lambda E, xs=xs, c=c: E.matmul(
                ps_s1.ap(), ones_col.ap(), xs, start=(c == 0), stop=(c == 7)),
                after=prev_done)
            eq = plan.do("scalar", lambda E, xs=xs, c=c: E.activation(
                sq.ap()[:, bass.ts(c, G)], xs, AF.Square), after=prev_done)
            e2 = plan.do("tensor", lambda E, c=c: E.matmul(
                ps_s2.ap()[:, 0:G], ones_col.ap(), sq.ap()[:, bass.ts(c, G)],
                start=(c == 0), stop=(c == 7)), after=[eq])
            if c == 7:
                ev_s = [e1, e2]
        e = plan.do("vector", lambda E: E.tensor_scalar_mul(mean, ps_s1.ap()[:, 0:G], 1.0 / D), after=ev_s)
        e = plan.do("vector", lambda E: E.tensor_scalar_mul(msq, ps_s2.ap()[:, 0:G], 1.0 / D), after=[e])
        e = plan.do("vector", lambda E: E.tensor_mul(tmp, mean, mean), after=[e], gate=True)
        e = plan.do("vector", lambda E: E.tensor_sub(var, msq, tmp), after=[e], gate=True)
        e = plan.do("scalar", lambda E: E.activation(tmp, var, AF.Sqrt, bias=eps_col.ap()[0:1, 0:1]), after=[e, ev_eps])
        e = plan.do("vector", lambda E: E.reciprocal(rstd_r.ap(), tmp), after=[e], gate=True)
        erow = plan.do("vector", lambda E: E.tensor_mul(nmr_r.ap(), mean, rstd_r.ap()), after=[e], gate=True)
        eb1 = plan.do("tensor", lambda E: E.matmul(
            ps_b.ap()[:, 0:G], ones_row.ap(), rstd_r.ap(), start=True, stop=True), after=[erow])
        eb2 = plan.do("tensor", lambda E: E.matmul(
            ps_b.ap()[:, G:2 * G], ones_row.ap(), nmr_r.ap(), start=True, stop=True), after=[erow])
        ev_rb = plan.do("scalar", lambda E: E.activation(rb_sb.ap(), ps_b.ap()[:, 0:G], AF.Identity), after=[eb1])
        ev_nb = plan.do("scalar", lambda E: E.activation(nb_sb.ap(), ps_b.ap()[:, G:2 * G], AF.Identity), after=[eb2])
        group_done = []
        for c in range(8):
            xs = xf32.ap()[:, c * T + c0: c * T + c0 + G]
            os = out.ap()[:, c * T + c0: c * T + c0 + G]
            ts = t1.ap()[:, bass.ts(c, G)]
            e1 = plan.do("gpsimd", lambda E, xs=xs, ts=ts: E.tensor_mul(ts, xs, rb_sb.ap()),
                         after=[ev_rb] + list(ev_in))
            e2 = plan.do("vector", lambda E, os=os, ts=ts: E.tensor_sub(os, ts, nb_sb.ap()),
                         after=[e1, ev_nb])
            if out_bf is not None:
                obs = out_bf.ap()[:, c * T + c0: c * T + c0 + G]
                e2 = plan.do("scalar", lambda E, os=os, obs=obs: E.activation(obs, os, AF.Identity),
                             after=[e2])
            group_done.append(e2)
        prev_done = group_done
        all_done.extend(group_done)
    return all_done


def emit_ln_tm(nc, plan, name, ntiles, T, xf32, out_f32, identf, eps_col,
               ev_eps, ev_id, ev_in, out_bf=None, scr=None):
    sb = nc.alloc_sbuf_tensor
    ps = nc.alloc_psum_tensor
    if scr is not None and "xtm" in scr:
        xtm, sqtm, xh, colz, ps_t = scr["xtm"], scr["sqtm"], scr["xh"], scr["colz"], scr["ps_t"]
    else:
        xtm = sb(f"{name}_xtm", [128, 1024], F32)
        sqtm = sb(f"{name}_sqtm", [128, 1024], F32)
        xh = sb(f"{name}_xh", [128, 1024], F32)
        colz = sb(f"{name}_cols", [128, 8], F32)
        ps_t = ps(f"{name}_pst", [128, 128], F32)
        if scr is not None:
            scr.update(xtm=xtm, sqtm=sqtm, xh=xh, colz=colz, ps_t=ps_t)
    s1c = colz.ap()[:, 0:1]
    s2c = colz.ap()[:, 1:2]
    mc_ = colz.ap()[:, 2:3]
    vc_ = colz.ap()[:, 3:4]
    t0c = colz.ap()[:, 4:5]
    rsc = colz.ap()[:, 5:6]
    done = []
    prev = list(ev_in)
    for t in range(ntiles):
        ev_tp = []
        for c in range(8):
            et = plan.do("tensor", lambda E, c=c, t=t: E.transpose(
                ps_t.ap(), xf32.ap()[:, c * T + t * 128: c * T + t * 128 + 128],
                identf.ap()), after=[ev_id] + prev)
            ee = plan.do("scalar", lambda E, c=c: E.activation(
                xtm.ap()[:, bass.ts(c, 128)], ps_t.ap(), AF.Identity), after=[et])
            plan.do("tensor", lambda E: E.nop(), after=[ee])
            ev_tp.append(ee)
        e1 = plan.do("vector", lambda E: E.reduce_sum(s1c, xtm.ap(), axis=AX), after=ev_tp)
        eq = plan.do("scalar", lambda E: E.activation(sqtm.ap(), xtm.ap(), AF.Square), after=ev_tp)
        e2 = plan.do("vector", lambda E: E.reduce_sum(s2c, sqtm.ap(), axis=AX), after=[eq, e1])
        e3 = plan.do("vector", lambda E: E.tensor_scalar_mul(mc_, s1c, 1.0 / D), after=[e2])
        e4 = plan.do("vector", lambda E: E.tensor_scalar(vc_, mc_, mc_, None, ALU.mult), after=[e3])
        e5 = plan.do("vector", lambda E: E.scalar_tensor_tensor(
            t0c, s2c, 1.0 / D, vc_, ALU.mult, ALU.subtract), after=[e4])
        e6 = plan.do("scalar", lambda E: E.activation(t0c, t0c, AF.Ln, bias=eps_col.ap()[:, 0:1]), after=[e5, ev_eps])
        e7 = plan.do("scalar", lambda E: E.activation(rsc, t0c, AF.Exp, scale=-0.5), after=[e6])
        e8 = plan.do("vector", lambda E: E.tensor_scalar(
            xh.ap(), xtm.ap(), mc_, rsc, ALU.subtract, ALU.mult), after=[e7])
        tile_done = []
        for c in range(8):
            et = plan.do("tensor", lambda E, c=c: E.transpose(
                ps_t.ap(), xh.ap()[:, bass.ts(c, 128)], identf.ap()), after=[e8, ev_id])
            ee = plan.do("scalar", lambda E, c=c, t=t: E.activation(
                out_f32.ap()[:, c * T + t * 128: c * T + t * 128 + 128],
                ps_t.ap(), AF.Identity), after=[et])
            if out_bf is not None:
                ee = plan.do("scalar", lambda E, c=c, t=t, ee=ee: E.activation(
                    out_bf.ap()[:, c * T + t * 128: c * T + t * 128 + 128],
                    out_f32.ap()[:, c * T + t * 128: c * T + t * 128 + 128],
                    AF.Identity), after=[ee])
            plan.do("tensor", lambda E: E.nop(), after=[ee])
            tile_done.append(ee)
        done.extend(tile_done)
        prev = tile_done
    return done



def build_ln0():
    nc = bass.Bass("TRN2", target_bir_lowering=False)
    xin = nc.dram_tensor("xin", [D, OWN], F32, kind="ExternalInput").ap()
    xn_out = nc.dram_tensor("xn", [D, OWN], F32, kind="ExternalOutput").ap()
    with ExitStack() as ctx:
        plan = Plan(nc)
        sb = nc.alloc_sbuf_tensor
        xf32 = sb("xf32", [128, 8 * OWN], F32)
        xo = sb("xo", [128, 8 * OWN], F32)
        identf, ev_id = alloc_identity(nc, plan, F32, name="identf")
        eps_col, ev_eps = const_col(nc, plan, "ln0_eps", 1e-5)
        x_t = xin.rearrange("(c k) t -> c k t", k=128)
        ev_ld = []
        for c in range(8):
            ev_ld.append(plan.dma("sync", xf32.ap()[:, bass.ts(c, OWN)], x_t[c]))
        evs = emit_ln_tm(nc, plan, "ln0", NT, OWN, xf32, xo, identf, eps_col,
                         ev_eps, ev_id, ev_ld)
        st_evs = []
        xo_t = xn_out.rearrange("(c k) t -> c k t", k=128)
        for c in range(8):
            st_evs.append(plan.dma("sync", xo_t[c], xo.ap()[:, bass.ts(c, OWN)],
                                   after=evs))
        plan.wait("sync", st_evs)
        plan.finalize(ctx)
    return nc



HALO_I = 5
TS_I = OWN + HALO_I * TILE
NTS_I = TS_I // TILE


def build_inner_attn():
    nc = bass.Bass("TRN2", target_bir_lowering=False)
    xnh = nc.dram_tensor("xnh", [D, TS_I], F32, kind="ExternalInput").ap()
    xres = nc.dram_tensor("xres", [D, OWN], F32, kind="ExternalInput").ap()
    wqkv = nc.dram_tensor("wqkv", [D, 1536], F32, kind="ExternalInput").ap()
    wo = nc.dram_tensor("wo", [512, D], F32, kind="ExternalInput").ap()
    tabs_in = {}
    for t in ("tqc", "tqs", "tkc", "tks"):
        tabs_in[t] = nc.dram_tensor(t, [TS_I, 64], F32, kind="ExternalInput").ap()
    qsb_in = nc.dram_tensor("qsb", [128, 512], F32, kind="ExternalInput").ap()
    ksb_in = nc.dram_tensor("ksb", [128, 512], F32, kind="ExternalInput").ap()
    maskT_in = nc.dram_tensor("maskT", [NT, 5, 128, 128], BF16, kind="ExternalInput").ap()
    xmid_out = nc.dram_tensor("xmid", [D, OWN], F32, kind="ExternalOutput").ap()

    with ExitStack() as ctx:
        plan = Plan(nc)
        sb = nc.alloc_sbuf_tensor
        ps = nc.alloc_psum_tensor

        xnbf = sb("xnbf", [128, 8 * TS_I], BF16)
        wqkv_bf = sb("wqkv_bf", [128, 8 * 1536], BF16)
        wo_bf = sb("wo_bf", [128, 4 * 1024], BF16)
        tb_s = sb("tb_s", [128, 2 * 256], F32)
        qsb = sb("sb_qsb", [128, 512], F32)
        ksb = sb("sb_ksb", [128, 512], F32)
        kfm = sb("kfm", [128, 4 * TS_I], BF16)
        qfm = sb("qfm", [128, 4 * OWN], BF16)
        vbuf = sb("vbuf", [128, NTS_I * 8 * 65], BF16)
        ofm = sb("ofm", [128, 4 * OWN], BF16)
        sq_s = sb("sq_s", [128, 512], F32)
        ss_s = sb("ss_s", [128, 32], F32)
        k1_s = sb("k1_s", [128, 512], F32)
        k2_s = sb("k2_s", [128, 512], F32)
        ra_s = sb("ra_s", [128, 512], F32)
        rb_s = sb("rb_s", [128, 512], F32)
        ktm_s = sb("ktm_s", [128, 512], BF16)
        pm_s2 = [sb("pm_s0", [128, 640], BF16), sb("pm_s1", [128, 640], BF16)]
        pmm_s2 = [sb("pmm_s0", [128, 640], BF16), sb("pmm_s1", [128, 640], BF16)]
        r_s2 = [sb("r_s0", [128, 1], F32), sb("r_s1", [128, 1], F32)]
        on_s2 = [sb("on_s0", [128, 64], BF16), sb("on_s1", [128, 64], BF16)]
        mk_s = sb("mk_s", [128, 2 * 5 * 128], BF16)
        xr_s = sb("xr_s", [128, 8 * 512], F32)
        xm_s = sb("xm_s", [128, 8 * 512], F32)
        bias_col = sb("bias_col", [128, 1], F32)
        ones_row = sb("ones_row", [1, 128], F32)

        ps_q = ps("ps_q", [128, 512], F32)
        ps_k = ps("ps_k", [128, 512], F32)
        ps_v = ps("ps_v", [128, 512], F32)
        ps_st2 = [ps("ps_st0", [128, 640], F32), ps("ps_st1", [128, 640], F32)]
        ps_tpd = ps("ps_tpd", [128, 256], BF16)

        identb, ev_id = alloc_identity(nc, plan, BF16)
        ev_vones = plan.do("gpsimd", lambda E: E.memset(vbuf.ap(), 1.0))
        ev_or = plan.do("vector", lambda E: E.memset(ones_row.ap(), 1.0))

        ev_w = []
        for c in range(8):
            ev_w.append(plan.dma("gpsimd", wqkv_bf.ap()[:, bass.ts(c, 1536)],
                                 wqkv.rearrange("(c k) n -> c k n", k=128)[c]))
        for c in range(4):
            ev_w.append(plan.dma("gpsimd", wo_bf.ap()[:, bass.ts(c, 1024)],
                                 wo.rearrange("(c k) n -> c k n", k=128)[c]))
        ev_xn = []
        for c in range(8):
            ev_xn.append(plan.dma("gpsimd", xnbf.ap()[:, bass.ts(c, TS_I)],
                                  xnh.rearrange("(c k) t -> c k t", k=128)[c]))
        ev_sc = [plan.dma("sync", qsb.ap(), qsb_in),
                 plan.dma("sync", ksb.ap(), ksb_in)]

        e = plan.do("vector", lambda E: E.tensor_mul(sq_s.ap(), qsb.ap(), ksb.ap()), after=ev_sc)
        e = plan.do("vector", lambda E: E.reduce_max(ss_s.ap()[:, 30:31], sq_s.ap(), axis=AX,
                                                     apply_absolute_value=True), after=[e], gate=True)
        e = plan.do("vector", lambda E: E.tensor_scalar(ss_s.ap()[:, 31:32], ss_s.ap()[:, 30:31],
                                                        -16.0, 60.0, ALU.mult, ALU.add), after=[e], gate=True)
        ev_bias = plan.do("vector", lambda E: E.tensor_scalar_min(bias_col.ap(), ss_s.ap()[:, 31:32], 0.0),
                          after=[e], gate=True)

        NH = 8
        kfm_done = {}
        vb_done = {}
        qfm_done = {}
        prev = ev_w + ev_xn + ev_sc + [ev_id, ev_vones, ev_bias]

        def qk_path(s, which, ps_in, ev_mm, prev, ev_tb):
            slot = (s % 2) * 256
            tco = slot + (0 if which == "q" else 128)
            tso = slot + (64 if which == "q" else 192)
            scb = qsb if which == "q" else ksb
            tgt = qfm if which == "q" else kfm
            tcol = (s - HALO_I) * TILE if which == "q" else s * TILE
            ssv = ss_s.ap()[:, 0:8]
            srv = ss_s.ap()[:, 8:16]
            riv = ss_s.ap()[:, 16:24]
            p3 = lambda a: a.rearrange("p (h d) -> p h d", h=NH)
            b3 = lambda a: a.rearrange("p (u d) -> p u d", u=1).broadcast_to([128, NH, 64])
            e1 = plan.do("scalar", lambda E: E.activation(sq_s.ap(), ps_in.ap(), AF.Square),
                         after=[ev_mm] + prev)
            e2 = plan.do("vector", lambda E: E.reduce_sum(
                ss_s.ap()[:, 0:8].rearrange("p (h u) -> p h u", u=1), p3(sq_s.ap()), axis=AX), after=[e1])
            e3 = plan.do("scalar", lambda E: E.activation(srv, ssv, AF.Sqrt), after=[e2], gate=True)
            e4 = plan.do("vector", lambda E: E.tensor_scalar_max(ss_s.ap()[:, 24:32], srv, 1e-12), after=[e3])
            e5 = plan.do("vector", lambda E: E.reciprocal(riv, ss_s.ap()[:, 24:32]), after=[e4], gate=True)
            e6 = plan.do("vector", lambda E: E.tensor_mul(
                p3(k1_s.ap()), p3(ps_in.ap()),
                riv.rearrange("p (h u) -> p h u", u=1).broadcast_to([128, NH, 64])), after=[e5], gate=True)
            e7 = plan.do("gpsimd", lambda E: E.tensor_mul(k2_s.ap(), k1_s.ap(), scb.ap()), after=[e6])
            tc_b = tb_s.ap()[:, tco:tco + 64].rearrange("p (u d) -> p u d", u=1).broadcast_to([128, NH, 64])
            ts_lo = tb_s.ap()[:, tso:tso + 32].rearrange("p (u d) -> p u d", u=1).broadcast_to([128, NH, 32])
            ts_hi = tb_s.ap()[:, tso + 32:tso + 64].rearrange("p (u d) -> p u d", u=1).broadcast_to([128, NH, 32])
            e8 = plan.do("vector", lambda E: E.tensor_mul(p3(ra_s.ap()), p3(k2_s.ap()), tc_b), after=[e7] + ev_tb)
            k2lo = p3(k2_s.ap())[:, :, 0:32]
            k2hi = p3(k2_s.ap())[:, :, 32:64]
            e9 = plan.do("gpsimd", lambda E: E.tensor_mul(p3(rb_s.ap())[:, :, 0:32], k2hi, ts_lo), after=[e7])
            e10 = plan.do("gpsimd", lambda E: E.tensor_mul(p3(rb_s.ap())[:, :, 32:64], k2lo, ts_hi), after=[e9])
            e11 = plan.do("vector", lambda E: E.tensor_add(ktm_s.ap(), ra_s.ap(), rb_s.ap()), after=[e8, e10])
            evs = []
            for cc in range(4):
                et = plan.do("tensor", lambda E, cc=cc: E.transpose(
                    ps_tpd.ap()[:, 0:128], ktm_s.ap()[:, bass.ts(cc, 128)], identb.ap()), after=[e11, ev_id])
                ee = plan.do("scalar", lambda E, cc=cc: E.activation(
                    tgt.ap()[:, cc * (TS_I if which == "k" else OWN) + tcol: cc * (TS_I if which == "k" else OWN) + tcol + 128],
                    ps_tpd.ap()[:, 0:128], AF.Identity), after=[et])
                evs.append(ee)
                plan.do("tensor", lambda E: E.nop(), after=[ee])
            return evs, e11

        tab_src = {t: tabs_in[t].rearrange("(s k) d -> s k d", k=128)
                   for t in ("tqc", "tqs", "tkc", "tks")}
        prev_tb = {}
        for s in range(1, NTS_I):
            slot = (s % 2) * 256
            ev_tb = []
            tdep = prev_tb.get(s - 2, [])
            for ti, t in enumerate(("tqc", "tqs", "tkc", "tks")):
                ev_tb.append(plan.dma("scalar", tb_s.ap()[:, slot + ti * 64: slot + (ti + 1) * 64],
                                      tab_src[t][s], after=tdep))
            ev_mm_q = ev_mm_k = ev_mm_v = None
            for kc in range(8):
                off = kc * TS_I + s * TILE - (1 if kc >= 4 else 0)
                lhs = xnbf.ap()[:, off: off + 128]
                ev_mm_q = plan.do("tensor", lambda E, lhs=lhs, kc=kc: E.matmul(
                    ps_q.ap(), lhs, wqkv_bf.ap()[:, kc * 1536 + 0: kc * 1536 + 512],
                    start=(kc == 0), stop=(kc == 7)), after=prev)
                ev_mm_k = plan.do("tensor", lambda E, lhs=lhs, kc=kc: E.matmul(
                    ps_k.ap(), lhs, wqkv_bf.ap()[:, kc * 1536 + 512: kc * 1536 + 1024],
                    start=(kc == 0), stop=(kc == 7)), after=prev)
                ev_mm_v = plan.do("tensor", lambda E, lhs=lhs, kc=kc: E.matmul(
                    ps_v.ap(), lhs, wqkv_bf.ap()[:, kc * 1536 + 1024: kc * 1536 + 1536],
                    start=(kc == 0), stop=(kc == 7)), after=prev)
            ev_v = plan.do("scalar", lambda E, s=s: E.activation(
                vbuf.ap()[:, s * 8 * 65: (s + 1) * 8 * 65].rearrange("p (h d) -> p h d", d=65)[:, :, 0:64],
                ps_v.ap().rearrange("p (h d) -> p h d", d=64), AF.Identity),
                after=[ev_mm_v, ev_vones])
            vb_done[s] = ev_v
            evk, lastk = qk_path(s, "k", ps_k, ev_mm_k, prev, ev_tb)
            kfm_done[s] = evk
            nxt = list(evk) + [ev_v]
            if s >= HALO_I:
                evq, lastq = qk_path(s, "q", ps_q, ev_mm_q, [lastk] + evk, ev_tb)
                qfm_done[s - HALO_I] = evq
                nxt += evq
            prev_tb[s] = [nxt[-1]]
            prev = nxt

        p1_done = prev
        attn_done = []
        last_use = {0: {"st": None, "o": None, "tp": None, "pm": None},
                    1: {"st": None, "o": None, "tp": None, "pm": None}}
        prev_masks = {0: p1_done, 1: p1_done}
        mask_ld = {}
        hi = 0
        for j in range(NT):
            mslot = j % 2
            mk_ap = mk_s.ap()[:, mslot * 640:(mslot + 1) * 640]
            ev_mk = None
            for k in range(5):
                ev_mk = plan.dma("sync", mk_ap[:, bass.ts(k, 128)], maskT_in[j][k],
                                 after=prev_masks[mslot] + p1_done)
            masked_chunks = list(range(5)) if j < 4 else [0, 4]
            for h in range(NH):
                pp = hi % 2
                ps_st = ps_st2[pp]
                pm_s, pmm_s, r_s, on_s = pm_s2[pp], pmm_s2[pp], r_s2[pp], on_s2[pp]
                ps_o_ap = (ps_v if pp == 0 else ps_q).ap()[:, 0:65]
                tp_ap = ps_tpd.ap()[0:64, pp * 128:pp * 128 + 128]
                lu = last_use[pp]
                cc = h // 2
                rows0 = (h % 2) * 64
                ev_st = None
                for k in range(5):
                    lhs = kfm.ap()[rows0:rows0 + 64, cc * TS_I + (j + 1 + k) * 128: cc * TS_I + (j + 1 + k) * 128 + 128]
                    rhs = qfm.ap()[rows0:rows0 + 64, cc * OWN + j * 128: cc * OWN + j * 128 + 128]
                    ev_st = plan.do("tensor", lambda E, lhs=lhs, rhs=rhs, k=k, ps_st=ps_st: E.matmul(
                        ps_st.ap()[:, bass.ts(k, 128)], lhs, rhs, start=True, stop=True),
                        after=[lu["st"]] + p1_done)
                ev_exp = plan.do("scalar", lambda E, pm_s=pm_s, ps_st=ps_st: E.activation(
                    pm_s.ap(), ps_st.ap(), AF.Exp, bias=bias_col.ap()[:, 0:1], scale=8.0),
                    after=[ev_st, ev_bias, lu["pm"]])
                ev_msk = []
                for k in masked_chunks:
                    ev_msk.append(plan.do("gpsimd", lambda E, k=k, mk_ap=mk_ap, pm_s=pm_s, pmm_s=pmm_s: E.tensor_mul(
                        pmm_s.ap()[:, bass.ts(k, 128)], pm_s.ap()[:, bass.ts(k, 128)],
                        mk_ap[:, bass.ts(k, 128)]), after=[ev_exp, ev_mk]))
                ev_pv = lu["o"]
                for k in range(5):
                    src = pmm_s if k in masked_chunks else pm_s
                    rhs = vbuf.ap()[:, ((j + 1 + k) * 8 + h) * 65: ((j + 1 + k) * 8 + h) * 65 + 65]
                    ev_pv = plan.do("tensor", lambda E, rhs=rhs, k=k, src=src, ps_o_ap=ps_o_ap: E.matmul(
                        ps_o_ap, src.ap()[:, bass.ts(k, 128)], rhs,
                        start=(k == 0), stop=(k == 4)),
                        after=[ev_exp, ev_pv] + ev_msk + [vb_done[j + 1 + k]])
                ev_r = plan.do("vector", lambda E, r_s=r_s, ps_o_ap=ps_o_ap: E.reciprocal(
                    r_s.ap(), ps_o_ap[:, 64:65]), after=[ev_pv])
                ev_on = plan.do("vector", lambda E, on_s=on_s, r_s=r_s, ps_o_ap=ps_o_ap: E.tensor_scalar_mul(
                    on_s.ap(), ps_o_ap[:, 0:64], r_s.ap()), after=[ev_r], gate=True)
                ev_tp = plan.do("tensor", lambda E, on_s=on_s, tp_ap=tp_ap: E.transpose(
                    tp_ap, on_s.ap(), identb.ap()), after=[ev_on, ev_id, lu["tp"]])
                ocol = cc * OWN + j * 128
                ev_o = plan.do("scalar", lambda E, ocol=ocol, rows0=rows0, tp_ap=tp_ap: E.activation(
                    ofm.ap()[rows0:rows0 + 64, ocol:ocol + 128], tp_ap,
                    AF.Identity), after=[ev_tp])
                attn_done.append(ev_o)
                lu["st"] = ev_exp
                lu["pm"] = ev_pv
                lu["o"] = ev_on
                lu["tp"] = ev_o
                hi += 1
            prev_masks[mslot] = attn_done[-8:]
        st_evs = []
        xr_done = {}
        prev_mc = attn_done[-8:]
        for g in range(4):
            xr_ap = xr_s.ap()[:, 0:8 * 512]
            dep = [] if g < 1 else [xr_done[g - 1]]
            ev_xr = []
            for c in range(8):
                ev_xr.append(plan.dma("sync", xr_ap[:, bass.ts(c, 512)],
                                      xres.rearrange("(c k) t -> c k t", k=128)[c][:, g * 512:(g + 1) * 512],
                                      after=dep))
            xm_done = []
            for mc in range(8):
                ev_mm = None
                for kc in range(4):
                    lhs = wo_bf.ap()[:, kc * 1024 + mc * 128: kc * 1024 + mc * 128 + 128]
                    rhs = ofm.ap()[:, kc * OWN + g * 512: kc * OWN + (g) * 512 + 512]
                    ev_mm = plan.do("tensor", lambda E, lhs=lhs, rhs=rhs, kc=kc: E.matmul(
                        ps_q.ap(), lhs, rhs, start=(kc == 0), stop=(kc == 3)),
                        after=attn_done + prev_mc)
                e = plan.do("vector", lambda E, mc=mc, xr_ap=xr_ap: E.scalar_tensor_tensor(
                    xm_s.ap()[:, bass.ts(mc, 512)], ps_q.ap(), 0.0,
                    xr_ap[:, bass.ts(mc, 512)], ALU.add, ALU.add),
                    after=[ev_mm] + ev_xr)
                xm_done.append(e)
                prev_mc = [e]
            ev_st_g = []
            for c in range(8):
                ev_st_g.append(plan.dma("sync", xmid_out.rearrange("(c k) t -> c k t", k=128)[c][:, g * 512:(g + 1) * 512],
                                        xm_s.ap()[:, bass.ts(c, 512)], after=xm_done))
            xr_done[g] = ev_st_g[-1]
            prev_mc = prev_mc + ev_st_g
            st_evs += ev_st_g
        plan.wait("sync", st_evs)
        plan.finalize(ctx)
    return nc



def rotary_tables_i(t0):
    pos = np.arange(t0 - HALO_I * TILE, t0 + OWN, dtype=np.float64)
    pos = np.clip(pos, 0, None)
    half = np.arange(0, 64, 2, dtype=np.float64)
    inv_freq = 1.0 / (10000.0 ** (half / 64.0))
    fr32 = pos[:, None] * inv_freq[None, :]
    base = (half + 0.4 * 64.0) / (1.4 * 64.0)
    power = (pos - SEQ // 2) / float(512 // 2)
    sc32 = base[None, :] ** power[:, None]
    cos = np.cos(fr32)
    sin = np.sin(fr32)
    cq32 = cos * sc32
    ck32 = cos / sc32
    sq32 = sin * sc32
    sk32 = sin / sc32
    def cat(a):
        return np.concatenate([a, a], axis=1).astype(np.float32)
    tqc = cat(cq32)
    tkc = cat(ck32)
    tqs = np.concatenate([-sq32, sq32], axis=1).astype(np.float32)
    tks = np.concatenate([-sk32, sk32], axis=1).astype(np.float32)
    return {"tqc": tqc, "tqs": tqs, "tkc": tkc, "tks": tks}


def masks_inner(t0):
    out = np.zeros((NT, 5, 128, 128), np.float32)
    r = np.arange(128)
    for j in range(NT):
        qt = t0 // 128 + j
        qtok = qt * 128 + r[None, :]
        for k in range(5):
            ktok = (qt - 4 + k) * 128 + r[:, None]
            valid = (ktok <= qtok) & (qtok - ktok <= 512) & (ktok >= 0)
            out[j, k] = valid
    return out.astype(ml_dtypes.bfloat16)


def inner_attn_inputs(xn_fm_b, x_res_fm_b, t0, wqkv_l, qs_l, ks_l, wo_l, tabs, maskT):
    lo = t0 - HALO_I * TILE
    if lo < 0:
        pad = np.zeros((D, -lo), np.float32)
        xnh = np.concatenate([pad, xn_fm_b[:, 0:t0 + OWN]], axis=1)
    else:
        xnh = xn_fm_b[:, lo:t0 + OWN]
    return {
        "xnh": np.ascontiguousarray(xnh),
        "xres": np.ascontiguousarray(x_res_fm_b[:, t0:t0 + OWN]),
        "wqkv": np.ascontiguousarray(wqkv_l),
        "wo": np.ascontiguousarray(wo_l),
        "qsb": np.ascontiguousarray(np.broadcast_to(np.tile(qs_l, 8)[None, :], (128, 512)).astype(np.float32)),
        "ksb": np.ascontiguousarray(np.broadcast_to(np.tile(ks_l, 8)[None, :], (128, 512)).astype(np.float32)),
        "maskT": maskT,
        **{k: np.ascontiguousarray(v) for k, v in tabs.items()},
    }



HID = 2730
HIDP = 2752
NBLK = 22


def _blk(j):
    off = 256 * j if j < 21 else 5376
    wa = 128 if j < 21 else 64
    return off, wa


def build_inner_ffn(ng_run=8, nblk_run=NBLK, do_w2=True, do_ln=7):
    nc = bass.Bass("TRN2", target_bir_lowering=False)
    xg_in = nc.dram_tensor("xg", [D, OWN + 1], F32, kind="ExternalInput").ap()
    w1_in = nc.dram_tensor("w1re", [D, 2 * HIDP], F32, kind="ExternalInput").ap()
    b1_in = nc.dram_tensor("b1re", [128, 2 * NBLK], F32, kind="ExternalInput").ap()
    w2_in = nc.dram_tensor("w2", [HIDP, D], F32, kind="ExternalInput").ap()
    b2_in = nc.dram_tensor("b2c", [128, 8], F32, kind="ExternalInput").ap()
    lng_in = nc.dram_tensor("lng", [128, 1024], F32, kind="ExternalInput").ap()
    lnb_in = nc.dram_tensor("lnb", [128, 1024], F32, kind="ExternalInput").ap()
    xo_out = nc.dram_tensor("xi1", [D, OWN], F32, kind="ExternalOutput").ap()

    G = 256
    NG = OWN // G
    TS = OWN + 1
    with ExitStack() as ctx:
        plan = Plan(nc)
        sb = nc.alloc_sbuf_tensor
        ps = nc.alloc_psum_tensor
        w1_bf = sb("w1_bf", [128, 8 * 2 * HIDP], BF16)
        w2_bf = sb("w2_bf", [128, NBLK * 1024], BF16)
        b1c = sb("b1c", [128, 2 * NBLK], F32)
        b2c = sb("b2c_s", [128, 8], F32)
        xg32 = sb("xg32", [128, 8 * (G + 1)], F32)
        xgbf = sb("xgbf", [128, 8 * G], BF16)
        h2n = sb("h2n", [128, 8 * G], BF16)
        h2tm = sb("h2tm", [128, 2 * 1024], BF16)
        sqtm = sb("sqtm", [128, 1024], F32)
        xhtm = sb("xhtm", [128, 2 * 1024], F32)
        hntm = sb("hntm", [128, 2 * 1024], BF16)
        cols = sb("cols_f", [128, 16], F32)
        lngb = sb("lngb", [128, 1024], F32)
        lnbb = sb("lnbb", [128, 1024], F32)
        gg = sb("gg_f", [128, G], F32)
        hg = sb("hg_f", [128, NBLK * G], BF16)
        xo_sb = sb("xo_sb", [128, 8 * G], F32)
        eps_col, ev_eps = const_col(nc, plan, "ffn_eps", 1e-5)
        identb, ev_id = alloc_identity(nc, plan, BF16, name="identb_f")

        ps_tp = ps("f_pstp", [128, 128], BF16)
        ps_a2 = [ps("f_psa0", [128, 512], F32), ps("f_psa1", [128, 512], F32)]
        ps_g2 = [ps("f_psg0", [128, 512], F32), ps("f_psg1", [128, 512], F32)]
        ps_acc2 = [ps("f_acc0", [128, 512], F32), ps("f_acc1", [128, 512], F32)]
        gg2 = [sb("gg_f0", [128, 256], F32), sb("gg_f1", [128, 256], F32)]

        ev_w = []
        for c in range(8):
            ev_w.append(plan.dma("gpsimd", w1_bf.ap()[:, bass.ts(c, 2 * HIDP)],
                                 w1_in.rearrange("(c k) n -> c k n", k=128)[c]))
        for j in range(NBLK):
            _, wa = _blk(j)
            ev_w.append(plan.dma("gpsimd", w2_bf.ap()[0:wa, bass.ts(j, 1024)],
                                 w2_in[j * 128: j * 128 + wa, :]))
        ev_w.append(plan.dma("sync", b1c.ap(), b1_in))
        ev_w.append(plan.dma("sync", b2c.ap(), b2_in))
        ev_w.append(plan.dma("sync", lngb.ap(), lng_in))
        ev_w.append(plan.dma("sync", lnbb.ap(), lnb_in))
        ev_w += [ev_eps, ev_id]

        xg_t = xg_in.rearrange("(c k) t -> c k t", k=128)
        st_evs = []
        prev = list(ev_w)
        for g in range(min(NG, ng_run)):
            ev_ld = []
            for c in range(8):
                e = plan.dma("sync", xg32.ap()[:, c * (G + 1): (c + 1) * (G + 1)],
                             xg_t[c][:, g * G: g * G + G + 1], after=prev)
                off = c * (G + 1) + (1 if c < 4 else 0)
                e = plan.do("scalar", lambda E, c=c, off=off: E.activation(
                    xgbf.ap()[:, bass.ts(c, G)],
                    xg32.ap()[:, off: off + G], AF.Identity), after=[e])
                ev_ld.append(e)

            def h2bf(c):
                return xgbf.ap()[:, bass.ts(c, G)]

            def h232(c):
                off = c * (G + 1) + (1 if c < 4 else 0)
                return xg32.ap()[:, off: off + G]

            def own32(c):
                return xg32.ap()[:, c * (G + 1) + 1: c * (G + 1) + 1 + G]

            ev_h2n = []
            prev_sub = list(ev_ld)
            for sub in range(2):
                ev_tp = []
                for c in range(8):
                    et = plan.do("tensor", lambda E, c=c, sub=sub: E.transpose(
                        ps_tp.ap(), h2bf(c)[:, sub * 128:(sub + 1) * 128], identb.ap()),
                        after=[ev_id] + prev_sub)
                    ee = plan.do("scalar", lambda E, c=c, sub=sub: E.activation(
                        h2tm.ap()[:, sub * 1024 + c * 128: sub * 1024 + (c + 1) * 128],
                        ps_tp.ap(), AF.Identity), after=[et])
                    eg = plan.do("tensor", lambda E: E.nop(), after=[ee])
                    ev_tp.append(ee)
                tm = h2tm.ap()[:, sub * 1024:(sub + 1) * 1024]
                s1c = cols.ap()[:, 0:1]
                s2c = cols.ap()[:, 1:2]
                mc_ = cols.ap()[:, 2:3]
                vc_ = cols.ap()[:, 3:4]
                t0c = cols.ap()[:, 4:5]
                rsc = cols.ap()[:, 5:6]
                e1 = plan.do("vector", lambda E, tm=tm: E.reduce_sum(s1c, tm, axis=AX), after=ev_tp)
                eq = plan.do("scalar", lambda E, tm=tm: E.activation(sqtm.ap(), tm, AF.Square), after=ev_tp)
                e2 = plan.do("vector", lambda E: E.reduce_sum(s2c, sqtm.ap(), axis=AX), after=[eq, e1])
                e3 = plan.do("vector", lambda E: E.tensor_scalar_mul(mc_, s1c, 1.0 / D), after=[e2])
                e4 = plan.do("vector", lambda E: E.tensor_scalar(vc_, mc_, mc_, None, ALU.mult), after=[e3])
                e5 = plan.do("vector", lambda E: E.scalar_tensor_tensor(
                    t0c, s2c, 1.0 / D, vc_, ALU.mult, ALU.subtract), after=[e4])
                e6 = plan.do("scalar", lambda E: E.activation(t0c, t0c, AF.Ln, bias=eps_col.ap()[:, 0:1]), after=[e5, ev_eps])
                e7 = plan.do("scalar", lambda E: E.activation(rsc, t0c, AF.Exp, scale=-0.5), after=[e6])
                xh = xhtm.ap()[:, sub * 1024:(sub + 1) * 1024]
                hn = hntm.ap()[:, sub * 1024:(sub + 1) * 1024]
                e8 = plan.do("vector", lambda E, tm=tm, xh=xh: E.tensor_scalar(
                    xh, tm, mc_, rsc, ALU.subtract, ALU.mult), after=[e7])
                e9 = plan.do("gpsimd", lambda E, xh=xh: E.tensor_mul(sqtm.ap(), xh, lngb.ap()), after=[e8])
                e9 = plan.do("vector", lambda E, hn=hn: E.tensor_add(hn, sqtm.ap(), lnbb.ap()), after=[e9])
                for c in range(8):
                    et = plan.do("tensor", lambda E, c=c, sub=sub, hn=hn: E.transpose(
                        ps_tp.ap(), hn[:, c * 128:(c + 1) * 128], identb.ap()),
                        after=[e9, ev_id])
                    ee = plan.do("scalar", lambda E, c=c, sub=sub: E.activation(
                        h2n.ap()[:, c * G + sub * 128: c * G + sub * 128 + 128],
                        ps_tp.ap(), AF.Identity), after=[et])
                    eg = plan.do("tensor", lambda E: E.nop(), after=[ee])
                    ev_h2n.append(ee)
                prev_sub = ev_h2n[-8:]

            ev_hg_all = []
            lu_a = {0: None, 1: None}
            for j in range(min(NBLK, nblk_run)):
                pp = j % 2
                ps_a = ps_a2[pp]
                ps_g = ps_g2[pp]
                gg = gg2[pp]
                off, wa = _blk(j)
                dep = [lu_a[pp]] if lu_a[pp] is not None else list(ev_h2n)
                ea = eg = None
                for kc in range(8):
                    la = w1_bf.ap()[:, kc * 2 * HIDP + off: kc * 2 * HIDP + off + wa]
                    lg = w1_bf.ap()[:, kc * 2 * HIDP + off + wa: kc * 2 * HIDP + off + 2 * wa]
                    ea = plan.do("tensor", lambda E, la=la, kc=kc, wa=wa, ps_a=ps_a: E.matmul(
                        ps_a.ap()[0:wa, 0:G], la, h2n.ap()[:, bass.ts(kc, G)],
                        start=(kc == 0), stop=(kc == 7)), after=dep + ev_h2n)
                    eg = plan.do("tensor", lambda E, lg=lg, kc=kc, wa=wa, ps_g=ps_g: E.matmul(
                        ps_g.ap()[0:wa, 0:G], lg, h2n.ap()[:, bass.ts(kc, G)],
                        start=(kc == 0), stop=(kc == 7)), after=dep + ev_h2n)
                e_gg = plan.do("scalar", lambda E, j=j, wa=wa, ps_g=ps_g, gg=gg: E.activation(
                    gg.ap()[0:wa, :], ps_g.ap()[0:wa, 0:G], AF.Gelu,
                    bias=b1c.ap()[0:wa, 2 * j + 1: 2 * j + 2]), after=[eg])
                e_hg = plan.do("vector", lambda E, j=j, wa=wa, ps_a=ps_a, gg=gg: E.scalar_tensor_tensor(
                    hg.ap()[0:wa, bass.ts(j, G)], ps_a.ap()[0:wa, 0:G], b1c.ap()[0:wa, 2 * j: 2 * j + 1],
                    gg.ap()[0:wa, :], ALU.add, ALU.mult), after=[ea, e_gg])
                lu_a[pp] = e_hg
                ev_hg_all.append(e_hg)
            xm_done = []
            lu_c = {0: None, 1: None}
            for mc in range(8):
                if not do_w2:
                    e = plan.do("vector", lambda E, mc=mc: E.tensor_copy(
                        xo_sb.ap()[:, bass.ts(mc, G)], own32(mc)), after=ev_hg_all + ev_h2n)
                    xm_done.append(e)
                    continue
                pp = mc % 2
                ps_acc = ps_acc2[pp]
                dep2 = [lu_c[pp]] if lu_c[pp] is not None else []
                ev_mm2 = None
                for j in range(min(NBLK, nblk_run)):
                    off, wa = _blk(j)
                    lw = w2_bf.ap()[0:wa, j * 1024 + mc * 128: j * 1024 + mc * 128 + 128]
                    ev_mm2 = plan.do("tensor", lambda E, lw=lw, j=j, wa=wa, ps_acc=ps_acc: E.matmul(
                        ps_acc.ap()[:, 0:G], lw, hg.ap()[0:wa, bass.ts(j, G)],
                        start=(j == 0), stop=(j == min(NBLK, nblk_run) - 1)),
                        after=ev_hg_all + dep2)
                e = plan.do("vector", lambda E, mc=mc, ps_acc=ps_acc: E.scalar_tensor_tensor(
                    xo_sb.ap()[:, bass.ts(mc, G)], ps_acc.ap()[:, 0:G], b2c.ap()[:, mc:mc + 1],
                    own32(mc), ALU.add, ALU.add), after=[ev_mm2] + ev_h2n)
                xm_done.append(e)
                lu_c[pp] = e
            ev_st = []
            for c in range(8):
                ev_st.append(plan.dma("sync", xo_out.rearrange("(c k) t -> c k t", k=128)[c][:, g * G:(g + 1) * G],
                                      xo_sb.ap()[:, bass.ts(c, G)], after=xm_done))

            st_evs += ev_st
            prev = xm_done + ev_st
        plan.wait("sync", st_evs)
        plan.finalize(ctx)
    return nc


def inner_ffn_inputs(xm_fm_b, t0, w1_l, b1_l, w2_l, b2_l, lng_l, lnb_l):
    if t0 == 0:
        first = np.zeros((D, 1), np.float32)
    else:
        first = xm_fm_b[:, t0 - 1:t0]
    xg = np.concatenate([first, xm_fm_b[:, t0:t0 + OWN]], axis=1)
    w1p = np.zeros((D, 2 * HIDP), np.float32)
    b1cols = np.zeros((128, 2 * NBLK), np.float32)
    for j in range(NBLK):
        off, wa = _blk(j)
        wreal = 128 if j < 21 else 42
        a0 = 128 * j
        ca = np.arange(a0, a0 + wreal)
        cg = HID + ca
        w1p[:, off:off + wreal] = w1_l[:, ca]
        w1p[:, off + wa:off + wa + wreal] = w1_l[:, cg]
        b1cols[0:wreal, 2 * j] = b1_l[ca]
        b1cols[0:wreal, 2 * j + 1] = b1_l[cg]
    w1re = np.ascontiguousarray(w1p)
    w2p = np.zeros((HIDP, D), np.float32)
    w2p[0:HID] = w2_l
    return {
        "xg": np.ascontiguousarray(xg),
        "w1re": w1re,
        "b1re": b1cols,
        "w2": w2p,
        "b2c": vec_cols(b2_l),
        "lng": np.ascontiguousarray(np.broadcast_to(np.asarray(lng_l, np.float32)[None, :], (128, 1024))),
        "lnb": np.ascontiguousarray(np.broadcast_to(np.asarray(lnb_l, np.float32)[None, :], (128, 1024))),
    }



NTW = 19
TSW = NTW * TILE


def build_attn0():
    nc = bass.Bass("TRN2", target_bir_lowering=False)
    xnw_in = nc.dram_tensor("xnw", [D, TSW], F32, kind="ExternalInput").ap()
    wq_in = nc.dram_tensor("wq", [D, D], F32, kind="ExternalInput").ap()
    wk_in = nc.dram_tensor("wk", [D, 256], F32, kind="ExternalInput").ap()
    wv_in = nc.dram_tensor("wv", [D, 256], F32, kind="ExternalInput").ap()
    wo_in = nc.dram_tensor("wo0", [D, D], F32, kind="ExternalInput").ap()
    maskT_in = nc.dram_tensor("maskT0", [NT, 3, 128, 128], BF16, kind="ExternalInput").ap()
    maskS_in = nc.dram_tensor("maskS0", [NT, 4, 128], BF16, kind="ExternalInput").ap()
    a0_out = nc.dram_tensor("a0", [D, OWN], F32, kind="ExternalOutput").ap()

    NH = 16
    with ExitStack() as ctx:
        plan = Plan(nc)
        sb = nc.alloc_sbuf_tensor
        ps = nc.alloc_psum_tensor
        xnw_bf = sb("xnw_bf", [128, 8 * TSW], BF16)
        wq_bf = sb("wq_bf", [128, 8 * 1024], BF16)
        wk_bf = sb("wk_bf", [128, 8 * 256], BF16)
        wv_bf = sb("wv_bf", [128, 8 * 256], BF16)
        wo_bf = sb("wo_bf", [128, 8 * 1024], BF16)
        kfm = sb("a_kfm", [128, 2 * TSW], BF16)
        vbuf = sb("a_vbuf", [128, NTW * 4 * 65], BF16)
        qfm = sb("a_qfm", [128, 8 * OWN], BF16)
        ofm = sb("a_ofm", [128, 8 * OWN], BF16)
        pm_s = sb("a_pm", [128, 512], BF16)
        pmm_s = sb("a_pmm", [128, 512], BF16)
        mk_s = sb("a_mk", [128, 3 * 128], BF16)
        mks_s = sb("a_mks", [4, 16 * 128], BF16)
        r_s = sb("a_r", [128, 1], F32)
        on_s = sb("a_on", [128, 64], BF16)
        xr_s = sb("a_xr", [128, 8 * 512], F32)
        bias20, ev_b20 = const_col(nc, plan, "a_bias20", -20.0)
        identb, ev_idb = alloc_identity(nc, plan, BF16, name="a_identb")

        ps_qkv = ps("a_psqkv", [128, 512], F32)
        ps_st = ps("a_psst", [128, 512], F32)
        ps_tp = ps("a_pstp", [128, 128], BF16)
        ps_o = ps("a_pso", [128, 128], F32)
        ps_x = ps("a_psx", [128, 512], F32)

        ev_vones = plan.do("gpsimd", lambda E: E.memset(vbuf.ap(), 1.0))

        ev_w = []
        for c in range(8):
            ev_w.append(plan.dma("gpsimd", wq_bf.ap()[:, bass.ts(c, 1024)],
                                 wq_in.rearrange("(c k) n -> c k n", k=128)[c]))
            ev_w.append(plan.dma("gpsimd", wk_bf.ap()[:, bass.ts(c, 256)],
                                 wk_in.rearrange("(c k) n -> c k n", k=128)[c]))
            ev_w.append(plan.dma("gpsimd", wv_bf.ap()[:, bass.ts(c, 256)],
                                 wv_in.rearrange("(c k) n -> c k n", k=128)[c]))
            ev_w.append(plan.dma("gpsimd", wo_bf.ap()[:, bass.ts(c, 1024)],
                                 wo_in.rearrange("(c k) n -> c k n", k=128)[c]))
            ev_w.append(plan.dma("gpsimd", xnw_bf.ap()[:, bass.ts(c, TSW)],
                                 xnw_in.rearrange("(c k) t -> c k t", k=128)[c]))
        for j in range(NT):
            ev_w.append(plan.dma("scalar", mks_s.ap()[:, bass.ts(j, 128)], maskS_in[j]))
        prev = ev_w + [ev_vones, ev_b20, ev_idb]

        vb_done = {}
        ngrp = [(0, 512), (512, 512), (1024, 512), (1536, 512), (2048, 384)]
        ev_kv = []
        for mc in range(2):
            for (n0, nn) in ngrp:
                e = None
                for kc in range(8):
                    lhs = wk_bf.ap()[:, kc * 256 + mc * 128: kc * 256 + mc * 128 + 128]
                    e = plan.do("tensor", lambda E, lhs=lhs, kc=kc, n0=n0, nn=nn: E.matmul(
                        ps_qkv.ap()[:, 0:nn], lhs, xnw_bf.ap().rearrange(
                            "p (c t) -> p c t", t=TSW)[:, :, n0:n0 + nn].rearrange("p c t -> p (c t)") if False else xnw_bf.ap()[:, kc * TSW + n0: kc * TSW + n0 + nn],
                        start=(kc == 0), stop=(kc == 7)), after=prev)
                ee = plan.do("scalar", lambda E, mc=mc, n0=n0, nn=nn: E.activation(
                    kfm.ap()[:, mc * TSW + n0: mc * TSW + n0 + nn], ps_qkv.ap()[:, 0:nn],
                    AF.Identity), after=[e])
                ev_kv.append(ee)
                prev = [ee]
        for t in range(NTW):
            e = None
            for kc in range(8):
                lhs = xnw_bf.ap()[:, kc * TSW + t * 128: kc * TSW + t * 128 + 128]
                e = plan.do("tensor", lambda E, lhs=lhs, kc=kc: E.matmul(
                    ps_qkv.ap()[:, 0:256], lhs, wv_bf.ap()[:, bass.ts(kc, 256)],
                    start=(kc == 0), stop=(kc == 7)), after=prev)
            ee = plan.do("scalar", lambda E, t=t: E.activation(
                vbuf.ap()[:, t * 260: (t + 1) * 260].rearrange("p (h d) -> p h d", d=65)[:, :, 0:64],
                ps_qkv.ap()[:, 0:256].rearrange("p (h d) -> p h d", d=64), AF.Identity),
                after=[e, ev_vones])
            vb_done[t] = ee
            prev = [ee]
        ev_q = []
        for mc in range(8):
            for g in range(4):
                e = None
                for kc in range(8):
                    lhs = wq_bf.ap()[:, kc * 1024 + mc * 128: kc * 1024 + mc * 128 + 128]
                    rhs = xnw_bf.ap()[:, kc * TSW + 3 * 128 + g * 512: kc * TSW + 3 * 128 + g * 512 + 512]
                    e = plan.do("tensor", lambda E, lhs=lhs, rhs=rhs, kc=kc: E.matmul(
                        ps_qkv.ap(), lhs, rhs, start=(kc == 0), stop=(kc == 7)), after=prev)
                ee = plan.do("scalar", lambda E, mc=mc, g=g: E.activation(
                    qfm.ap()[:, mc * OWN + g * 512: mc * OWN + (g + 1) * 512], ps_qkv.ap(),
                    AF.Identity), after=[e])
                ev_q.append(ee)
                prev = [ee]

        attn_done = []
        ev_prev_head = list(prev) + ev_kv
        mask_prev = []
        for j in range(NT):
            ev_mk = None
            for k in range(3):
                ev_mk = plan.dma("sync", mk_s.ap()[:, bass.ts(k, 128)], maskT_in[j][k],
                                 after=mask_prev)
            masked_chunks = [0, 1, 2] if j < 4 else [0, 2]
            for h in range(NH):
                g = h // 4
                cc2 = g // 2
                rows0 = (g % 2) * 64
                qcc = (h % 4) + 4 * (h // 8)
                qrows0 = ((h // 4) % 2) * 64
                occ = h // 2
                orows0 = (h % 2) * 64
                ev_st = None
                for k in range(3):
                    lhs = kfm.ap()[rows0:rows0 + 64, cc2 * TSW + (j + 1 + k) * 128: cc2 * TSW + (j + 1 + k) * 128 + 128]
                    rhs = qfm.ap()[qrows0:qrows0 + 64, qcc * OWN + j * 128: qcc * OWN + j * 128 + 128]
                    ev_st = plan.do("tensor", lambda E, lhs=lhs, rhs=rhs, k=k: E.matmul(
                        ps_st.ap()[:, bass.ts(k, 128)], lhs, rhs, start=True, stop=True),
                        after=ev_prev_head)
                lhs_sink = kfm.ap()[rows0:rows0 + 64, cc2 * TSW + 0: cc2 * TSW + 4]
                rhs_q = qfm.ap()[qrows0:qrows0 + 64, qcc * OWN + j * 128: qcc * OWN + j * 128 + 128]
                ev_sk = plan.do("tensor", lambda E, lhs_sink=lhs_sink, rhs_q=rhs_q: E.matmul(
                    ps_st.ap()[0:4, 384:512], lhs_sink, rhs_q, start=True, stop=True),
                    after=ev_prev_head)
                ev_exp = plan.do("scalar", lambda E: E.activation(
                    pm_s.ap()[:, 0:384], ps_st.ap()[:, 0:384], AF.Exp,
                    bias=bias20.ap()[:, 0:1], scale=0.125), after=[ev_st, ev_b20])
                ev_exp2 = plan.do("scalar", lambda E: E.activation(
                    pm_s.ap()[0:4, 384:512], ps_st.ap()[0:4, 384:512], AF.Exp,
                    bias=bias20.ap()[0:4, 0:1], scale=0.125), after=[ev_sk, ev_exp])
                ev_msk = []
                for k in masked_chunks:
                    ev_msk.append(plan.do("gpsimd", lambda E, k=k: E.tensor_mul(
                        pmm_s.ap()[:, bass.ts(k, 128)], pm_s.ap()[:, bass.ts(k, 128)],
                        mk_s.ap()[:, bass.ts(k, 128)]), after=[ev_exp, ev_mk]))
                ev_msk.append(plan.do("gpsimd", lambda E, j=j: E.tensor_mul(
                    pmm_s.ap()[0:4, 384:512], pm_s.ap()[0:4, 384:512],
                    mks_s.ap()[:, bass.ts(j, 128)]), after=[ev_exp2]))
                ev_pv = None
                for k in range(3):
                    src = pmm_s if k in masked_chunks else pm_s
                    rhs = vbuf.ap()[:, ((j + 1 + k) * 4 + g) * 65: ((j + 1 + k) * 4 + g) * 65 + 65]
                    ev_pv = plan.do("tensor", lambda E, rhs=rhs, k=k, src=src: E.matmul(
                        ps_o.ap()[:, 0:65], src.ap()[:, bass.ts(k, 128)], rhs,
                        start=(k == 0), stop=False),
                        after=[ev_exp] + ev_msk + [vb_done[j + 1 + k]])
                rhs_vs = vbuf.ap()[0:4, (0 * 4 + g) * 65: (0 * 4 + g) * 65 + 65]
                ev_pv = plan.do("tensor", lambda E, rhs_vs=rhs_vs: E.matmul(
                    ps_o.ap()[:, 0:65], pmm_s.ap()[0:4, 384:512], rhs_vs,
                    start=False, stop=True), after=[ev_pv] + ev_msk + [vb_done[0]])
                ev_r = plan.do("vector", lambda E: E.reciprocal(r_s.ap(), ps_o.ap()[:, 64:65]),
                               after=[ev_pv])
                ev_on = plan.do("vector", lambda E: E.tensor_scalar_mul(
                    on_s.ap(), ps_o.ap()[:, 0:64], r_s.ap()), after=[ev_r], gate=True)
                ev_tp = plan.do("tensor", lambda E: E.transpose(
                    ps_tp.ap()[0:64, 0:128], on_s.ap(), identb.ap()), after=[ev_on, ev_idb])
                ocol = occ * OWN + j * 128
                ev_o = plan.do("scalar", lambda E, ocol=ocol, orows0=orows0: E.activation(
                    ofm.ap()[orows0:orows0 + 64, ocol:ocol + 128], ps_tp.ap()[0:64, :],
                    AF.Identity), after=[ev_tp])
                attn_done.append(ev_o)
                ev_prev_head = [ev_exp2, ev_on, ev_o]
            mask_prev = attn_done[-1:]

        st_evs = []
        prev_g = attn_done[-16:]
        a0_t = a0_out.rearrange("(c k) t -> c k t", k=128)
        for g in range(4):
            ev_prev_mc = list(prev_g)
            g_done = []
            for mc in range(8):
                e = None
                for kc in range(8):
                    lhs = wo_bf.ap()[:, kc * 1024 + mc * 128: kc * 1024 + mc * 128 + 128]
                    rhs = ofm.ap()[:, kc * OWN + g * 512: kc * OWN + g * 512 + 512]
                    e = plan.do("tensor", lambda E, lhs=lhs, rhs=rhs, kc=kc: E.matmul(
                        ps_x.ap(), lhs, rhs, start=(kc == 0), stop=(kc == 7)),
                        after=attn_done + ev_prev_mc)
                e = plan.do("scalar", lambda E, mc=mc: E.activation(
                    xr_s.ap()[:, bass.ts(mc, 512)], ps_x.ap(), AF.Identity), after=[e])
                ev_prev_mc = [e]
                g_done.append(e)
            for c in range(8):
                ev = plan.dma("sync", a0_t[c][:, g * 512:(g + 1) * 512],
                              xr_s.ap()[:, bass.ts(c, 512)], after=g_done)
                st_evs.append(ev)
            prev_g = g_done + st_evs[-8:]
        plan.wait("sync", st_evs)
        plan.finalize(ctx)
    return nc



def build_cmb():
    nc = bass.Bass("TRN2", target_bir_lowering=False)
    x_in = nc.dram_tensor("cx", [D, OWN], F32, kind="ExternalInput").ap()
    a_in = nc.dram_tensor("ca", [D, OWN], F32, kind="ExternalInput").ap()
    f_in = nc.dram_tensor("cf", [D, OWN], F32, kind="ExternalInput").ap()
    x1_out = nc.dram_tensor("x1", [D, OWN], F32, kind="ExternalOutput").ap()
    xn1_out = nc.dram_tensor("xn1", [D, OWN], F32, kind="ExternalOutput").ap()
    G = 512
    with ExitStack() as ctx:
        plan = Plan(nc)
        sb = nc.alloc_sbuf_tensor
        xg = sb("c_x", [128, 8 * G], F32)
        ag = sb("c_a", [128, 8 * G], F32)
        fg = sb("c_f", [128, 8 * G], F32)
        tg = sb("c_t", [128, 8 * G], F32)
        x1g = sb("c_x1", [128, 8 * G], F32)
        xn1g = sb("c_xn1", [128, 8 * G], F32)
        identf, ev_idf = alloc_identity(nc, plan, F32, name="c_identf")
        eps_col, ev_eps = const_col(nc, plan, "c_eps", 1e-5)
        scr = {}
        x_t = x_in.rearrange("(c k) t -> c k t", k=128)
        a_t = a_in.rearrange("(c k) t -> c k t", k=128)
        f_t = f_in.rearrange("(c k) t -> c k t", k=128)
        x1_t = x1_out.rearrange("(c k) t -> c k t", k=128)
        xn1_t = xn1_out.rearrange("(c k) t -> c k t", k=128)
        st_evs = []
        prev = [ev_idf, ev_eps]
        for g in range(OWN // G):
            ev_ld = []
            for c in range(8):
                ev_ld.append(plan.dma("sync", xg.ap()[:, bass.ts(c, G)],
                                      x_t[c][:, g * G:(g + 1) * G], after=prev))
                ev_ld.append(plan.dma("scalar", ag.ap()[:, bass.ts(c, G)],
                                      a_t[c][:, g * G:(g + 1) * G], after=prev))
                ev_ld.append(plan.dma("gpsimd", fg.ap()[:, bass.ts(c, G)],
                                      f_t[c][:, g * G:(g + 1) * G], after=prev))
            add_done = []
            for c in range(8):
                e = plan.do("vector", lambda E, c=c: E.tensor_add(
                    tg.ap()[:, bass.ts(c, G)], ag.ap()[:, bass.ts(c, G)],
                    fg.ap()[:, bass.ts(c, G)]), after=ev_ld)
                e = plan.do("gpsimd", lambda E, c=c: E.tensor_add(
                    x1g.ap()[:, bass.ts(c, G)], tg.ap()[:, bass.ts(c, G)],
                    xg.ap()[:, bass.ts(c, G)]), after=[e] + ev_ld)
                add_done.append(e)
            ev_st1 = []
            for c in range(8):
                ev_st1.append(plan.dma("sync", x1_t[c][:, g * G:(g + 1) * G],
                                       x1g.ap()[:, bass.ts(c, G)], after=add_done))
            ln_done = emit_ln_tm(nc, plan, f"cln{g}", 4, G, x1g, xn1g, identf,
                                 eps_col, ev_eps, ev_idf, add_done, scr=scr)
            ev_st2 = []
            for c in range(8):
                ev_st2.append(plan.dma("sync", xn1_t[c][:, g * G:(g + 1) * G],
                                       xn1g.ap()[:, bass.ts(c, G)], after=ln_done))
            st_evs += ev_st1 + ev_st2
            prev = ln_done + ev_st1 + ev_st2
        plan.wait("sync", st_evs)
        plan.finalize(ctx)
    return nc



def masks_attn0(t0):
    mw = np.zeros((NT, 3, 128, 128), np.float32)
    msk = np.zeros((NT, 4, 128), np.float32)
    r = np.arange(128)
    for j in range(NT):
        qt = t0 // 128 + j
        qtok = qt * 128 + r[None, :]
        for k in range(3):
            ktok = (qt - 2 + k) * 128 + r[:, None]
            valid = (ktok <= qtok) & (qtok - ktok <= 255) & (ktok >= 4)
            mw[j, k] = valid
        ksink = np.arange(4)[:, None]
        msk[j] = (ksink <= qtok)
    return mw.astype(ml_dtypes.bfloat16), msk.astype(ml_dtypes.bfloat16)


def attn0_inputs(xn_fm_b, t0, wq, wk, wv, wo):
    lo = t0 - 2 * TILE
    sink = xn_fm_b[:, 0:128]
    if lo < 0:
        pad = np.zeros((D, -lo), np.float32)
        win = np.concatenate([pad, xn_fm_b[:, 0:t0 + OWN]], axis=1)
    else:
        win = xn_fm_b[:, lo:t0 + OWN]
    xnw = np.concatenate([sink, win], axis=1)
    mw, msk = masks_attn0(t0)
    perm = np.zeros(1024, np.int64)
    for ck in range(8):
        hd0 = ck if ck < 4 else ck + 4
        hd1 = ck + 4 if ck < 4 else ck + 8
        perm[ck * 128: ck * 128 + 64] = np.arange(hd0 * 64, hd0 * 64 + 64)
        perm[ck * 128 + 64: ck * 128 + 128] = np.arange(hd1 * 64, hd1 * 64 + 64)
    wq = np.asarray(wq, np.float32)[:, perm]
    return {
        "xnw": np.ascontiguousarray(xnw),
        "wq": np.ascontiguousarray(wq), "wk": np.ascontiguousarray(wk),
        "wv": np.ascontiguousarray(wv), "wo0": np.ascontiguousarray(wo),
        "maskT0": mw, "maskS0": msk,
    }



NTW1 = 21
TSW1 = NTW1 * TILE


def build_l1attn():
    nc = bass.Bass("TRN2", target_bir_lowering=False)
    xnw_in = nc.dram_tensor("xnw1", [D, TSW1], F32, kind="ExternalInput").ap()
    wq_in = nc.dram_tensor("wq1", [D, D], F32, kind="ExternalInput").ap()
    wk_in = nc.dram_tensor("wk1", [D, D], F32, kind="ExternalInput").ap()
    wv_in = nc.dram_tensor("wv1", [D, D], F32, kind="ExternalInput").ap()
    wo_in = nc.dram_tensor("wo1", [D, D], F32, kind="ExternalInput").ap()
    maskT_in = nc.dram_tensor("maskT1", [NT, 5, 128, 128], BF16, kind="ExternalInput").ap()
    a1_out = nc.dram_tensor("a1", [D, OWN], F32, kind="ExternalOutput").ap()

    with ExitStack() as ctx:
        plan = Plan(nc)
        sb = nc.alloc_sbuf_tensor
        ps = nc.alloc_psum_tensor
        xnw_bf = sb("l_xnw", [128, 8 * TSW1], BF16)
        wbuf = sb("l_wbuf", [128, 8 * 1024], BF16)
        wo_bf = sb("l_wo", [128, 8 * 1024], BF16)
        kfm = sb("l_kfm", [128, 4 * TSW1], BF16)
        vbuf = sb("l_vbuf", [128, NTW1 * 8 * 65], BF16)
        qfm = sb("l_qfm", [128, 4 * OWN], BF16)
        ofm = sb("l_ofm", [128, 8 * OWN], BF16)
        pm_s2 = [sb("l_pm_0", [128, 640], BF16), sb("l_pm_1", [128, 640], BF16)]
        pmm_s2 = [sb("l_pmm_0", [128, 640], BF16), sb("l_pmm_1", [128, 640], BF16)]
        pm2_s2 = [sb("l_pm2_0", [16, 128], BF16), sb("l_pm2_1", [16, 128], BF16)]
        mk_s = sb("l_mk", [128, 2 * 5 * 128], BF16)
        r_s2 = [sb("l_r0", [128, 1], F32), sb("l_r1", [128, 1], F32)]
        on_s2 = [sb("l_on0", [128, 64], BF16), sb("l_on1", [128, 64], BF16)]
        ast_s = sb("l_ast", [128, 8 * 256], F32)
        bias20, ev_b20 = const_col(nc, plan, "l_bias20", -20.0)
        identb, ev_idb = alloc_identity(nc, plan, BF16, name="l_identb")

        ps_qkv = ps("l_psqkv", [128, 512], F32)
        ps_st2 = [ps("l_psst0", [128, 640], F32), ps("l_psst1", [128, 640], F32)]
        ps_sk = ps("l_pssk", [128, 256], F32)
        ps_tpd = ps("l_pstpd", [128, 256], BF16)
        ps_o = ps("l_pso", [128, 128], F32)

        ev_vones = plan.do("gpsimd", lambda E: E.memset(vbuf.ap(), 1.0))
        ev_xn = []
        for c in range(8):
            ev_xn.append(plan.dma("gpsimd", xnw_bf.ap()[:, bass.ts(c, TSW1)],
                                  xnw_in.rearrange("(c k) t -> c k t", k=128)[c]))
        ev_wo = []
        for c in range(8):
            ev_wo.append(plan.dma("gpsimd", wo_bf.ap()[:, bass.ts(c, 1024)],
                                  wo_in.rearrange("(c k) n -> c k n", k=128)[c]))
        prev = ev_xn + ev_wo + [ev_vones, ev_b20, ev_idb]

        attn_done_all = []
        ngrp = [(0, 512), (512, 512), (1024, 512), (1536, 512), (2048, 512), (2560, 128)]
        for hp in range(2):
            hoff = hp * 512
            ev_w = []
            for c in range(8):
                ev_w.append(plan.dma("gpsimd", wbuf.ap()[:, bass.ts(c, 512)],
                                     wk_in.rearrange("(c k) n -> c k n", k=128)[c][:, hoff:hoff + 512],
                                     after=prev))
            ev_kv = []
            for mc in range(4):
                for (n0, nn) in ngrp:
                    e = None
                    for kc in range(8):
                        lhs = wbuf.ap()[:, kc * 512 + mc * 128: kc * 512 + mc * 128 + 128]
                        e = plan.do("tensor", lambda E, lhs=lhs, kc=kc, n0=n0, nn=nn: E.matmul(
                            ps_qkv.ap()[:, 0:nn], lhs, xnw_bf.ap()[:, kc * TSW1 + n0: kc * TSW1 + n0 + nn],
                            start=(kc == 0), stop=(kc == 7)), after=ev_w + prev)
                    ee = plan.do("scalar", lambda E, mc=mc, n0=n0, nn=nn: E.activation(
                        kfm.ap()[:, mc * TSW1 + n0: mc * TSW1 + n0 + nn], ps_qkv.ap()[:, 0:nn],
                        AF.Identity), after=[e])
                    ev_kv.append(ee)
                    prev = [ee]
            ev_w2 = []
            for c in range(8):
                ev_w2.append(plan.dma("gpsimd", wbuf.ap()[:, bass.ts(c, 512)],
                                      wv_in.rearrange("(c k) n -> c k n", k=128)[c][:, hoff:hoff + 512],
                                      after=prev + ev_kv))
            vb_done = {}
            for t in range(NTW1):
                e = None
                for kc in range(8):
                    lhs = xnw_bf.ap()[:, kc * TSW1 + t * 128: kc * TSW1 + t * 128 + 128]
                    e = plan.do("tensor", lambda E, lhs=lhs, kc=kc: E.matmul(
                        ps_qkv.ap(), lhs, wbuf.ap()[:, bass.ts(kc, 512)],
                        start=(kc == 0), stop=(kc == 7)), after=ev_w2 + prev)
                ee = plan.do("scalar", lambda E, t=t: E.activation(
                    vbuf.ap()[:, t * 520: (t + 1) * 520].rearrange("p (h d) -> p h d", d=65)[:, :, 0:64],
                    ps_qkv.ap().rearrange("p (h d) -> p h d", d=64), AF.Identity),
                    after=[e, ev_vones])
                vb_done[t] = ee
                prev = [ee]
            ev_w3 = []
            for c in range(8):
                ev_w3.append(plan.dma("gpsimd", wbuf.ap()[:, bass.ts(c, 512)],
                                      wq_in.rearrange("(c k) n -> c k n", k=128)[c][:, hoff:hoff + 512],
                                      after=prev))
            for mc in range(4):
                for g in range(4):
                    e = None
                    for kc in range(8):
                        lhs = wbuf.ap()[:, kc * 512 + mc * 128: kc * 512 + mc * 128 + 128]
                        rhs = xnw_bf.ap()[:, kc * TSW1 + 5 * 128 + g * 512: kc * TSW1 + 5 * 128 + g * 512 + 512]
                        e = plan.do("tensor", lambda E, lhs=lhs, rhs=rhs, kc=kc: E.matmul(
                            ps_qkv.ap(), lhs, rhs, start=(kc == 0), stop=(kc == 7)),
                            after=ev_w3 + prev)
                    ee = plan.do("scalar", lambda E, mc=mc, g=g: E.activation(
                        qfm.ap()[:, mc * OWN + g * 512: mc * OWN + (g + 1) * 512], ps_qkv.ap(),
                        AF.Identity), after=[e])
                    prev = [ee]
            attn_done = []
            last_use = {0: {"st": None, "o": None, "tp": None, "pm": None, "sk": None},
                        1: {"st": None, "o": None, "tp": None, "pm": None, "sk": None}}
            p2_prev = list(prev)
            prev_masks = {0: p2_prev, 1: p2_prev}
            lu_o_any = [None]
            hi = 0
            for j in range(NT):
                mslot = j % 2
                mk_ap = mk_s.ap()[:, mslot * 640:(mslot + 1) * 640]
                ev_mk = None
                for k in range(5):
                    ev_mk = plan.dma("sync", mk_ap[:, bass.ts(k, 128)], maskT_in[j][k],
                                     after=prev_masks[mslot] + p2_prev)
                masked_chunks = [0, 1, 2, 3, 4] if j < 5 else [0, 4]
                for hl in range(8):
                    pp = hi % 2
                    ps_st = ps_st2[pp]
                    pm_s, pmm_s, pm2_s = pm_s2[pp], pmm_s2[pp], pm2_s2[pp]
                    r_s, on_s = r_s2[pp], on_s2[pp]
                    sk_ap = ps_sk.ap()[0:16, pp * 128:pp * 128 + 128]
                    tp_ap = ps_tpd.ap()[0:64, pp * 128:pp * 128 + 128]
                    lu = last_use[pp]
                    cc = hl // 2
                    rows0 = (hl % 2) * 64
                    ev_st = None
                    for k in range(5):
                        lhs = kfm.ap()[rows0:rows0 + 64, cc * TSW1 + (j + 1 + k) * 128: cc * TSW1 + (j + 1 + k) * 128 + 128]
                        rhs = qfm.ap()[rows0:rows0 + 64, cc * OWN + j * 128: cc * OWN + j * 128 + 128]
                        ev_st = plan.do("tensor", lambda E, lhs=lhs, rhs=rhs, k=k, ps_st=ps_st: E.matmul(
                            ps_st.ap()[:, bass.ts(k, 128)], lhs, rhs, start=True, stop=True),
                            after=[lu["st"]] + p2_prev)
                    lhs_sink = kfm.ap()[rows0:rows0 + 64, cc * TSW1 + 0: cc * TSW1 + 16]
                    rhs_q = qfm.ap()[rows0:rows0 + 64, cc * OWN + j * 128: cc * OWN + j * 128 + 128]
                    ev_sk = plan.do("tensor", lambda E, lhs_sink=lhs_sink, rhs_q=rhs_q, sk_ap=sk_ap: E.matmul(
                        sk_ap, lhs_sink, rhs_q, start=True, stop=True),
                        after=[lu["sk"], ev_st])
                    ev_exp = plan.do("scalar", lambda E, pm_s=pm_s, ps_st=ps_st: E.activation(
                        pm_s.ap(), ps_st.ap(), AF.Exp, bias=bias20.ap()[:, 0:1], scale=0.125),
                        after=[ev_st, ev_b20, lu["pm"]])
                    ev_exp2 = plan.do("scalar", lambda E, pm2_s=pm2_s, sk_ap=sk_ap: E.activation(
                        pm2_s.ap(), sk_ap, AF.Exp,
                        bias=bias20.ap()[0:16, 0:1], scale=0.125), after=[ev_sk, ev_exp])
                    ev_msk = []
                    for k in masked_chunks:
                        ev_msk.append(plan.do("gpsimd", lambda E, k=k, mk_ap=mk_ap, pm_s=pm_s, pmm_s=pmm_s: E.tensor_mul(
                            pmm_s.ap()[:, bass.ts(k, 128)], pm_s.ap()[:, bass.ts(k, 128)],
                            mk_ap[:, bass.ts(k, 128)]), after=[ev_exp, ev_mk]))
                    ev_pv = lu_o_any[0]
                    for k in range(5):
                        src = pmm_s if k in masked_chunks else pm_s
                        rhs = vbuf.ap()[:, ((j + 1 + k) * 8 + hl) * 65: ((j + 1 + k) * 8 + hl) * 65 + 65]
                        ev_pv = plan.do("tensor", lambda E, rhs=rhs, k=k, src=src: E.matmul(
                            ps_o.ap()[:, 0:65], src.ap()[:, bass.ts(k, 128)], rhs,
                            start=(k == 0), stop=False),
                            after=[ev_exp, ev_pv] + ev_msk + [vb_done[j + 1 + k]])
                    rhs_vs = vbuf.ap()[0:16, (0 * 8 + hl) * 65: (0 * 8 + hl) * 65 + 65]
                    ev_pv = plan.do("tensor", lambda E, rhs_vs=rhs_vs, pm2_s=pm2_s: E.matmul(
                        ps_o.ap()[:, 0:65], pm2_s.ap(), rhs_vs,
                        start=False, stop=True), after=[ev_pv, ev_exp2, vb_done[0]])
                    ev_r = plan.do("vector", lambda E, r_s=r_s: E.reciprocal(r_s.ap(), ps_o.ap()[:, 64:65]),
                                   after=[ev_pv])
                    ev_on = plan.do("vector", lambda E, on_s=on_s, r_s=r_s: E.tensor_scalar_mul(
                        on_s.ap(), ps_o.ap()[:, 0:64], r_s.ap()), after=[ev_r], gate=True)
                    ev_tp = plan.do("tensor", lambda E, on_s=on_s, tp_ap=tp_ap: E.transpose(
                        tp_ap, on_s.ap(), identb.ap()), after=[ev_on, ev_idb, lu["tp"]])
                    h = hp * 8 + hl
                    ocol = (h // 2) * OWN + j * 128
                    orows0 = (h % 2) * 64
                    ev_o = plan.do("scalar", lambda E, ocol=ocol, orows0=orows0, tp_ap=tp_ap: E.activation(
                        ofm.ap()[orows0:orows0 + 64, ocol:ocol + 128], tp_ap,
                        AF.Identity), after=[ev_tp])
                    attn_done.append(ev_o)
                    lu["st"] = ev_exp
                    lu["sk"] = ev_exp2
                    lu["pm"] = ev_pv
                    lu["tp"] = ev_o
                    lu_o_any[0] = ev_on
                    hi += 1
                prev_masks[mslot] = attn_done[-8:]
            attn_done_all += attn_done
            prev = attn_done[-8:]

        st_evs = []
        prev_g = attn_done_all[-16:]
        a1_t = a1_out.rearrange("(c k) t -> c k t", k=128)
        for g in range(8):
            ev_prev_mc = list(prev_g)
            g_done = []
            for mc in range(8):
                e = None
                for kc in range(8):
                    lhs = wo_bf.ap()[:, kc * 1024 + mc * 128: kc * 1024 + mc * 128 + 128]
                    rhs = ofm.ap()[:, kc * OWN + g * 256: kc * OWN + g * 256 + 256]
                    e = plan.do("tensor", lambda E, lhs=lhs, rhs=rhs, kc=kc: E.matmul(
                        ps_qkv.ap()[:, 0:256], lhs, rhs, start=(kc == 0), stop=(kc == 7)),
                        after=attn_done_all + ev_prev_mc)
                e = plan.do("scalar", lambda E, mc=mc: E.activation(
                    ast_s.ap()[:, bass.ts(mc, 256)], ps_qkv.ap()[:, 0:256], AF.Identity), after=[e])
                ev_prev_mc = [e]
                g_done.append(e)
            for c in range(8):
                st_evs.append(plan.dma("sync", a1_t[c][:, g * 256:(g + 1) * 256],
                                       ast_s.ap()[:, bass.ts(c, 256)], after=g_done))
            prev_g = g_done + st_evs[-8:]
        plan.wait("sync", st_evs)
        plan.finalize(ctx)
    return nc


def masks_l1(t0):
    mw = np.zeros((NT, 5, 128, 128), np.float32)
    r = np.arange(128)
    for j in range(NT):
        qt = t0 // 128 + j
        qtok = qt * 128 + r[None, :]
        for k in range(5):
            ktok = (qt - 4 + k) * 128 + r[:, None]
            valid = (ktok <= qtok) & (qtok - ktok <= 512) & (ktok >= 16)
            mw[j, k] = valid
    return mw.astype(ml_dtypes.bfloat16)


def l1attn_inputs(xn1_fm_b, t0, w_qkv, w_out):
    lo = t0 - 4 * TILE
    sink = xn1_fm_b[:, 0:128]
    if lo < 0:
        pad = np.zeros((D, -lo), np.float32)
        win = np.concatenate([pad, xn1_fm_b[:, 0:t0 + OWN]], axis=1)
    else:
        win = xn1_fm_b[:, lo:t0 + OWN]
    xnw = np.concatenate([sink, win], axis=1)
    w_qkv = np.asarray(w_qkv, np.float32)
    return {
        "xnw1": np.ascontiguousarray(xnw),
        "wq1": np.ascontiguousarray(w_qkv[0:1024].T),
        "wk1": np.ascontiguousarray(w_qkv[1024:2048].T),
        "wv1": np.ascontiguousarray(w_qkv[2048:3072].T),
        "wo1": np.ascontiguousarray(np.asarray(w_out, np.float32).T),
        "maskT1": masks_l1(t0),
    }



def build_wquant():
    nc = bass.Bass("TRN2", target_bir_lowering=False)
    wg_in = nc.dram_tensor("wgt", [D, 4096], F32, kind="ExternalInput").ap()
    wu_in = nc.dram_tensor("wut", [D, 4096], F32, kind="ExternalInput").ap()
    wd_in = nc.dram_tensor("wdt", [4096, D], F32, kind="ExternalInput").ap()
    wgq_out = nc.dram_tensor("wgq", [D, 4096], BF16, kind="ExternalOutput").ap()
    wuq_out = nc.dram_tensor("wuq", [D, 4096], BF16, kind="ExternalOutput").ap()
    wdq_out = nc.dram_tensor("wdq", [4096, D], BF16, kind="ExternalOutput").ap()
    minv_out = nc.dram_tensor("minv", [128, 3], F32, kind="ExternalOutput").ap()

    with ExitStack() as ctx:
        plan = Plan(nc)
        sb = nc.alloc_sbuf_tensor
        ps = nc.alloc_psum_tensor
        W = 4096
        wbuf = sb("wq_wbuf", [128, 2 * W], F32)
        tbuf = sb("wq_tbuf", [128, 2 * W], F32)
        t2buf = sb("wq_t2buf", [128, 2 * W], F32)
        ibuf = sb("wq_ibuf", [128, 2 * W], I32)
        qbuf = sb("wq_qbuf", [128, 2 * W], BF16)
        colz = sb("wq_cols", [128, 16], F32)
        hib = sb("wq_hib", [128, 2], BF16)
        ones128 = sb("wq_ones", [128, 128], BF16)
        ps_tot = ps("wq_pstot", [128, 512], F32)
        ev_on = plan.do("vector", lambda E: E.memset(ones128.ap(), 1.0))
        minv_cols = sb("wq_minv", [128, 4], F32)

        specs = [(wg_in, wgq_out, 8, 4096, 1024 * 4096, 0),
                 (wu_in, wuq_out, 8, 4096, 1024 * 4096, 1),
                 (wd_in, wdq_out, 32, 1024, 4096 * 1024, 2)]
        prev = [ev_on]
        st_evs = []
        qprev = {}
        for (win, wout, nchunks, width, nelem, widx) in specs:
            w_t = win.rearrange("(c k) n -> c k n", k=128)
            wo_t = wout.rearrange("(c k) n -> c k n", k=128)
            acc = [colz.ap()[:, 0:1], colz.ap()[:, 1:2]]
            red = colz.ap()[:, 2:3]
            h32 = colz.ap()[:, 3:4]
            lo32 = colz.ap()[:, 4:5]
            tot = colz.ap()[:, 5:6]
            minv = minv_cols.ap()[:, widx:widx + 1]
            sw = colz.ap()[:, 6:7]
            e_acc = plan.do("vector", lambda E: E.memset(acc[0], 0.0), after=prev)
            prevdma = []
            par = 0
            ev_ld = {}
            for c in range(nchunks):
                sl = wbuf.ap()[:, (c % 2) * W: (c % 2) * W + width]
                ev = plan.dma("sync", sl, w_t[c], after=prevdma + prev)
                ev_ld[c] = ev
                prevdma = [ev]
                e_r = plan.do("vector", lambda E, sl=sl: E.reduce_sum(
                    red, sl, axis=AX, apply_absolute_value=True), after=[ev], gate=True)
                e_acc = plan.do("vector", lambda E, par=par: E.tensor_add(
                    acc[1 - par], acc[par], red), after=[e_r, e_acc], gate=True)
                par = 1 - par
            e = plan.do("vector", lambda E, par=par: E.tensor_copy(hib.ap()[:, 0:1], acc[par]),
                        after=[e_acc], gate=True)
            e = plan.do("vector", lambda E: E.tensor_copy(h32, hib.ap()[:, 0:1]), after=[e], gate=True)
            e = plan.do("vector", lambda E, par=par: E.tensor_sub(lo32, acc[par], h32), after=[e], gate=True)
            e = plan.do("vector", lambda E: E.tensor_copy(hib.ap()[:, 1:2], lo32), after=[e], gate=True)
            em1 = plan.do("tensor", lambda E: E.matmul(
                ps_tot.ap()[:, 0:1], ones128.ap(), hib.ap()[:, 0:1], start=True, stop=False),
                after=[e, ev_on])
            em2 = plan.do("tensor", lambda E: E.matmul(
                ps_tot.ap()[:, 0:1], ones128.ap(), hib.ap()[:, 1:2], start=False, stop=True),
                after=[e, em1])
            e = plan.do("vector", lambda E, nelem=nelem: E.tensor_scalar_mul(
                tot, ps_tot.ap()[:, 0:1], 1.0 / nelem), after=[em2])
            e = plan.do("vector", lambda E, minv=minv: E.tensor_scalar_max(minv, tot, 1e-5),
                        after=[e], gate=True)
            e_sw = plan.do("vector", lambda E, minv=minv: E.reciprocal(sw, minv), after=[e], gate=True)
            prevq = [e_sw]
            sl_e1 = {0: qprev.get("e1_0"), 1: qprev.get("e1_1")}
            sl_e2 = {0: None, 1: None}
            sl_e3 = {0: None, 1: None}
            sl_e4 = {0: None, 1: None}
            sl_st = {0: None, 1: None}
            for c in range(nchunks):
                s = c % 2
                sl = wbuf.ap()[:, s * W: s * W + width]
                ev = plan.dma("sync", sl, w_t[c], after=[ev_ld[c], sl_e1[s], e_sw])
                e1 = plan.do("scalar", lambda E, sl=sl, s=s, width=width: E.activation(
                    tbuf.ap()[:, s * W: s * W + width], sl, AF.Identity, scale=sw),
                    after=[ev, e_sw, sl_e2[s]])
                e2 = plan.do("gpsimd", lambda E, s=s, width=width: E.tensor_scalar(
                    t2buf.ap()[:, s * W: s * W + width],
                    tbuf.ap()[:, s * W: s * W + width], 1.49, -1.49, ALU.min, ALU.max),
                    after=[e1, sl_e3[s]])
                e3 = plan.do("vector", lambda E, s=s, width=width: E.tensor_copy(
                    ibuf.ap()[:, s * W: s * W + width],
                    t2buf.ap()[:, s * W: s * W + width]), after=[e2, sl_e4[s]], gate=True)
                qsl = qbuf.ap()[:, s * W: s * W + width]
                e4 = plan.do("vector", lambda E, qsl=qsl, s=s, width=width: E.tensor_copy(
                    qsl, ibuf.ap()[:, s * W: s * W + width]), after=[e3, sl_st[s]], gate=True)
                ev_st = plan.dma("scalar", wo_t[c], qsl, after=[e4])
                st_evs.append(ev_st)
                sl_e1[s], sl_e2[s], sl_e3[s] = e1, e2, e3
                sl_e4[s], sl_st[s] = e4, ev_st
                prevq = [e4, ev_st]
            qprev = {"e1_0": sl_e1[0], "e1_1": sl_e1[1]}
            prev = prevq
        for widx in range(3):
            st_evs.append(plan.dma("sync", minv_out[:, widx:widx + 1],
                                   minv_cols.ap()[:, widx:widx + 1],
                                   allow_slow_non_contiguous=True, after=prev))
        plan.wait("sync", st_evs)
        plan.finalize(ctx)
    return nc



def build_l1f2():
    nc = bass.Bass("TRN2", target_bir_lowering=False)
    xn_in = nc.dram_tensor("xn1t", [OWN, D], F32, kind="ExternalInput").ap()
    wg_in = nc.dram_tensor("wgq2", [D, 4096], BF16, kind="ExternalInput").ap()
    wu_in = nc.dram_tensor("wuq2", [D, 4096], BF16, kind="ExternalInput").ap()
    mg_in = nc.dram_tensor("minvg", [128, 1], F32, kind="ExternalInput").ap()
    mu_in = nc.dram_tensor("minvu", [128, 1], F32, kind="ExternalInput").ap()
    xq2_out = nc.dram_tensor("xq2", [OWN, 4096], BF16, kind="ExternalOutput").ap()
    c2_out = nc.dram_tensor("c2", [128, NT], F32, kind="ExternalOutput").ap()

    with ExitStack() as ctx:
        plan = Plan(nc)
        sb = nc.alloc_sbuf_tensor
        ps = nc.alloc_psum_tensor
        xt_s = sb("f2_xt", [128, 1024], F32)
        ti_s = sb("f2_ti", [128, 1024], F32)
        ii_s = sb("f2_ii", [128, 1024], I32)
        ti2_s = sb("f2_ti2", [128, 1024], F32)
        ii2_s = sb("f2_ii2", [128, 1024], I32)
        xqtm_s = sb("f2_xqtm", [128, 1024], BF16)
        xqfm = sb("f2_xqfm", [128, 8 * OWN], BF16)
        wg_s = sb("f2_wg", [128, 2 * 8 * 512], BF16)
        wu_s = sb("f2_wu", [128, 2 * 8 * 512], BF16)
        prod = sb("f2_prod", [128, 4 * 4096], F32)
        gu_s2 = [sb("f2_gu0", [128, 512], F32), sb("f2_gu1", [128, 512], F32)]
        up_s2 = [sb("f2_up0", [128, 512], F32), sb("f2_up1", [128, 512], F32)]
        cols = sb("f2_cols", [128, 16], F32)
        cg_s = sb("f2_cg", [128, NT], F32)
        cu_s = sb("f2_cu", [128, NT], F32)
        c2_s = sb("f2_c2", [128, NT], F32)
        mg_s = sb("f2_mg", [128, 1], F32)
        mu_s = sb("f2_mu", [128, 1], F32)
        xq2_s = sb("f2_xq2", [128, 4096], BF16)
        identb, ev_id = alloc_identity(nc, plan, BF16, name="f2_identb")
        ps_tp = ps("f2_pstp", [128, 128], BF16)
        ps_g2 = [ps("f2_psg0", [128, 512], F32), ps("f2_psg1", [128, 512], F32)]
        ps_u2 = [ps("f2_psu0", [128, 512], F32), ps("f2_psu1", [128, 512], F32)]

        ev_m = [plan.dma("sync", mg_s.ap(), mg_in), plan.dma("sync", mu_s.ap(), mu_in)]
        xn_t = xn_in.rearrange("(t k) d -> t k d", k=128)
        xq2_t = xq2_out.rearrange("(t k) d -> t k d", k=128)

        qdone = []
        tq = {}

        def emit_quant_tile(t, prev):
            ev = plan.dma("scalar", xt_s.ap(), xn_t[t], after=prev)
            msc = cols.ap()[:, 0:1]
            rsc = cols.ap()[:, 1:2]
            ssc = cols.ap()[:, 2:3]
            e1 = plan.do("vector", lambda E: E.reduce_max(
                msc, xt_s.ap(), axis=AX, apply_absolute_value=True), after=[ev], gate=True)
            e2 = plan.do("vector", lambda E: E.tensor_scalar_max(rsc, msc, 1e-5), after=[e1], gate=True)
            e3 = plan.do("vector", lambda E: E.reciprocal(ssc, rsc), after=[e2], gate=True)
            e4 = plan.do("vector", lambda E, t=t: E.tensor_scalar(
                cg_s.ap()[:, t:t + 1], rsc, mg_s.ap()[:, 0:1], 1.0 / 127.0, ALU.mult, ALU.mult),
                after=[e3] + ev_m, gate=True)
            e5 = plan.do("vector", lambda E, t=t: E.tensor_scalar(
                cu_s.ap()[:, t:t + 1], rsc, mu_s.ap()[:, 0:1], 1.0 / 127.0, ALU.mult, ALU.mult),
                after=[e4] + ev_m, gate=True)
            e3b = plan.do("vector", lambda E: E.tensor_scalar_mul(
                cols.ap()[:, 3:4], ssc, 127.0), after=[e3], gate=True)
            e6 = plan.do("scalar", lambda E: E.activation(
                ti_s.ap(), xt_s.ap(), AF.Identity, scale=cols.ap()[:, 3:4]), after=[e3b])
            e7 = plan.do("vector", lambda E: E.tensor_copy(ii_s.ap(), ti_s.ap()),
                         after=[e6], gate=True)
            e8 = plan.do("vector", lambda E: E.tensor_copy(xqtm_s.ap(), ii_s.ap()),
                         after=[e7], gate=True)
            tdone = []
            for c in range(8):
                et = plan.do("tensor", lambda E, c=c: E.transpose(
                    ps_tp.ap(), xqtm_s.ap()[:, bass.ts(c, 128)], identb.ap()), after=[e8, ev_id])
                ee = plan.do("scalar", lambda E, c=c, t=t: E.activation(
                    xqfm.ap()[:, c * OWN + t * 128: c * OWN + t * 128 + 128], ps_tp.ap(),
                    AF.Identity), after=[et])
                plan.do("tensor", lambda E: E.nop(), after=[ee])
                tdone.append(ee)
            qdone.extend(tdone)
            tq[t] = tdone
            return tdone

        prev = [ev_id] + ev_m
        for t in range(4):
            prev = emit_quant_tile(t, prev)
        st_evs = []
        c2done = []
        mm_last = {0: [], 1: []}
        lu_f = {0: [], 1: []}
        prod_free = {tl: [] for tl in range(4)}
        for tb in range(4):
            for nck in range(8):
                slot = (nck % 2)
                ev_w = []
                for kc in range(8):
                    ev_w.append(plan.dma("gpsimd", wg_s.ap()[:, slot * 8 * 512 + kc * 512: slot * 8 * 512 + (kc + 1) * 512],
                                         wg_in.rearrange("(c k) n -> c k n", k=128)[kc][:, nck * 512:(nck + 1) * 512],
                                         after=mm_last[slot]))
                    ev_w.append(plan.dma("sync", wu_s.ap()[:, slot * 8 * 512 + kc * 512: slot * 8 * 512 + (kc + 1) * 512],
                                         wu_in.rearrange("(c k) n -> c k n", k=128)[kc][:, nck * 512:(nck + 1) * 512],
                                         after=mm_last[slot]))
                for tl in range(4):
                    pp = tl % 2
                    ps_g, ps_u = ps_g2[pp], ps_u2[pp]
                    gu_s, up_s = gu_s2[pp], up_s2[pp]
                    t = tb * 4 + tl
                    eg = eu = None
                    for kc in range(8):
                        lhs = xqfm.ap()[:, kc * OWN + t * 128: kc * OWN + t * 128 + 128]
                        eg = plan.do("tensor", lambda E, lhs=lhs, kc=kc, slot=slot, ps_g=ps_g: E.matmul(
                            ps_g.ap(), lhs, wg_s.ap()[:, slot * 8 * 512 + kc * 512: slot * 8 * 512 + (kc + 1) * 512],
                            start=(kc == 0), stop=(kc == 7)), after=ev_w + tq[t] + lu_f[pp])
                        eu = plan.do("tensor", lambda E, lhs=lhs, kc=kc, slot=slot, ps_u=ps_u: E.matmul(
                            ps_u.ap(), lhs, wu_s.ap()[:, slot * 8 * 512 + kc * 512: slot * 8 * 512 + (kc + 1) * 512],
                            start=(kc == 0), stop=(kc == 7)), after=ev_w + tq[t] + lu_f[pp])
                    es = plan.do("scalar", lambda E, t=t, ps_g=ps_g, gu_s=gu_s: E.activation(
                        gu_s.ap(), ps_g.ap(), AF.Silu, scale=cg_s.ap()[:, t:t + 1]), after=[eg])
                    eup = plan.do("scalar", lambda E, t=t, ps_u=ps_u, up_s=up_s, gu_s=gu_s: E.activation(
                        up_s.ap(), ps_u.ap(), AF.Identity, scale=cu_s.ap()[:, t:t + 1]), after=[eu, es])
                    epr = plan.do("vector", lambda E, tl=tl, nck=nck, gu_s=gu_s, up_s=up_s: E.tensor_mul(
                        prod.ap()[:, tl * 4096 + nck * 512: tl * 4096 + (nck + 1) * 512],
                        gu_s.ap(), up_s.ap()), after=[es, eup] + prod_free[tl])
                    lu_f[pp] = [epr]
                mm_last[slot] = [eu]
            if tb < 3:
                for t in range(4 * (tb + 1), 4 * (tb + 2)):
                    prev = emit_quant_tile(t, prev)
            for tl in range(4):
                t = tb * 4 + tl
                pr = prod.ap()[:, tl * 4096: (tl + 1) * 4096]
                m2 = cols.ap()[:, 4:5]
                r2 = cols.ap()[:, 5:6]
                s2 = cols.ap()[:, 6:7]
                e1 = plan.do("vector", lambda E, pr=pr: E.reduce_max(
                    m2, pr, axis=AX, apply_absolute_value=True),
                    after=lu_f[0] + lu_f[1] + [c2done[-1] if c2done else None], gate=True)
                e2 = plan.do("vector", lambda E: E.tensor_scalar_max(r2, m2, 1e-5), after=[e1], gate=True)
                e3 = plan.do("vector", lambda E: E.reciprocal(s2, r2), after=[e2], gate=True)
                e4 = plan.do("vector", lambda E, t=t: E.tensor_scalar(
                    c2_s.ap()[:, t:t + 1], r2, 1.0 / 127.0, None, ALU.mult), after=[e3], gate=True)
                e4b = plan.do("vector", lambda E: E.tensor_scalar_mul(
                    cols.ap()[:, 7:8], s2, 127.0), after=[e4], gate=True)
                subdone = []
                ea = None
                for sc in range(4):
                    prs = pr[:, sc * 1024:(sc + 1) * 1024]
                    ea = plan.do("scalar", lambda E, prs=prs: E.activation(
                        ti2_s.ap(), prs, AF.Identity, scale=cols.ap()[:, 7:8]), after=[e4b] + subdone)
                    eb = plan.do("vector", lambda E: E.tensor_copy(ii2_s.ap(), ti2_s.ap()),
                                 after=[ea], gate=True)
                    ec = plan.do("vector", lambda E, sc=sc: E.tensor_copy(
                        xq2_s.ap()[:, sc * 1024:(sc + 1) * 1024], ii2_s.ap()), after=[eb], gate=True)
                    subdone.append(ec)
                prod_free[tl] = [ea]
                ev_st = plan.dma("scalar", xq2_t[t], xq2_s.ap(), after=subdone)
                st_evs.append(ev_st)
                c2done.append(ev_st)
        st_evs.append(plan.dma("sync", c2_out, c2_s.ap(), after=c2done + qdone))
        plan.wait("sync", st_evs)
        plan.finalize(ctx)
    return nc



def build_l1f3():
    nc = bass.Bass("TRN2", target_bir_lowering=False)
    xq_in = nc.dram_tensor("xq2f", [4096, OWN], BF16, kind="ExternalInput").ap()
    wd_in = nc.dram_tensor("wdq3", [4096, D], BF16, kind="ExternalInput").ap()
    c2_in = nc.dram_tensor("c2b", [128, NT], F32, kind="ExternalInput").ap()
    md_in = nc.dram_tensor("minvd", [128, 1], F32, kind="ExternalInput").ap()
    x1_in = nc.dram_tensor("x1t", [OWN, D], F32, kind="ExternalInput").ap()
    a1_in = nc.dram_tensor("a1t", [OWN, D], F32, kind="ExternalInput").ap()
    x2_out = nc.dram_tensor("x2", [OWN, D], F32, kind="ExternalOutput").ap()

    with ExitStack() as ctx:
        plan = Plan(nc)
        sb = nc.alloc_sbuf_tensor
        ps = nc.alloc_psum_tensor
        wd_s = sb("f3_wd", [128, 32 * 1024], BF16)
        xq_s = sb("f3_xq", [128, 2 * 32 * 128], BF16)
        c2_s = sb("f3_c2", [128, NT], F32)
        cd_s = sb("f3_cd", [128, NT], F32)
        md_s = sb("f3_md", [128, 1], F32)
        x1_s = sb("f3_x1", [128, 2 * 1024], F32)
        a1_s = sb("f3_a1", [128, 2 * 1024], F32)
        f1_s = sb("f3_f1", [128, 1024], F32)
        t1_s = sb("f3_t1", [128, 1024], F32)
        x2_s = sb("f3_x2", [128, 2 * 1024], F32)
        ps_y = [ps("f3_psy0", [128, 512], F32), ps("f3_psy1", [128, 512], F32)]

        ev_w = []
        for c in range(32):
            ev_w.append(plan.dma("gpsimd", wd_s.ap()[:, bass.ts(c, 1024)],
                                 wd_in.rearrange("(c k) n -> c k n", k=128)[c]))
        ev_m = [plan.dma("sync", c2_s.ap(), c2_in), plan.dma("sync", md_s.ap(), md_in)]
        e_cd = plan.do("vector", lambda E: E.tensor_scalar_mul(
            cd_s.ap(), c2_s.ap(), md_s.ap()[:, 0:1]), after=ev_m)
        xq_t = xq_in.rearrange("(c k) t -> c k t", k=128)
        x1_t = x1_in.rearrange("(t k) d -> t k d", k=128)
        a1_t = a1_in.rearrange("(t k) d -> t k d", k=128)
        x2_t = x2_out.rearrange("(t k) d -> t k d", k=128)
        st_evs = []
        mm_slot = {0: [], 1: []}
        cmb_slot = {0: [], 1: []}
        st_slot = {0: None, 1: None}
        dq_prev = {0: None, 1: None}
        e4_prev = None
        for t in range(NT):
            slot = (t % 2)
            ev_x = []
            for c in range(32):
                q = "scalar" if c % 2 == 0 else "sync"
                ev_x.append(plan.dma(q, xq_s.ap()[:, slot * 4096 + c * 128: slot * 4096 + (c + 1) * 128],
                                     xq_t[c][:, t * 128:(t + 1) * 128], after=mm_slot[slot]))
            ev_l1 = plan.dma("sync", x1_s.ap()[:, slot * 1024: (slot + 1) * 1024], x1_t[t],
                             after=cmb_slot[slot])
            ev_l2 = plan.dma("gpsimd", a1_s.ap()[:, slot * 1024: (slot + 1) * 1024], a1_t[t],
                             after=cmb_slot[slot])
            ev_mm = []
            for nb in range(2):
                e = None
                for kc in range(32):
                    lhs = xq_s.ap()[:, slot * 4096 + kc * 128: slot * 4096 + (kc + 1) * 128]
                    e = plan.do("tensor", lambda E, lhs=lhs, kc=kc, nb=nb: E.matmul(
                        ps_y[nb].ap(), lhs, wd_s.ap()[:, kc * 1024 + nb * 512: kc * 1024 + (nb + 1) * 512],
                        start=(kc == 0), stop=(kc == 31)),
                        after=ev_x + ev_w + [e_cd, dq_prev[nb]])
                ev_mm.append(e)
            e1 = plan.do("vector", lambda E, t=t: E.tensor_scalar_mul(
                f1_s.ap()[:, 0:512], ps_y[0].ap(), cd_s.ap()[:, t:t + 1]), after=[ev_mm[0], e_cd])
            e2 = plan.do("vector", lambda E, t=t: E.tensor_scalar_mul(
                f1_s.ap()[:, 512:1024], ps_y[1].ap(), cd_s.ap()[:, t:t + 1]), after=[ev_mm[1], e1])
            e3 = plan.do("gpsimd", lambda E, slot=slot: E.tensor_add(
                t1_s.ap(), x1_s.ap()[:, slot * 1024:(slot + 1) * 1024],
                a1_s.ap()[:, slot * 1024:(slot + 1) * 1024]), after=[ev_l1, ev_l2, e4_prev])
            e4 = plan.do("vector", lambda E, slot=slot: E.tensor_add(
                x2_s.ap()[:, slot * 1024:(slot + 1) * 1024], t1_s.ap(), f1_s.ap()),
                after=[e2, e3, st_slot[slot]], gate=True)
            ev_st = plan.dma("sync", x2_t[t], x2_s.ap()[:, slot * 1024:(slot + 1) * 1024],
                             after=[e4])
            st_evs.append(ev_st)
            mm_slot[slot] = ev_mm
            cmb_slot[slot] = [e3]
            st_slot[slot] = ev_st
            dq_prev = {0: e1, 1: e2}
            e4_prev = e4
        plan.wait("sync", st_evs)
        plan.finalize(ctx)
    return nc



_nc_cache = {}


def _get_nc(name, builder):
    if name not in _nc_cache:
        _nc_cache[name] = builder()
    return _nc_cache[name]


def _assemble_fm(per_core):
    full = np.zeros((4, D, SEQ), np.float32)
    for c in range(NCORES):
        b, h = c // 2, c % 2
        full[b, :, h * OWN:(h + 1) * OWN] = np.asarray(per_core[c])
    return full


def kernel(L0_attn_X, L0_attn_Wq, L0_attn_Wk, L0_attn_Wv, L0_attn_Wo,
           L0_ffn_wqkv, L0_ffn_q_scale, L0_ffn_k_scale, L0_ffn_wo,
           L0_ffn_ln_g, L0_ffn_ln_b, L0_ffn_w1, L0_ffn_b1, L0_ffn_w2, L0_ffn_b2,
           L1_attn_W_qkv, L1_attn_W_out, L1_ffn_w_gate, L1_ffn_w_up, L1_ffn_w_down):
    cores = list(range(NCORES))
    x = np.ascontiguousarray(np.asarray(L0_attn_X, np.float32))
    xfm = to_fm(x)

    def run(name, builder, in_maps):
        nc = _get_nc(name, builder)
        return run_bass_kernel_spmd(nc, in_maps, core_ids=cores)

    def slices_of(full_fm):
        return [np.ascontiguousarray(full_fm[c // 2][:, (c % 2) * OWN:(c % 2) * OWN + OWN])
                for c in cores]

    res = run("ln0", build_ln0, [{"xin": s} for s in slices_of(xfm)])
    xn_fm = _assemble_fm([r["xn"] for r in res.results])

    tabs = {}
    masksI = {}
    for c in cores:
        t0 = (c % 2) * OWN
        if t0 not in tabs:
            tabs[t0] = rotary_tables_i(t0)
            masksI[t0] = masks_inner(t0)
    cur = xn_fm
    for layer in range(2):
        in_maps = []
        for c in cores:
            b, t0 = c // 2, (c % 2) * OWN
            in_maps.append(inner_attn_inputs(
                cur[b], cur[b], t0,
                np.asarray(L0_ffn_wqkv[layer], np.float32),
                np.asarray(L0_ffn_q_scale[layer], np.float32),
                np.asarray(L0_ffn_k_scale[layer], np.float32),
                np.asarray(L0_ffn_wo[layer], np.float32),
                tabs[t0], masksI[t0]))
        res = run("ia", build_inner_attn, in_maps)
        xmid_fm = _assemble_fm([r["xmid"] for r in res.results])
        in_maps = []
        for c in cores:
            b, t0 = c // 2, (c % 2) * OWN
            in_maps.append(inner_ffn_inputs(
                xmid_fm[b], t0,
                np.asarray(L0_ffn_w1[layer], np.float32),
                np.asarray(L0_ffn_b1[layer], np.float32),
                np.asarray(L0_ffn_w2[layer], np.float32),
                np.asarray(L0_ffn_b2[layer], np.float32),
                np.asarray(L0_ffn_ln_g[layer], np.float32),
                np.asarray(L0_ffn_ln_b[layer], np.float32)))
        res = run("if", build_inner_ffn, in_maps)
        cur = _assemble_fm([r["xi1"] for r in res.results])
    f0_fm = cur

    in_maps = []
    for c in cores:
        b, t0 = c // 2, (c % 2) * OWN
        in_maps.append(attn0_inputs(xn_fm[b], t0,
                                    np.asarray(L0_attn_Wq, np.float32),
                                    np.asarray(L0_attn_Wk, np.float32),
                                    np.asarray(L0_attn_Wv, np.float32),
                                    np.asarray(L0_attn_Wo, np.float32)))
    res = run("a0", build_attn0, in_maps)
    a0_fm = _assemble_fm([r["a0"] for r in res.results])

    in_maps = []
    for c in cores:
        b, t0 = c // 2, (c % 2) * OWN
        sl = slice(t0, t0 + OWN)
        in_maps.append({"cx": np.ascontiguousarray(xfm[b][:, sl]),
                        "ca": np.ascontiguousarray(a0_fm[b][:, sl]),
                        "cf": np.ascontiguousarray(f0_fm[b][:, sl])})
    res = run("cmb", build_cmb, in_maps)
    x1_fm = _assemble_fm([r["x1"] for r in res.results])
    xn1_fm = _assemble_fm([r["xn1"] for r in res.results])

    in_maps = []
    for c in cores:
        b, t0 = c // 2, (c % 2) * OWN
        in_maps.append(l1attn_inputs(xn1_fm[b], t0,
                                     np.asarray(L1_attn_W_qkv, np.float32),
                                     np.asarray(L1_attn_W_out, np.float32)))
    res = run("l1a", build_l1attn, in_maps)
    a1_fm = _assemble_fm([r["a1"] for r in res.results])

    im1 = {"wgt": np.ascontiguousarray(np.asarray(L1_ffn_w_gate, np.float32).T),
           "wut": np.ascontiguousarray(np.asarray(L1_ffn_w_up, np.float32).T),
           "wdt": np.ascontiguousarray(np.asarray(L1_ffn_w_down, np.float32).T)}
    res = run("wq", build_wquant, [im1] * NCORES)
    wgq = np.asarray(res.results[0]["wgq"])
    wuq = np.asarray(res.results[0]["wuq"])
    wdq = np.asarray(res.results[0]["wdq"])
    minv = np.asarray(res.results[0]["minv"])

    in_maps = []
    for c in cores:
        b, t0 = c // 2, (c % 2) * OWN
        xt = np.ascontiguousarray(np.swapaxes(xn1_fm[b][:, t0:t0 + OWN], 0, 1))
        in_maps.append({"xn1t": xt, "wgq2": wgq, "wuq2": wuq,
                        "minvg": np.ascontiguousarray(minv[:, 0:1]),
                        "minvu": np.ascontiguousarray(minv[:, 1:2])})
    res2 = run("l1f2", build_l1f2, in_maps)

    in_maps = []
    for c in cores:
        b, t0 = c // 2, (c % 2) * OWN
        xq2 = np.asarray(res2.results[c]["xq2"])
        c2 = np.asarray(res2.results[c]["c2"])
        in_maps.append({"xq2f": np.ascontiguousarray(xq2.T), "wdq3": wdq,
                        "c2b": c2, "minvd": np.ascontiguousarray(minv[:, 2:3]),
                        "x1t": np.ascontiguousarray(np.swapaxes(x1_fm[b][:, t0:t0 + OWN], 0, 1)),
                        "a1t": np.ascontiguousarray(np.swapaxes(a1_fm[b][:, t0:t0 + OWN], 0, 1))})
    res3 = run("l1f3", build_l1f3, in_maps)
    out = np.zeros((4, SEQ, D), np.float32)
    for c in cores:
        b, t0 = c // 2, (c % 2) * OWN
        out[b, t0:t0 + OWN] = np.asarray(res3.results[c]["x2"])
    return out
